# version 05
# baseline (speedup 1.0000x reference)
import numpy as np
from contextlib import ExitStack, contextmanager
import concourse.bass as bass
import concourse.mybir as mybir
from concourse.bass_utils import run_bass_kernel_spmd
from concourse.alu_op_type import AluOpType as ALU

F32 = mybir.dt.float32
BF16 = mybir.dt.bfloat16
U32 = mybir.dt.uint32
AF = mybir.ActivationFunctionType
AX = mybir.AxisListType

L = 2048
DM = 1024
NCT = DM // 128
SEG = 256
NSEG = L // SEG
GNTT = SEG // 128
C = 32
H, D = 16, 64
INNER = 1024
EPS = 1e-5
STOP_AFTER = None
SSM_DEBUG = False
P4_MODE = "full"

ENG_SEM = {"tensor": "T", "vector": "V", "scalar": "S", "gpsimd": "G", "sync": "Y"}
ALL_SEMS = ("T", "V", "S", "G", "Y", "LD", "LG", "ST")



def host_consts():
    import ml_dtypes
    BF = ml_dtypes.bfloat16
    c = {}
    c["ident"] = np.eye(128, dtype=np.float32).astype(BF)
    c["identf"] = np.eye(128, dtype=np.float32)
    c["ones128"] = np.ones((128, 128), np.float32).astype(BF)
    c["ones128f"] = np.ones((128, 128), np.float32)
    m = (np.arange(32)[:, None] <= np.arange(32)[None, :]).astype(np.float32)
    c["maskA"] = np.tile(m, (4, 16)).astype(BF)
    bt = np.zeros((128, 128), np.float32)
    for b in range(4):
        bt[b * 32:(b + 1) * 32, b * 32:(b + 1) * 32] = -m
    c["btrilneg"] = bt
    e8 = np.zeros((8, 8 * 128), np.float32)
    for e in range(8):
        e8[e, e * 128:(e + 1) * 128] = 1.0
    c["esel8"] = e8.astype(BF)
    e4 = np.zeros((128, 256), np.float32)
    for cc in range(4):
        e4[32 * cc + 31, cc * 64:(cc + 1) * 64] = 1.0
    c["esel4"] = e4.astype(BF)
    return c


CONST_SPECS = [("ident", [128, 128], BF16), ("identf", [128, 128], F32),
               ("esel8", [8, 1024], BF16), ("ones128", [128, 128], BF16),
               ("ones128f", [128, 128], F32), ("maskA", [128, 512], BF16),
               ("btrilneg", [128, 128], F32), ("esel4", [128, 256], BF16)]



class Env:
    def __init__(self, nc):
        self.nc = nc
        self.ctx = ExitStack()
        self.block = None
        self.sems = {}
        self.cnt = {}

    def __enter__(self):
        self.ctx.__enter__()
        self._scopes = []
        for name in ALL_SEMS:
            self.sems[name] = self.ctx.enter_context(self.nc.semaphore(name))
            self.cnt[name] = 0
        self.block = self.ctx.enter_context(self.nc.Block())
        return self

    @contextmanager
    def scope(self):
        st = ExitStack()
        self._scopes.append(st)
        try:
            yield
        finally:
            st.close()
            self._scopes.pop()

    def __exit__(self, *a):
        return self.ctx.__exit__(*a)

    def _actx(self):
        return self._scopes[-1] if getattr(self, "_scopes", None) else self.ctx

    def sb(self, name, shape, dtype=F32):
        return self._actx().enter_context(self.nc.sbuf_tensor(name, shape, dtype))

    def ps(self, name, shape, dtype=F32):
        return self._actx().enter_context(self.nc.psum_tensor(name, shape, dtype))

    def inc(self, inst, sem, n=1):
        inst.then_inc(self.sems[sem], n)
        self.cnt[sem] += n
        return inst

    def phase(self, parts):
        self.nphase = getattr(self, "nphase", 0) + 1
        if STOP_AFTER is not None and self.nphase > STOP_AFTER:
            return
        snap = {k: self.cnt[k] for k in ALL_SEMS}
        for name, fn in parts:
            dec = getattr(self.block, name)
            own = ENG_SEM[name]

            def body(eng, fn=fn, own=own):
                for k in ALL_SEMS:
                    if k != own and snap[k] > 0:
                        eng.wait_ge(self.sems[k], snap[k])
                before = sum(self.cnt.values())
                fn(eng)
                assert sum(self.cnt.values()) > before, "phase fn must inc a sem"
            dec(body)



def emit_ln(env, cst, xsrc, xln, scratch, nct=NCT, seg_w=SEG,
            src_stride=None, src_off=0, dst_stride=None, dst_off=0):
    ss = src_stride if src_stride is not None else seg_w
    ds = dst_stride if dst_stride is not None else seg_w
    sq, mu, var, rstd, lntmp = (scratch[k] for k in ("sqseg", "mu", "var", "rstd", "lntmp"))
    pS, pQ = scratch["pS"], scratch["pQ"]
    ones_sq = cst["ones128f"] if sq[:].dtype == F32 else cst["ones128"]
    bs = env.cnt["S"]
    bt = env.cnt["T"]
    bv = env.cnt["V"]

    def ssl(ct):
        return slice(ct * ss + src_off, ct * ss + src_off + seg_w)

    def f_s(s):
        for ct in range(nct):
            last = s.activation(sq[:, ct * seg_w:(ct + 1) * seg_w], xsrc[:, ssl(ct)], AF.Square)
        env.inc(last, "S")
        s.wait_ge(env.sems["V"], bv + 3)
        s.activation(lntmp[:], var[:], AF.Ln, bias=cst["epsc"][:])
        last = s.activation(rstd[:], lntmp[:], AF.Exp, scale=cst["neghc"][:])
        env.inc(last, "S")

    def f_t(t):
        for ct in range(nct):
            t.matmul(pS, cst["ones128f"][:], xsrc[:, ssl(ct)],
                     start=(ct == 0), stop=(ct == nct - 1))
        t.wait_ge(env.sems["S"], bs + 1)
        for ct in range(nct):
            last = t.matmul(pQ, ones_sq[:], sq[:, ct * seg_w:(ct + 1) * seg_w],
                            start=(ct == 0), stop=(ct == nct - 1))
        env.inc(last, "T")

    def f_v(v):
        v.wait_ge(env.sems["T"], bt + 1)
        inst = v.tensor_scalar(mu[:], pS, 1.0 / DM, None, ALU.mult)
        env.inc(inst, "V"); v.wait_ge(env.sems["V"], env.cnt["V"])
        inst = v.tensor_tensor(var[:], mu[:], mu[:], ALU.mult)
        env.inc(inst, "V"); v.wait_ge(env.sems["V"], env.cnt["V"])
        inst = v.scalar_tensor_tensor(var[:], pQ, 1.0 / DM, var[:], ALU.mult, ALU.subtract)
        env.inc(inst, "V")
        v.wait_ge(env.sems["S"], bs + 2)
        for ct in range(nct):
            v.tensor_tensor(lntmp[:], xsrc[:, ssl(ct)], mu[:], ALU.subtract)
            last = v.tensor_tensor(xln[:, ct * ds + dst_off: ct * ds + dst_off + seg_w],
                                   lntmp[:], rstd[:], ALU.mult)
        env.inc(last, "V")
    env.phase([("scalar", f_s), ("tensor", f_t), ("vector", f_v)])


def emit_proj_feat(env, dst, w_sb, src, pbank0, pbank1, njt=8, nct=NCT, seg_w=SEG,
                   evac="copy", evac_arg=None, dst_slicer=None, evac_custom=None):
    base_t = env.cnt["T"]
    jw = njt * 128
    ev_eng = "vector" if evac_custom is None else evac_custom[0]
    ev_sem = ENG_SEM[ev_eng]
    base_e = env.cnt[ev_sem]

    def _hw(h):
        return bass.AP(h.tensor, h.offset, [h.ap[0], [1, seg_w]])

    def f_t(t):
        for jt in range(njt):
            half = _hw(pbank0 if jt % 2 == 0 else pbank1)
            if jt >= 2:
                t.wait_ge(env.sems[ev_sem], base_e + jt - 1)
            for ct in range(nct):
                last = t.matmul(half, w_sb[:, ct * jw + jt * 128: ct * jw + (jt + 1) * 128],
                                src[:, ct * seg_w:(ct + 1) * seg_w],
                                start=(ct == 0), stop=(ct == nct - 1))
            env.inc(last, "T")

    def f_e(eng):
        for jt in range(njt):
            half = _hw(pbank0 if jt % 2 == 0 else pbank1)
            eng.wait_ge(env.sems["T"], base_t + jt + 1)
            if evac_custom is not None:
                last = evac_custom[1](eng, jt, half)
            else:
                dsl = dst_slicer(jt) if dst_slicer else dst[:, jt * seg_w:(jt + 1) * seg_w]
                if evac == "copy":
                    last = eng.tensor_copy(dsl, half)
                else:
                    last = eng.tensor_tensor(dsl, half, evac_arg[:, jt * seg_w:(jt + 1) * seg_w], ALU.add)
            env.inc(last, ev_sem)
    env.phase([("tensor", f_t), (ev_eng, f_e)])


def emit_proj_tok(env, dst, w_sb, xln, pb0, pb1, bias_row=None, ones128=None,
                  ntc=None, nct=NCT, seg_w=SEG, jw=1024):
    if ntc is None:
        ntc = seg_w // 128
    """Token-major projection: dst (128, ntc*jw) bf16, dst[t, j] = sum_c x[c,t] W[c,j]."""
    base_t = env.cnt["T"]
    base_v = env.cnt["V"]

    def f_t(t):
        for tc in range(ntc):
            pb = pb0 if tc % 2 == 0 else pb1
            if tc >= 2:
                t.wait_ge(env.sems["V"], base_v + tc - 1)
            nhalf = jw // 512
            for ct in range(nct):
                stat = xln[:, ct * seg_w + tc * 128: ct * seg_w + (tc + 1) * 128]
                mv = w_sb[:, ct * jw:(ct + 1) * jw]
                for hf in range(nhalf):
                    last = t.matmul(pb[:, hf * 512:(hf + 1) * 512], stat, mv[:, hf * 512:(hf + 1) * 512],
                                    start=(ct == 0), stop=(ct == nct - 1 and bias_row is None))
            if bias_row is not None:
                oc = ones128[0:1, 0:128]
                for hf in range(nhalf):
                    last = t.matmul(pb[:, hf * 512:(hf + 1) * 512], oc, bias_row[0:1, hf * 512:(hf + 1) * 512],
                                    start=False, stop=True)
            env.inc(last, "T")

    def f_v(v):
        for tc in range(ntc):
            pb = pb0 if tc % 2 == 0 else pb1
            v.wait_ge(env.sems["T"], base_t + tc + 1)
            last = v.tensor_copy(dst[:, tc * jw:(tc + 1) * jw], pb[:, 0:jw])
            env.inc(last, "V")
    env.phase([("tensor", f_t), ("vector", f_v)])



def emit_gla(env, cst, xfull, dr, bg_d):
    nc = env.nc

    w_sb = {nm: env.sb("g_" + nm + "_wsb", [128, NCT * 1024], BF16)
            for nm in ("wq", "wk", "wv", "wg", "wo")}
    bgrow = env.sb("g_bgrow", [1, INNER], BF16)

    def p0_gp(gp):
        for nm in ("wq", "wk", "wv", "wg", "wo"):
            r = dr[nm].rearrange("(a p) j -> p a j", p=128)
            last = gp.dma_start(out=w_sb[nm][:].rearrange("p (a j) -> p a j", j=1024), in_=r)
            env.inc(last, "LG", 16)
        last = gp.dma_start(out=bgrow[:], in_=bg_d)
        env.inc(last, "LG", 16)
    env.phase([("gpsimd", p0_gp)])

    xln = env.sb("g_xln", [128, NCT * SEG], BF16)
    scratch = {}
    for nm in ("mu", "var", "rstd"):
        scratch[nm] = env.sb("g_" + nm, [128, SEG], F32)
    scratch["msq"] = scratch["var"]
    scratch["lntmp"] = scratch["var"]

    QT = env.sb("g_QT", [128, NCT * SEG], BF16)
    KT = env.sb("g_KT", [128, NCT * SEG], BF16)
    Ktok = env.sb("g_Ktok", [128, GNTT * 1024], BF16)
    Vtok = env.sb("g_Vtok", [128, GNTT * 1024], BF16)
    cum = env.sb("g_cum", [128, GNTT * 1024], BF16)
    lgn = env.sb("g_lgn", [128, 512], F32)
    Am = env.sb("g_Am", [128, 512], BF16)
    S = env.sb("g_S", [128, 512], F32)
    Sbf = env.sb("g_Sbf", [128, 512], BF16)
    otok = env.sb("g_otok", [128, 1024], BF16)
    oT = env.sb("g_oT", [128, NCT * SEG], BF16)
    scratch["sqseg"] = oT
    expneg = otok

    pB = env.ps("g_pB", [128, 1024], F32)
    pI = env.ps("g_pI", [128, 1024], F32)
    pU = env.ps("g_pU", [128, 512], F32)
    pC = env.ps("g_pC", [128, 512], F32)
    pT = env.ps("g_pT", [128, 1024], BF16)
    pb0, pb1 = pB, pI
    scratch["pS"] = pU[:, 0:SEG]
    scratch["pQ"] = pC[:, 0:SEG]
    tmpB = env.sb("g_tmpB", [128, 512], F32)
    etile = tmpB
    Stmp = tmpB

    def p0_v(v):
        v.memset(S[:], 0.0)
        last = v.memset(Sbf[:], 0.0)
        env.inc(last, "V")
    env.phase([("vector", p0_v)])

    for seg in range(NSEG):
        t0 = seg * SEG

        emit_ln(env, cst, xfull, xln, scratch, src_stride=L, src_off=t0)

        emit_proj_feat(env, QT, w_sb["wq"], xln, pb0[:, 0:512], pb0[:, 512:1024])
        emit_proj_feat(env, KT, w_sb["wk"], xln, pb0[:, 0:512], pb0[:, 512:1024])
        emit_proj_tok(env, Vtok, w_sb["wv"], xln, pb0, pb1)
        emit_proj_tok(env, Ktok, w_sb["wk"], xln, pb0, pb1)

        for tc in range(GNTT):
            base_t = env.cnt["T"]
            base_s = env.cnt["S"]

            def g_t(t, tc=tc, base_t=base_t, base_s=base_s):
                for ct in range(NCT):
                    stat = xln[:, ct * SEG + tc * 128: ct * SEG + (tc + 1) * 128]
                    mv = w_sb["wg"][:, ct * 1024:(ct + 1) * 1024]
                    t.matmul(pb0[:, 0:512], stat, mv[:, 0:512], start=(ct == 0), stop=False)
                    t.matmul(pb0[:, 512:1024], stat, mv[:, 512:1024], start=(ct == 0), stop=False)
                oc = cst["ones128"][0:1, 0:128]
                t.matmul(pb0[:, 0:512], oc, bgrow[0:1, 0:512], start=False, stop=True)
                last = t.matmul(pb0[:, 512:1024], oc, bgrow[0:1, 512:1024], start=False, stop=True)
                env.inc(last, "T")
                for hf in range(2):
                    t.wait_ge(env.sems["S"], base_s + 1 + hf)
                    last = t.matmul(pb1[:, hf * 512:(hf + 1) * 512], cst["btrilneg"][:],
                                    lgn[:], start=True, stop=True)
                    env.inc(last, "T")

            def g_s(s, tc=tc, base_t=base_t):
                s.wait_ge(env.sems["T"], base_t + 1)
                for hf in range(2):
                    if hf == 1:
                        s.wait_ge(env.sems["T"], base_t + 2)
                    hsl = slice(hf * 512, (hf + 1) * 512)
                    s.activation(etile[:], pb0[:, hsl], AF.Exp, scale=cst["neg1c"][:])
                    last = s.activation(lgn[:], etile[:], AF.Ln, bias=cst["onec"][:])
                    env.inc(last, "S")
                s.wait_ge(env.sems["T"], base_t + 3)
                s.activation(expneg[:], pb1[:], AF.Exp, scale=cst["neg1c"][:])
                last = s.activation(cum[:, tc * 1024:(tc + 1) * 1024], pb1[:], AF.Exp)
                env.inc(last, "S")

            def g_v(v, tc=tc, base_s=base_s):
                v.wait_ge(env.sems["S"], base_s + 3)
                sl = slice(tc * 1024, (tc + 1) * 1024)
                last = v.tensor_tensor(Vtok[:, sl], expneg[:], Vtok[:, sl], ALU.mult)
                env.inc(last, "V")
            env.phase([("tensor", g_t), ("scalar", g_s), ("vector", g_v)])

        base_t = env.cnt["T"]
        base_v = env.cnt["V"]
        base_s = env.cnt["S"]

        def p4_t(t):
            for tt in range(GNTT):
                if tt >= 1:
                    t.wait_ge(env.sems["V"], base_v + (tt - 1) * 8 + 5)
                for cc in range(4):
                    base = 32 * cc
                    for h in [hh for hh in range(H) if hh % 2 == 0] + \
                             [hh for hh in range(H) if hh % 2 == 1]:
                        jt = h // 2
                        b64 = 64 * (h % 2)
                        dstp = pU if h % 2 == 0 else pC
                        tsl = slice(jt * SEG + tt * 128 + cc * 32,
                                    jt * SEG + tt * 128 + cc * 32 + 32)
                        last = t.matmul(dstp[base:base + 32, 32 * jt:32 * jt + 32],
                                        KT[b64:b64 + 64, tsl], QT[b64:b64 + 64, tsl],
                                        start=True, stop=True, tile_position=(b64, base))
                env.inc(last, "T")
                for cc in range(4):
                    base = 32 * cc
                    t.wait_ge(env.sems["V"], base_v + tt * 8 + 1 + cc)
                    vtile = Vtok[:, tt * 1024:(tt + 1) * 1024]
                    ktile = Ktok[:, tt * 1024:(tt + 1) * 1024]
                    for h in [hh for hh in range(H) if hh % 2 == 0] + \
                             [hh for hh in range(H) if hh % 2 == 1]:
                        jt = h // 2
                        b64 = 64 * (h % 2)
                        ocol = 64 * jt + 512 * (h % 2)
                        tsl = slice(jt * SEG + tt * 128 + cc * 32,
                                    jt * SEG + tt * 128 + cc * 32 + 32)
                        t.matmul(pB[base:base + 32, ocol:ocol + 64],
                                 QT[b64:b64 + 64, tsl], Sbf[b64:b64 + 64, 64 * jt:64 * jt + 64],
                                 start=True, stop=True, tile_position=(b64, base))
                    for h in range(H):
                        jt = h // 2
                        acol = 32 * jt + 256 * (h % 2)
                        ocol = 64 * jt + 512 * (h % 2)
                        t.matmul(pI[base:base + 32, ocol:ocol + 64],
                                 Am[base:base + 32, acol:acol + 32],
                                 vtile[base:base + 32, 64 * h:64 * h + 64],
                                 start=True, stop=True, tile_position=(base, base))
                    for h in range(H):
                        jt = h // 2
                        b64 = 64 * (h % 2)
                        last = t.matmul(pU[b64:b64 + 64, 64 * jt:64 * jt + 64],
                                        ktile[base:base + 32, 64 * h:64 * h + 64],
                                        vtile[base:base + 32, 64 * h:64 * h + 64],
                                        start=True, stop=True, tile_position=(base, b64))
                    ctile = cum[:, tt * 1024:(tt + 1) * 1024]
                    ceven = ctile[:, 0:1024]
                    ap_ev = bass.AP(ceven.tensor, ceven.offset, [ceven.ap[0], [128, 8], [1, 64]])
                    codd = ctile[:, 64:1024]
                    ap_od = bass.AP(codd.tensor, codd.offset, [codd.ap[0], [128, 8], [1, 64]])
                    t.matmul(pC[0:64, 0:512], esel_slice(cst, cc), ap_ev,
                             start=True, stop=True, tile_position=(0, 0))
                    last = t.matmul(pC[64:128, 0:512], esel_slice(cst, cc), ap_od,
                                    start=True, stop=True, tile_position=(0, 64))
                    env.inc(last, "T")
                t.wait_ge(env.sems["V"], base_v + tt * 8 + 7)
                for k in range(8):
                    last = t.transpose(pT[:, k * 128:(k + 1) * 128],
                                       otok[:, k * 128:(k + 1) * 128], cst["ident"][:])
                env.inc(last, "T")

        def p4_s(s):
            for tt in range(GNTT):
                s.wait_ge(env.sems["T"], base_t + tt * 6 + 5)
                s.wait_ge(env.sems["V"], base_v + tt * 8 + 5)
                last = s.activation(tmpB[:], pB[:, 0:512], AF.Copy)
                env.inc(last, "S")
                s.wait_ge(env.sems["V"], base_v + tt * 8 + 6)
                last = s.activation(tmpB[:], pB[:, 512:1024], AF.Copy)
                env.inc(last, "S")

        def p4_v(v):
            tv = base_t
            for tt in range(GNTT):
                v.wait_ge(env.sems["T"], tv + 1); tv += 1
                v.tensor_tensor(Am[:, 0:256], pU[:, 0:256], cst["maskA"][:, 0:256], ALU.mult)
                last = v.tensor_tensor(Am[:, 256:512], pC[:, 0:256], cst["maskA"][:, 0:256], ALU.mult)
                env.inc(last, "V")
                for cc in range(4):
                    v.wait_ge(env.sems["T"], tv + 1); tv += 1
                    v.tensor_tensor(Stmp[:], S[:], pU[:], ALU.add)
                    v.tensor_tensor(S[:], Stmp[:], pC[:], ALU.mult)
                    last = v.tensor_copy(Sbf[:], S[:])
                    env.inc(last, "V")
                for hf in range(2):
                    v.wait_ge(env.sems["S"], base_s + 2 * tt + 1 + hf)
                    hsl = slice(hf * 512, (hf + 1) * 512)
                    v.tensor_tensor(tmpB[:], pI[:, hsl], tmpB[:], ALU.add)
                    ctile = cum[:, tt * 1024:(tt + 1) * 1024]
                    cperm = bass.AP(ctile.tensor, ctile.offset + 64 * hf,
                                    [ctile.ap[0], [128, 8], [1, 64]])
                    otok3 = otok[:, hsl].rearrange("p (b c) -> p b c", c=64)
                    last = v.tensor_tensor(otok3, tmpB[:].rearrange("p (b c) -> p b c", c=64),
                                           cperm, ALU.mult)
                    env.inc(last, "V")
                v.wait_ge(env.sems["T"], tv + 1); tv += 1
                ot3 = oT[:].rearrange("p (a t) -> p a t", t=SEG)[:, :, tt * 128:(tt + 1) * 128]
                last = v.tensor_copy(ot3, pT[:].rearrange("p (a t) -> p a t", t=128))
                env.inc(last, "V")
        env.phase([("tensor", p4_t), ("scalar", p4_s), ("vector", p4_v)])

        def evac_res(eng, jt, half, t0=t0):
            dsl = xfull[:, jt * L + t0: jt * L + t0 + SEG]
            return eng.tensor_tensor(dsl, half, dsl, ALU.add)
        emit_proj_feat(env, None, w_sb["wo"], oT, pb0[:, 0:512], pb0[:, 512:1024],
                       evac_custom=("vector", evac_res))


def esel_slice(cst, cc):
    return cst["esel4"][:, cc * 64:(cc + 1) * 64]


def permute_wo_rows(wo):
    w = wo.reshape(2 * 8, 64, wo.shape[1])
    perm = [2 * jt + p for p in range(2) for jt in range(8)]
    return np.ascontiguousarray(w[perm].reshape(wo.shape))



def build_gla():
    nc = bass.Bass("TRN2", target_bir_lowering=False, detect_race_conditions=False)
    xin = nc.dram_tensor("xT", [DM, L], F32, kind="ExternalInput").ap()
    dr = {}
    for nm, shp in (("wq", [DM, INNER]), ("wk", [DM, INNER]), ("wv", [DM, INNER]),
                    ("wg", [DM, INNER]), ("wo", [INNER, DM])):
        dr[nm] = nc.dram_tensor(nm, shp, F32, kind="ExternalInput").ap()
    bg = nc.dram_tensor("bg", [1, INNER], F32, kind="ExternalInput").ap()
    GLA_CONSTS = [c for c in CONST_SPECS if c[0] != "esel8" and c[0] != "identf"]
    cst_d = {nm: nc.dram_tensor(nm, shp, dt, kind="ExternalInput").ap()
             for nm, shp, dt in GLA_CONSTS}
    xout = nc.dram_tensor("xoutT", [DM, L], F32, kind="ExternalOutput").ap()

    with Env(nc) as env:
        cst = {nm: env.sb(nm + "_sb", shp, dt) for nm, shp, dt in GLA_CONSTS}
        for nm, val in (("epsc", EPS), ("onec", 1.0), ("neg1c", -1.0), ("neghc", -0.5)):
            cst[nm] = env.sb(nm, [128, 1], F32)
            cst[nm + "_val"] = val
        xfull = env.sb("xfull", [128, NCT * L], F32)

        def p0_v(v):
            for nm in ("epsc", "onec", "neg1c", "neghc"):
                last = v.memset(cst[nm][:], cst[nm + "_val"])
            env.inc(last, "V")

        def p0_sy(sy):
            for nm, shp, dt in GLA_CONSTS:
                sy.dma_start(out=cst[nm][:], in_=cst_d[nm]).then_inc(env.sems["LD"], 16)
                env.cnt["LD"] += 16
            last = sy.dma_start(out=xfull[:].rearrange("p (a t) -> p a t", t=L),
                                in_=xin.rearrange("(a p) t -> p a t", p=128))
            env.inc(last, "LD", 16)
        env.phase([("sync", p0_sy), ("vector", p0_v)])

        with env.scope():
            emit_gla(env, cst, xfull, dr, bg)

        def pf_sy(sy):
            last = sy.dma_start(out=xout.rearrange("(a p) t -> p a t", p=128),
                                in_=xfull[:].rearrange("p (a t) -> p a t", t=L))
            env.inc(last, "ST", 16)
        env.phase([("sync", pf_sy)])

        @env.block.sync
        def _(sy):
            sy.wait_ge(env.sems["ST"], env.cnt["ST"])
    return nc


def run_gla(x_full, Wq, Wk, Wv, Wg, bgv, Wo):
    consts = host_consts()
    nc = build_gla()
    in_maps = []
    B = x_full.shape[0]
    for b in range(B):
        m = {"xT": np.ascontiguousarray(x_full[b].T),
             "wq": Wq, "wk": Wk, "wv": Wv, "wg": Wg,
             "bg": np.ascontiguousarray(bgv.reshape(1, -1)), "wo": permute_wo_rows(Wo)}
        m.update(consts)
        in_maps.append(m)
    res = run_bass_kernel_spmd(nc, in_maps, core_ids=list(range(B)))
    return np.stack([np.ascontiguousarray(res.results[b]["xoutT"]).T for b in range(B)])





DI = 2048
DS = 16
NTI = DI // 128
T0 = 32
SSEG = 512
XPW = SSEG + 32


def emit_ssm(env, cst, xfull, w_in_d, w_gate_d, w_out_d,
             b_in_d, b_gate_d, b_out_d, A_d, B_d, C_d):
    nc = env.nc

    xln = env.sb("s_xln", [128, NCT * SSEG], BF16)
    scratch = {}
    for nm in ("mu", "msq", "var", "rstd", "lntmp"):
        scratch[nm] = env.sb("s_" + nm, [128, SSEG], F32)
    wbuf = env.sb("s_wbuf", [128, 16384], BF16)
    xpseg = env.sb("s_xp", [128, NTI * XPW], BF16)
    y1 = env.sb("s_y1", [128, NTI * SSEG], F32)
    y2 = env.sb("s_y2", [128, NTI * SSEG], BF16)
    gate = env.sb("s_gate", [128, NTI * SSEG], BF16)
    scratch["sqseg"] = gate
    kap = env.sb("s_kap", [128, NTI * T0], F32)
    vA = env.sb("s_vA", [128, NTI * DS], F32)
    vB = env.sb("s_vB", [128, NTI * DS], F32)
    vtmp = env.sb("s_vtmp", [128, 256], F32)
    ktmp = env.sb("s_ktmp", [128, DS], F32)
    Bbuf = env.sb("s_B", [128, NTI * DS], F32)
    Cbuf = env.sb("s_C", [128, NTI * DS], F32)
    bincol = env.sb("s_bin", [128, NTI], F32)
    bgcol = env.sb("s_bg", [128, NTI], F32)
    bocol = env.sb("s_bo", [128, NCT], F32)

    pB = env.ps("s_pB", [128, 1024], F32)
    pI = env.ps("s_pI", [128, 1024], F32)
    pU = env.ps("s_pU", [128, 512], F32)
    pC = env.ps("s_pC", [128, 512], F32)
    scratch["pS"] = pU[:, 0:SSEG]
    scratch["pQ"] = pC[:, 0:SSEG]

    Abuf = y1

    def p_ld(sy):
        sy.dma_start(out=bincol[:], in_=b_in_d).then_inc(env.sems["LD"], 16)
        env.cnt["LD"] += 16
        sy.dma_start(out=bgcol[:], in_=b_gate_d).then_inc(env.sems["LD"], 16)
        env.cnt["LD"] += 16
        sy.dma_start(out=bocol[:], in_=b_out_d).then_inc(env.sems["LD"], 16)
        env.cnt["LD"] += 16
        sy.dma_start(out=Abuf[:, 0:NTI * 256].rearrange("p (a m) -> p a m", m=256),
                     in_=A_d.rearrange("(a p) m -> p a m", p=128)).then_inc(env.sems["LD"], 16)
        env.cnt["LD"] += 16
        sy.dma_start(out=Bbuf[:].rearrange("p (a n) -> p a n", n=DS),
                     in_=B_d.rearrange("(a p) n -> p a n", p=128)).then_inc(env.sems["LD"], 16)
        env.cnt["LD"] += 16
        last = sy.dma_start(out=Cbuf[:].rearrange("p (a n) -> p a n", n=DS),
                            in_=C_d.rearrange("(a p) n -> p a n", p=128))
        env.inc(last, "LD", 16)
    env.phase([("sync", p_ld)])

    def kprec(eng, tiles, vt, kt, own):
        def fn(e):
            def step(inst):
                env.inc(inst, own)
                e.wait_ge(env.sems[own], env.cnt[own])
            for ct in tiles:
                vsl = slice(ct * DS, (ct + 1) * DS)
                step(e.tensor_tensor(kt[:], Cbuf[:, vsl], Bbuf[:, vsl], ALU.mult))
                step(e.tensor_reduce(kap[:, ct * T0:ct * T0 + 1], kt[:], AX.X, ALU.add))
                vcur = Bbuf
                for tau in range(1, T0):
                    vnext = vA if vcur is not vA else vB
                    a3 = Abuf[:, ct * 256:(ct + 1) * 256].rearrange("p (m n) -> p m n", n=DS)
                    vc = vcur[:, vsl]
                    vbc = bass.AP(vc.tensor, vc.offset, [vc.ap[0], [1, DS], [0, DS]])
                    vt_mn = bass.AP(vt[:].tensor, vt[:].offset,
                                    [vt[:].ap[0], [1, DS], [DS, DS]])
                    step(e.tensor_tensor(vt_mn, a3, vbc, ALU.mult))
                    step(e.tensor_reduce(vnext[:, vsl], vt[:].rearrange("p (n m) -> p n m", m=DS),
                                         AX.X, ALU.add))
                    step(e.tensor_tensor(kt[:], Cbuf[:, vsl], vnext[:, vsl], ALU.mult))
                    step(e.tensor_reduce(kap[:, ct * T0 + tau:ct * T0 + tau + 1], kt[:],
                                         AX.X, ALU.add))
                    vcur = vnext
        return fn
    env.phase([("vector", kprec(None, range(0, NTI), vtmp, ktmp, "V"))])

    def p_ms(v):
        last = v.memset(xpseg[:], 0.0)
        env.inc(last, "V")
    env.phase([("vector", p_ms)])

    for s in range(L // SSEG):
        t0 = s * SSEG
        emit_ln(env, cst, xfull, xln, scratch, seg_w=SSEG,
                src_stride=L, src_off=t0)

        def ld_w(gp, wd=w_in_d, n=16):
            r = wd.rearrange("(a p) j -> p a j", p=128)
            last = gp.dma_start(out=wbuf[:].rearrange("p (a j) -> p a j", j=r.shape[2]), in_=r)
            env.inc(last, "LG", 16)
        env.phase([("gpsimd", ld_w)])

        def evac_xp(eng, jt, half):
            return eng.activation(xpseg[:, jt * XPW + 32: jt * XPW + 32 + SSEG], half,
                                  AF.Identity, bias=bincol[:, jt:jt + 1])
        emit_proj_feat(env, None, wbuf, xln, pB[:, 0:512], pB[:, 512:1024],
                       njt=16, nct=8, seg_w=SSEG, evac_custom=("scalar", evac_xp))

        def ld_wg(gp):
            r = w_gate_d.rearrange("(a p) j -> p a j", p=128)
            last = gp.dma_start(out=wbuf[:].rearrange("p (a j) -> p a j", j=2048), in_=r)
            env.inc(last, "LG", 16)
        env.phase([("gpsimd", ld_wg)])

        def evac_gate(eng, jt, half):
            return eng.activation(gate[:, jt * SSEG:(jt + 1) * SSEG], half,
                                  AF.Sigmoid, bias=bgcol[:, jt:jt + 1])
        emit_proj_feat(env, None, wbuf, xln, pB[:, 0:512], pB[:, 512:1024],
                       njt=16, nct=8, seg_w=SSEG, evac_custom=("scalar", evac_gate))

        def conv_v(v):
            for ct in range(NTI):
                ysl = y1[:, ct * SSEG:(ct + 1) * SSEG]
                for tau in range(0, T0):
                    xin = xpseg[:, ct * XPW + 32 - tau: ct * XPW + 32 - tau + SSEG]
                    kcol = kap[:, ct * T0 + tau: ct * T0 + tau + 1]
                    if tau == 0:
                        last = v.tensor_scalar(ysl, xin, kcol, None, ALU.mult)
                    else:
                        last = v.scalar_tensor_tensor(ysl, xin, kcol, ysl, ALU.mult, ALU.add)
            env.inc(last, "V")
        env.phase([("vector", conv_v)])

        def epi_v(v):
            for ct in range(NTI):
                ysl = slice(ct * SSEG, (ct + 1) * SSEG)
                last = v.tensor_tensor(y2[:, ysl], y1[:, ysl], gate[:, ysl], ALU.mult)
            srcap = xpseg[:].rearrange("p (a w) -> p a w", w=XPW)[:, :, SSEG:SSEG + 32]
            dstap = xpseg[:].rearrange("p (a w) -> p a w", w=XPW)[:, :, 0:32]
            last = v.tensor_copy(dstap, srcap)
            env.inc(last, "V")
        env.phase([("vector", epi_v)])

        if SSM_DEBUG and s == 0:
            dbg = {}
            for nm, buf, w in (("d_kap", kap, NTI * T0), ("d_xp", xpseg, NTI * XPW),
                               ("d_y1", y1, NTI * SSEG), ("d_gate", gate, NTI * SSEG),
                               ("d_xln", xln, NCT * SSEG), ("d_y2", y2, NTI * SSEG)):
                dt = {"d_kap": F32, "d_y1": F32}.get(nm, BF16)
                dbg[nm] = nc.dram_tensor(nm, [128, w], dt, kind="ExternalOutput").ap()

            def p_dbg(sy, dbg=dbg):
                for nm, buf in (("d_kap", kap), ("d_xp", xpseg), ("d_y1", y1),
                                ("d_gate", gate), ("d_xln", xln), ("d_y2", y2)):
                    last = sy.dma_start(out=dbg[nm], in_=buf[:])
                    env.inc(last, "ST", 16)
            env.phase([("sync", p_dbg)])

        def ld_wo(gp):
            r = w_out_d.rearrange("(a p) j -> p a j", p=128)
            last = gp.dma_start(out=wbuf[:].rearrange("p (a j) -> p a j", j=1024), in_=r)
            env.inc(last, "LG", 16)
        env.phase([("gpsimd", ld_wo)])

        def evac_out(eng, jt, half, t0=t0):
            dsl = xfull[:, jt * L + t0: jt * L + t0 + SSEG]
            return eng.scalar_tensor_tensor(dsl, half, bocol[:, jt:jt + 1], dsl,
                                            ALU.add, ALU.add)
        emit_proj_feat(env, None, wbuf, y2, pB[:, 0:512], pB[:, 512:1024],
                       njt=8, nct=16, seg_w=SSEG, evac_custom=("vector", evac_out))


def build_ssm():
    nc = bass.Bass("TRN2", target_bir_lowering=False, detect_race_conditions=False)
    xin = nc.dram_tensor("xT", [DM, L], F32, kind="ExternalInput").ap()
    w_in_d = nc.dram_tensor("w_in", [DM, DI], F32, kind="ExternalInput").ap()
    w_gate_d = nc.dram_tensor("w_gate", [DM, DI], F32, kind="ExternalInput").ap()
    w_out_d = nc.dram_tensor("w_out", [DI, DM], F32, kind="ExternalInput").ap()
    b_in_d = nc.dram_tensor("b_in", [128, NTI], F32, kind="ExternalInput").ap()
    b_gate_d = nc.dram_tensor("b_gate", [128, NTI], F32, kind="ExternalInput").ap()
    b_out_d = nc.dram_tensor("b_out", [128, NCT], F32, kind="ExternalInput").ap()
    A_d = nc.dram_tensor("A", [DI, 256], F32, kind="ExternalInput").ap()
    B_d = nc.dram_tensor("B_ssm", [DI, DS], F32, kind="ExternalInput").ap()
    C_d = nc.dram_tensor("C_ssm", [DI, DS], F32, kind="ExternalInput").ap()
    SSM_CONSTS = [c for c in CONST_SPECS if c[0] in ("ones128", "ones128f")]
    cst_d = {nm: nc.dram_tensor(nm, shp, dt, kind="ExternalInput").ap()
             for nm, shp, dt in SSM_CONSTS}
    xout = nc.dram_tensor("xoutT", [DM, L], F32, kind="ExternalOutput").ap()

    with Env(nc) as env:
        cst = {nm: env.sb(nm + "_sb", shp, dt) for nm, shp, dt in SSM_CONSTS}
        for nm, val in (("epsc", EPS), ("onec", 1.0), ("neg1c", -1.0), ("neghc", -0.5)):
            cst[nm] = env.sb(nm, [128, 1], F32)
            cst[nm + "_val"] = val
        xfull = env.sb("s_xfull", [128, NCT * L], F32)

        def p0_v(v):
            for nm in ("epsc", "onec", "neg1c", "neghc"):
                last = v.memset(cst[nm][:], cst[nm + "_val"])
            env.inc(last, "V")

        def p0_sy(sy):
            for nm, shp, dt in SSM_CONSTS:
                sy.dma_start(out=cst[nm][:], in_=cst_d[nm]).then_inc(env.sems["LD"], 16)
                env.cnt["LD"] += 16
            last = sy.dma_start(out=xfull[:].rearrange("p (a t) -> p a t", t=L),
                                in_=xin.rearrange("(a p) t -> p a t", p=128))
            env.inc(last, "LD", 16)
        env.phase([("sync", p0_sy), ("vector", p0_v)])
        emit_ssm(env, cst, xfull, w_in_d, w_gate_d, w_out_d,
                 b_in_d, b_gate_d, b_out_d, A_d, B_d, C_d)

        def pf_sy(sy):
            last = sy.dma_start(out=xout.rearrange("(a p) t -> p a t", p=128),
                                in_=xfull[:].rearrange("p (a t) -> p a t", t=L))
            env.inc(last, "ST", 16)
        env.phase([("sync", pf_sy)])

        @env.block.sync
        def _(sy):
            sy.wait_ge(env.sems["ST"], env.cnt["ST"])
    return nc


def run_ssm(x_full, w_in, b_in, w_gate, b_gate, A, B_ssm, C_ssm, w_out, b_out):
    consts = host_consts()
    nc = build_ssm()
    in_maps = []
    B = x_full.shape[0]
    for b in range(B):
        m = {"xT": np.ascontiguousarray(x_full[b].T),
             "w_in": w_in, "w_gate": w_gate, "w_out": w_out,
             "b_in": np.ascontiguousarray(b_in.reshape(NTI, 128).T),
             "b_gate": np.ascontiguousarray(b_gate.reshape(NTI, 128).T),
             "b_out": np.ascontiguousarray(b_out.reshape(NCT, 128).T),
             "A": np.ascontiguousarray(A.transpose(0, 2, 1).reshape(DI, 256)),
             "B_ssm": B_ssm, "C_ssm": C_ssm}
        m.update(consts)
        in_maps.append(m)
    res = run_bass_kernel_spmd(nc, in_maps, core_ids=list(range(B)))
    return np.stack([np.ascontiguousarray(res.results[b]["xoutT"]).T for b in range(B)])



HID = 2816
NHT = HID // 128
FSEG = 128
FNTC = FSEG // 128


def quant_s_body(env, s, jobs, q, xn_counts):
    V = env.sems["V"]
    ab, l2 = q["ab"], q["l2"]
    for i, (srca, dsta, w) in enumerate(jobs):
        s.wait_ge(V, xn_counts[i])
        s.activation(l2[:, 0:w], ab[:, 0:w], AF.Sign)
        inst = s.activation(ab[:, 0:w], ab[:, 0:w], AF.Abs)
        env.inc(inst, "S")


def quant_v_body(env, v, jobs, q, s_counts):
    V, S_ = env.sems["V"], env.sems["S"]
    scl, rscl, ab, lv, l1, l2 = (q[k] for k in ("scl", "rscl", "ab", "lv", "l1", "l2"))
    for i, (srca, dsta, w) in enumerate(jobs):
        nb = w // 16
        inst = v.tensor_reduce(scl[:, 0:nb], srca.rearrange("p (a b) -> p a b", b=16),
                               AX.X, ALU.max, apply_absolute_value=True)
        env.inc(inst, "V"); v.wait_ge(V, env.cnt["V"])
        inst = v.tensor_scalar(scl[:, 0:nb], scl[:, 0:nb], 1.0 / 6.0, 1e-12, ALU.mult, ALU.max)
        env.inc(inst, "V"); v.wait_ge(V, env.cnt["V"])
        inst = v.reciprocal(rscl[:, 0:nb], scl[:, 0:nb])
        env.inc(inst, "V"); v.wait_ge(V, env.cnt["V"])
        rb = rscl[:, 0:nb]
        rbc = bass.AP(rb.tensor, rb.offset, [rb.ap[0], [1, nb], [0, 16]])
        inst = v.tensor_tensor(ab[:, 0:w].rearrange("p (a b) -> p a b", b=16),
                               srca.rearrange("p (a b) -> p a b", b=16), rbc, ALU.mult)
        env.inc(inst, "V")
        v.wait_ge(S_, s_counts[i])
        v.tensor_scalar(lv[:, 0:w], ab[:, 0:w], 0.25, None, ALU.is_gt)
        for t in (0.75, 1.25, 1.75):
            v.scalar_tensor_tensor(lv[:, 0:w], ab[:, 0:w], t, lv[:, 0:w],
                                   ALU.is_gt, ALU.add)
        v.tensor_scalar(l1[:, 0:w], ab[:, 0:w], 2.5, None, ALU.is_gt)
        v.scalar_tensor_tensor(l1[:, 0:w], ab[:, 0:w], 3.5, l1[:, 0:w],
                               ALU.is_gt, ALU.add)
        v.scalar_tensor_tensor(lv[:, 0:w], lv[:, 0:w], 0.5, l1[:, 0:w],
                               ALU.mult, ALU.add)
        v.tensor_scalar(l1[:, 0:w], ab[:, 0:w], 5.0, 2.0, ALU.is_gt, ALU.mult)
        v.tensor_tensor(lv[:, 0:w], lv[:, 0:w], l1[:, 0:w], ALU.add)
        v.tensor_tensor(lv[:, 0:w], lv[:, 0:w], l2[:, 0:w], ALU.mult)
        sb_ = scl[:, 0:nb]
        sbc = bass.AP(sb_.tensor, sb_.offset, [sb_.ap[0], [1, nb], [0, 16]])
        inst = v.tensor_tensor(dsta.rearrange("p (a b) -> p a b", b=16),
                               lv[:, 0:w].rearrange("p (a b) -> p a b", b=16), sbc, ALU.mult)
        env.inc(inst, "V")


def emit_quant_multi(env, jobs, q):
    bv = env.cnt["V"]
    bs = env.cnt["S"]
    xn_counts = [bv + 5 * i + 4 for i in range(len(jobs))]
    s_counts = [bs + i + 1 for i in range(len(jobs))]

    def fs(s):
        quant_s_body(env, s, jobs, q, xn_counts)

    def fv(v):
        quant_v_body(env, v, jobs, q, s_counts)
    env.phase([("scalar", fs), ("vector", fv)])


def quant_chunks(env, src_ap_fn, dst_ap_fn, q, total, chunk=256):
    jobs = []
    off = 0
    while off < total:
        w = min(chunk, total - off)
        jobs.append((src_ap_fn(off, w), dst_ap_fn(off, w), w))
        off += w
    emit_quant_multi(env, jobs, q)


def emit_transpose_tiles(env, pairs, psum, ident, evac_dtype_note=None):
    ngrp = psum.shape[1] // 128 if hasattr(psum, "shape") else 4
    base_t = env.cnt["T"]
    base_v = env.cnt["V"]
    n = len(pairs)
    ngroups = (n + ngrp - 1) // ngrp

    def f_t(t):
        for g in range(ngroups):
            if g >= 1:
                t.wait_ge(env.sems["V"], base_v + g)
            lo, hi = g * ngrp, min((g + 1) * ngrp, n)
            for i in range(lo, hi):
                src, dstp = pairs[i]
                last = t.transpose(psum[:, (i - lo) * 128:(i - lo + 1) * 128], src, ident)
            env.inc(last, "T")

    def f_v(v):
        for g in range(ngroups):
            v.wait_ge(env.sems["T"], base_t + g + 1)
            lo, hi = g * ngrp, min((g + 1) * ngrp, n)
            for i in range(lo, hi):
                src, dstp = pairs[i]
                last = v.tensor_copy(dstp, psum[:, (i - lo) * 128:(i - lo + 1) * 128])
            env.inc(last, "V")
    env.phase([("tensor", f_t), ("vector", f_v)])


def emit_wprep_gu(env, cst, q, wdram, wdstT, wstage, wq_bf, pTb):
    bv, bs, bt, bld = env.cnt["V"], env.cnt["S"], env.cnt["T"], env.cnt["LD"]
    V, S_, T_, LD = env.sems["V"], env.sems["S"], env.sems["T"], env.sems["LD"]

    def jobs_for(wt):
        sl = (wt % 2) * 1024
        return [(wstage[:, sl + o: sl + o + 256], wq_bf[:, sl + o: sl + o + 256], 256)
                for o in range(0, 1024, 256)]

    def f_ld(sy):
        for wt in range(NHT):
            if wt >= 2:
                sy.wait_ge(V, bv + 21 * (wt - 2) + 20)
            last = sy.dma_start(out=wstage[:, (wt % 2) * 1024:(wt % 2) * 1024 + 1024],
                                in_=wdram[wt * 128:(wt + 1) * 128, :])
            env.inc(last, "LD", 16)

    def f_v(v):
        for wt in range(NHT):
            if wt >= 1:
                v.wait_ge(T_, bt + wt)
                k = wt - 1
                for kc in range(8):
                    last = v.tensor_copy(wdstT[:, kc * HID + k * 128:kc * HID + (k + 1) * 128],
                                         pTb[:, kc * 128:(kc + 1) * 128])
                env.inc(last, "V")
            v.wait_ge(LD, bld + 16 * (wt + 1))
            if wt >= 2:
                v.wait_ge(T_, bt + wt - 1)
            quant_v_body(env, v, jobs_for(wt), q,
                         [bs + 4 * wt + i + 1 for i in range(4)])
        v.wait_ge(T_, bt + NHT)
        k = NHT - 1
        for kc in range(8):
            last = v.tensor_copy(wdstT[:, kc * HID + k * 128:kc * HID + (k + 1) * 128],
                                 pTb[:, kc * 128:(kc + 1) * 128])
        env.inc(last, "V")

    def f_s(s):
        for wt in range(NHT):
            quant_s_body(env, s, jobs_for(wt), q,
                         [bv + 21 * wt + 5 * i + 4 for i in range(4)])

    def f_t(t):
        for wt in range(NHT):
            t.wait_ge(V, bv + 21 * wt + 20)
            sl = (wt % 2) * 1024
            for kc in range(8):
                last = t.transpose(pTb[:, kc * 128:(kc + 1) * 128],
                                   wq_bf[:, sl + kc * 128: sl + (kc + 1) * 128],
                                   cst["ident"][:])
            env.inc(last, "T")
    env.phase([("sync", f_ld), ("vector", f_v), ("scalar", f_s), ("tensor", f_t)])


def emit_wprep_wd(env, cst, q, wd_d, wdq_dram, wstage, wq_bf, wdk, pTb):
    bv, bs, bt, bld, bst = (env.cnt["V"], env.cnt["S"], env.cnt["T"],
                            env.cnt["LD"], env.cnt["ST"])
    V, S_, T_, LD, ST = (env.sems["V"], env.sems["S"], env.sems["T"],
                         env.sems["LD"], env.sems["ST"])
    NJ = HID // 256
    NG = (NHT + 7) // 8
    PTV = 5 * NJ + NG

    def jobs_for(wt):
        return [(wstage[:, o:o + 256], wq_bf[:, o:o + 256], 256)
                for o in range(0, HID, 256)]

    def stores_for(sy, wt):
        for g in range(NG):
            m = NG * wt + g
            sy.wait_ge(V, bv + PTV * wt + 5 * NJ + g + 1)
            lo, hi = g * 8, min(g * 8 + 8, NHT)
            dst = wdq_dram.rearrange("(a p) j -> p a j", p=128)[:, lo:hi,
                                                                wt * 128:(wt + 1) * 128]
            last = sy.dma_start(out=dst,
                                in_=wdk[m % 2][:, 0:(hi - lo) * 128].rearrange(
                                    "p (a j) -> p a j", j=128))
            env.inc(last, "ST", 16)

    def f_ld(sy):
        for wt in range(8):
            if wt >= 1:
                sy.wait_ge(V, bv + PTV * (wt - 1) + 5 * NJ)
            last = sy.dma_start(out=wstage[:], in_=wd_d[wt * 128:(wt + 1) * 128, :])
            env.inc(last, "LD", 16)
            if wt >= 1:
                stores_for(sy, wt - 1)
        stores_for(sy, 7)

    def f_v(v):
        for wt in range(8):
            v.wait_ge(LD, bld + 16 * (wt + 1))
            if wt >= 1:
                v.wait_ge(T_, bt + 3 * wt)
            quant_v_body(env, v, jobs_for(wt), q,
                         [bs + NJ * wt + i + 1 for i in range(NJ)])
            for g in range(NG):
                m = NG * wt + g
                v.wait_ge(T_, bt + m + 1)
                if m >= 2:
                    v.wait_ge(ST, bst + 16 * (m - 1))
                lo, hi = g * 8, min(g * 8 + 8, NHT)
                for kc in range(lo, hi):
                    last = v.tensor_copy(wdk[m % 2][:, (kc - lo) * 128:(kc - lo + 1) * 128],
                                         pTb[:, (kc - lo) * 128:(kc - lo + 1) * 128])
                env.inc(last, "V")

    def f_s(s):
        for wt in range(8):
            quant_s_body(env, s, jobs_for(wt), q,
                         [bv + PTV * wt + 5 * i + 4 for i in range(NJ)])

    def f_t(t):
        for wt in range(8):
            for g in range(NG):
                m = NG * wt + g
                if g == 0:
                    t.wait_ge(V, bv + PTV * wt + 5 * NJ)
                else:
                    t.wait_ge(V, bv + PTV * wt + 5 * NJ + g)
                lo, hi = g * 8, min(g * 8 + 8, NHT)
                for kc in range(lo, hi):
                    last = t.transpose(pTb[:, (kc - lo) * 128:(kc - lo + 1) * 128],
                                       wq_bf[:, kc * 128:(kc + 1) * 128], cst["ident"][:])
                env.inc(last, "T")

    env.phase([("sync", f_ld), ("vector", f_v), ("scalar", f_s), ("tensor", f_t)])


def emit_fp4ffn(env, cst, xfull, wdq_dram, wg_d, wu_d, wd_d):
    nc = env.nc

    wgqT = env.sb("f_wgqT", [128, 8 * HID], BF16)
    wuqT = env.sb("f_wuqT", [128, 8 * HID], BF16)
    wdk = [env.sb("f_wdk0", [128, DM], BF16), env.sb("f_wdk1", [128, DM], BF16)]

    q = {"scl": env.sb("f_scl", [128, 64], F32),
         "rscl": env.sb("f_rscl", [128, 64], F32),
         "ab": env.sb("f_qab", [128, 256], F32),
         "lv": env.sb("f_qlv", [128, 256], F32),
         "l1": env.sb("f_ql1", [128, 256], F32),
         "l2": env.sb("f_ql2", [128, 256], F32)}

    xlnf = env.sb("f_xlnf", [128, NCT * FSEG], F32)
    scratch = {}
    for nm in ("mu", "msq", "var", "rstd", "lntmp"):
        scratch[nm] = env.sb("f_" + nm, [128, FSEG], F32)
    xtok = env.sb("f_xtok", [128, FNTC * DM], F32)
    dsb = xtok

    xqT = env.sb("f_xqT", [128, NCT * FSEG], BF16)
    hmt = env.sb("f_hmt", [128, FSEG], F32)
    hmtok = env.sb("f_hmtok", [128, HID], F32)
    wstage = hmtok
    hqtok = env.sb("f_hqtok", [128, HID], BF16)
    xqtok = hqtok
    wq_bf = hqtok
    hqT = env.sb("f_hqT", [128, NHT * FSEG], BF16)
    scratch["sqseg"] = hqT
    sgt = env.sb("f_sgt", [128, FSEG], F32)

    pg = env.ps("f_pg", [128, 512], F32)
    pu = env.ps("f_pu", [128, 512], F32)
    pTf = env.ps("f_pTf", [128, 512], F32)
    pTb = env.ps("f_pTb", [128, 1024], BF16)
    pd = env.ps("f_pd", [128, NCT * FSEG], F32)
    scratch["pS"] = pg[:, 0:FSEG]
    scratch["pQ"] = pu[:, 0:FSEG]

    for wdram, wdstT in ((wg_d, wgqT), (wu_d, wuqT)):
        emit_wprep_gu(env, cst, q, wdram, wdstT, wstage, wq_bf, pTb)
    emit_wprep_wd(env, cst, q, wd_d, wdq_dram, wstage, wq_bf, wdk, pTb)

    for s_ in range(L // FSEG):
        t0 = s_ * FSEG

        emit_ln(env, cst, xfull, xlnf, scratch, seg_w=FSEG, src_stride=L, src_off=t0)

        pairs = [(xlnf[:, ct * FSEG + tc * 128: ct * FSEG + (tc + 1) * 128],
                  xtok[:, tc * DM + ct * 128: tc * DM + (ct + 1) * 128])
                 for ct in range(NCT) for tc in range(FNTC)]
        emit_transpose_tiles(env, pairs, pTf[:, 0:512], cst["identf"][:])

        jobs = []
        for tc in range(FNTC):
            o = 0
            while o < DM:
                w = min(256, DM - o)
                jobs.append((xtok[:, tc * DM + o: tc * DM + o + w],
                             xqtok[:, tc * DM + o: tc * DM + o + w], w))
                o += w
        emit_quant_multi(env, jobs, q)

        pairs = [(xqtok[:, tc * DM + ct * 128: tc * DM + (ct + 1) * 128],
                  xqT[:, ct * FSEG + tc * 128: ct * FSEG + (tc + 1) * 128])
                 for tc in range(FNTC) for ct in range(NCT)]
        emit_transpose_tiles(env, pairs, pTb[:, 0:1024], cst["ident"][:])

        for tsub in range(FNTC):
            base_t = env.cnt["T"]
            base_s = env.cnt["S"]
            base_v = env.cnt["V"]

            def hid_t(t, tsub=tsub):
                for nt in range(NHT):
                    if nt >= 1:
                        t.wait_ge(env.sems["V"], base_v + 2 * nt)
                    for kc in range(8):
                        t.matmul(pg[:, 0:FSEG],
                                 wgqT[:, kc * HID + nt * 128: kc * HID + (nt + 1) * 128],
                                 xqT[:, kc * FSEG:(kc + 1) * FSEG],
                                 start=(kc == 0), stop=(kc == 7))
                    for kc in range(8):
                        last = t.matmul(pu[:, 0:FSEG],
                                        wuqT[:, kc * HID + nt * 128: kc * HID + (nt + 1) * 128],
                                        xqT[:, kc * FSEG:(kc + 1) * FSEG],
                                        start=(kc == 0), stop=(kc == 7))
                    env.inc(last, "T")
                    t.wait_ge(env.sems["V"], base_v + 2 * nt + 1)
                    for tc2 in range(FNTC):
                        last = t.transpose(pTf[:, tc2 * 128:(tc2 + 1) * 128],
                                           hmt[:, tc2 * 128:(tc2 + 1) * 128], cst["identf"][:])
                    env.inc(last, "T")

            def hid_s(s, tsub=tsub):
                for nt in range(NHT):
                    s.wait_ge(env.sems["T"], base_t + 2 * nt + 1)
                    last = s.activation(sgt[:], pg[:, 0:FSEG], AF.Silu)
                    env.inc(last, "S")

            def hid_v(v, tsub=tsub):
                for nt in range(NHT):
                    v.wait_ge(env.sems["S"], base_s + nt + 1)
                    last = v.tensor_tensor(hmt[:], sgt[:], pu[:, 0:FSEG], ALU.mult)
                    env.inc(last, "V")
                    v.wait_ge(env.sems["T"], base_t + 2 * nt + 2)
                    last = v.tensor_copy(hmtok[:, nt * 128:(nt + 1) * 128],
                                         pTf[:, tsub * 128:(tsub + 1) * 128])
                    env.inc(last, "V")
            env.phase([("tensor", hid_t), ("scalar", hid_s), ("vector", hid_v)])

            quant_chunks(env, lambda o, w: hmtok[:, o:o + w],
                         lambda o, w: hqtok[:, o:o + w], q, HID)

            pairs = [(hqtok[:, nt * 128:(nt + 1) * 128],
                      hqT[:, nt * FSEG + tsub * 128: nt * FSEG + tsub * 128 + 128])
                     for nt in range(NHT)]
            emit_transpose_tiles(env, pairs, pTb[:, 0:1024], cst["ident"][:])

        base_t = env.cnt["T"]
        base_ld = env.cnt["LD"]

        def down_ld(sy):
            for kc in range(NHT):
                if kc >= 2:
                    sy.wait_ge(env.sems["T"], base_t + kc - 1)
                src_ = wdq_dram[kc * 128:(kc + 1) * 128, :]
                last = sy.dma_start(out=wdk[kc % 2][:], in_=src_)
                env.inc(last, "LD", 16)

        def down_t(t):
            for kc in range(NHT):
                t.wait_ge(env.sems["LD"], base_ld + 16 * (kc + 1))
                for tsub in range(FNTC):
                    for hf in range(2):
                        last = t.matmul(pd[:, tsub * DM + hf * 512: tsub * DM + (hf + 1) * 512],
                                        hqT[:, kc * FSEG + tsub * 128: kc * FSEG + (tsub + 1) * 128],
                                        wdk[kc % 2][:, hf * 512:(hf + 1) * 512],
                                        start=(kc == 0), stop=(kc == NHT - 1))
                env.inc(last, "T")

        def down_ev(v):
            v.wait_ge(env.sems["T"], base_t + NHT)
            for tsub in range(FNTC):
                last = v.tensor_copy(dsb[:, tsub * DM:(tsub + 1) * DM],
                                     pd[:, tsub * DM:(tsub + 1) * DM])
            env.inc(last, "V")
        env.phase([("sync", down_ld), ("tensor", down_t), ("vector", down_ev)])

        base_t = env.cnt["T"]
        base_v = env.cnt["V"]

        def dtr_t(t):
            n = 0
            for tsub in range(FNTC):
                for ct in range(NCT):
                    if n % 4 == 0 and n >= 4:
                        t.wait_ge(env.sems["V"], base_v + n // 4)
                    last = t.transpose(pTf[:, (n % 4) * 128:(n % 4 + 1) * 128],
                                       dsb[:, tsub * DM + ct * 128: tsub * DM + (ct + 1) * 128],
                                       cst["identf"][:])
                    n += 1
                    if n % 4 == 0:
                        env.inc(last, "T")

        def dtr_v(v, t0=t0):
            n = 0
            for tsub in range(FNTC):
                for ct in range(NCT):
                    if n % 4 == 0:
                        v.wait_ge(env.sems["T"], base_t + n // 4 + 1)
                    sl = slice(ct * L + t0 + tsub * 128, ct * L + t0 + (tsub + 1) * 128)
                    last = v.tensor_tensor(xfull[:, sl], pTf[:, (n % 4) * 128:(n % 4 + 1) * 128],
                                           xfull[:, sl], ALU.add)
                    n += 1
                    if n % 4 == 0:
                        env.inc(last, "V")
        env.phase([("tensor", dtr_t), ("vector", dtr_v)])


def build_fp4():
    nc = bass.Bass("TRN2", target_bir_lowering=False, detect_race_conditions=False)
    xin = nc.dram_tensor("xT", [DM, L], F32, kind="ExternalInput").ap()
    wg_d = nc.dram_tensor("w_gate", [HID, DM], F32, kind="ExternalInput").ap()
    wu_d = nc.dram_tensor("w_up", [HID, DM], F32, kind="ExternalInput").ap()
    wd_d = nc.dram_tensor("w_down", [DM, HID], F32, kind="ExternalInput").ap()
    FP4_CONSTS = [c for c in CONST_SPECS if c[0] in ("ones128", "ones128f", "ident", "identf")]
    cst_d = {nm: nc.dram_tensor(nm, shp, dt, kind="ExternalInput").ap()
             for nm, shp, dt in FP4_CONSTS}
    wdq_dram = nc.dram_tensor("f_wdq_scratch", [NHT * 128, DM], BF16, kind="Internal").ap()
    xout = nc.dram_tensor("xoutT", [DM, L], F32, kind="ExternalOutput").ap()

    with Env(nc) as env:
        cst = {nm: env.sb(nm + "_sb", shp, dt) for nm, shp, dt in FP4_CONSTS}
        for nm, val in (("epsc", EPS), ("neghc", -0.5)):
            cst[nm] = env.sb(nm, [128, 1], F32)
            cst[nm + "_val"] = val
        xfull = env.sb("f_xfull", [128, NCT * L], F32)

        def p0_v(v):
            for nm in ("epsc", "neghc"):
                last = v.memset(cst[nm][:], cst[nm + "_val"])
            env.inc(last, "V")

        def p0_sy(sy):
            for nm, shp, dt in FP4_CONSTS:
                sy.dma_start(out=cst[nm][:], in_=cst_d[nm]).then_inc(env.sems["LD"], 16)
                env.cnt["LD"] += 16
            last = sy.dma_start(out=xfull[:].rearrange("p (a t) -> p a t", t=L),
                                in_=xin.rearrange("(a p) t -> p a t", p=128))
            env.inc(last, "LD", 16)
        env.phase([("sync", p0_sy), ("vector", p0_v)])
        with env.scope():
            emit_fp4ffn(env, cst, xfull, wdq_dram, wg_d, wu_d, wd_d)

        def pf_sy(sy):
            last = sy.dma_start(out=xout.rearrange("(a p) t -> p a t", p=128),
                                in_=xfull[:].rearrange("p (a t) -> p a t", t=L))
            env.inc(last, "ST", 16)
        env.phase([("sync", pf_sy)])

        @env.block.sync
        def _(sy):
            sy.wait_ge(env.sems["ST"], env.cnt["ST"])
    return nc


def run_fp4(x_full, w_gate, w_up, w_down):
    consts = host_consts()
    nc = build_fp4()
    in_maps = []
    B = x_full.shape[0]
    for b in range(B):
        m = {"xT": np.ascontiguousarray(x_full[b].T),
             "w_gate": w_gate, "w_up": w_up, "w_down": w_down}
        m.update(consts)
        in_maps.append(m)
    res = run_bass_kernel_spmd(nc, in_maps, core_ids=list(range(B)))
    return np.stack([np.ascontiguousarray(res.results[b]["xoutT"]).T for b in range(B)])





NEXP = 8
MSEG = 512
MNTC = MSEG // 128
PSEG = 128


def emit_moe(env, cst, xfull, xln_dram, w2q_dram, w3q_dram, gate_wT_d, w1_d, w3_d, w2_d):
    nc = env.nc

    with env.scope():
        w2stage = env.sb("m_w2stage", [128, NHT * DM], BF16)
        for e in range(NEXP):
            def cst_gp(gp, e=e):
                r = w2_d[e].rearrange("(a p) j -> p a j", p=128)
                last = gp.dma_start(out=w2stage[:].rearrange("p (a j) -> p a j", j=DM), in_=r)
                env.inc(last, "LG", 16)
            env.phase([("gpsimd", cst_gp)])

            def cst_st(sy, e=e):
                for jp in range(2):
                    dst = w2q_dram[(e * 2 + jp) * NHT * 128:(e * 2 + jp + 1) * NHT * 128,
                                   :].rearrange("(nt p) j -> p nt j", p=128)
                    srcs = w2stage[:].rearrange("p (nt j) -> p nt j", j=DM)[
                        :, :, jp * 512:(jp + 1) * 512]
                    last = sy.dma_start(out=dst, in_=srcs)
                    env.inc(last, "ST", 16)
            env.phase([("sync", cst_st)])
            def cst3_gp(gp, e=e):
                for a in range(NCT):
                    srca = w3_d[e, a * 128:(a + 1) * 128, :].rearrange(
                        "p (nt h) -> p nt h", h=128)
                    dsta = w2stage[:, 0:NHT * 1024].rearrange(
                        "p (nt c) -> p nt c", c=1024)[:, :, a * 128:(a + 1) * 128]
                    last = gp.dma_start(out=dsta, in_=srca)
                    env.inc(last, "LG", 16)
            env.phase([("gpsimd", cst3_gp)])

            def cst3_st(sy, e=e):
                dst = w3q_dram[e * NHT * 128:(e + 1) * NHT * 128, :].rearrange(
                    "(nt p) c -> p nt c", p=128)
                last = sy.dma_start(out=dst, in_=w2stage[:, 0:NHT * 1024].rearrange(
                    "p (nt c) -> p nt c", c=1024))
                env.inc(last, "ST", 16)
            env.phase([("sync", cst3_st)])

    w1buf = env.sb("m_w1", [128, NCT * HID], BF16)
    w3k = [env.sb("m_w3k0", [128, NCT * 128], BF16), env.sb("m_w3k1", [128, NCT * 128], BF16)]
    wdk = [env.sb("m_wdk0", [128, 512], BF16), env.sb("m_wdk1", [128, 512], BF16)]
    xlnf = env.sb("m_xlnf", [128, NCT * PSEG], F32)
    xlnbf = env.sb("m_xlnbf", [128, NCT * PSEG], BF16)
    scratch = {}
    for nm in ("mu", "msq", "var", "rstd", "lntmp"):
        scratch[nm] = env.sb("m_" + nm, [128, PSEG], F32)
    gwT = env.sb("m_gwT", [128, NCT * NEXP], F32)
    lgt = env.sb("m_lgt", [128, NEXP], F32)
    rmax = env.sb("m_rmax", [128, 1], F32)
    rnegm = env.sb("m_rnegm", [128, 1], F32)
    rex = env.sb("m_rex", [128, NEXP], F32)
    rsum = env.sb("m_rsum", [128, 1], F32)
    rrs = env.sb("m_rrs", [128, 1], F32)
    req1 = env.sb("m_req1", [128, NEXP], F32)
    rtmp = env.sb("m_rtmp", [128, NEXP], F32)
    rm2 = env.sb("m_rm2", [128, 1], F32)
    re2x = env.sb("m_re2x", [128, 1], F32)
    rs2 = env.sb("m_rs2", [128, 1], F32)
    req2 = env.sb("m_req2", [128, NEXP], F32)
    rt1 = env.sb("m_rt1", [128, NEXP], F32)
    mrowtok = env.sb("m_mrowtok", [128, (L // 128) * NEXP], F32)
    mtr = env.sb("m_mtr", [8, L], BF16)
    mbc = env.sb("m_mbc", [128, MSEG], BF16)
    xeseg = env.sb("m_xeseg", [128, NCT * MSEG], BF16)
    scratch["sqseg"] = env.sb("m_sqf", [128, NCT * PSEG], F32)
    hbuf = env.sb("m_hbuf", [128, NHT * MSEG], BF16)
    sgt = env.sb("m_sgt", [128, MSEG], F32)
    dsbb = env.sb("m_dsbb", [128, MNTC * 512], BF16)

    pg = env.ps("m_pg", [128, 512], F32)
    pu = env.ps("m_pu", [128, 512], F32)
    pd = env.ps("m_pd", [128, MNTC * 512], F32)
    pT = env.ps("m_pT", [128, 1024], BF16)
    scratch["pS"] = pg[:, 0:PSEG]
    scratch["pQ"] = pu[:, 0:PSEG]

    def p_gw(sy):
        last = sy.dma_start(out=gwT[:].rearrange("p (a j) -> p a j", j=NEXP),
                            in_=gate_wT_d.rearrange("(a p) j -> p a j", p=128))
        env.inc(last, "LD", 16)
    env.phase([("sync", p_gw)])

    for ps in range(L // PSEG):
        t0 = ps * PSEG
        emit_ln(env, cst, xfull, xlnf, scratch, seg_w=PSEG, src_stride=L, src_off=t0)

        def p_cp(v, ps=ps):
            for ct in range(NCT):
                last = v.tensor_copy(xlnbf[:, ct * PSEG:(ct + 1) * PSEG],
                                     xlnf[:, ct * PSEG:(ct + 1) * PSEG])
            env.inc(last, "V")
        env.phase([("vector", p_cp)])

        def p_st(sy, t0=t0):
            dst = xln_dram.rearrange("(a p) t -> p a t", p=128)[:, :, t0:t0 + PSEG]
            last = sy.dma_start(out=dst, in_=xlnbf[:].rearrange("p (a t) -> p a t", t=PSEG))
            env.inc(last, "ST", 16)
        env.phase([("sync", p_st)])

        def p_rt(t, ps=ps):
            for ct in range(NCT):
                last = t.matmul(pd[:, 0:NEXP], xlnf[:, ct * PSEG:(ct + 1) * PSEG],
                                gwT[:, ct * NEXP:(ct + 1) * NEXP],
                                start=(ct == 0), stop=(ct == NCT - 1))
            env.inc(last, "T")
        env.phase([("tensor", p_rt)])

        s_base = env.cnt["S"]
        v_base = env.cnt["V"]

        def p_rm(v, ps=ps, s_base=s_base):
            V = env.sems["V"]

            def stp(i):
                env.inc(i, "V"); v.wait_ge(V, env.cnt["V"])
            stp(v.tensor_copy(lgt[:], pd[:, 0:NEXP]))
            stp(v.tensor_reduce(rmax[:], lgt[:], AX.X, ALU.max))
            stp(v.tensor_scalar(rnegm[:], rmax[:], -1.0, None, ALU.mult))
            v.wait_ge(env.sems["S"], s_base + 1)
            stp(v.tensor_reduce(rsum[:], rex[:], AX.X, ALU.add))
            stp(v.reciprocal(rrs[:], rsum[:]))
            mb = bass.AP(rmax[:].tensor, rmax[:].offset, [rmax[:].ap[0], [0, NEXP]])
            stp(v.tensor_tensor(req1[:], lgt[:], mb, ALU.is_equal))
            stp(v.scalar_tensor_tensor(rtmp[:], req1[:], -1e30, lgt[:], ALU.mult, ALU.add))
            stp(v.tensor_reduce(rm2[:], rtmp[:], AX.X, ALU.max))
            v.wait_ge(env.sems["S"], s_base + 2)
            stp(v.tensor_tensor(rs2[:], re2x[:], rrs[:], ALU.mult))
            m2b = bass.AP(rm2[:].tensor, rm2[:].offset, [rm2[:].ap[0], [0, NEXP]])
            stp(v.tensor_tensor(req2[:], rtmp[:], m2b, ALU.is_equal))
            stp(v.tensor_scalar(rt1[:], req1[:], rrs[:], None, ALU.mult))
            stp(v.scalar_tensor_tensor(mrowtok[:, ps * NEXP:(ps + 1) * NEXP],
                                       req2[:], rs2[:], rt1[:], ALU.mult, ALU.add))

        def p_rs(s, ps=ps, v_base=v_base):
            s.wait_ge(env.sems["V"], v_base + 3)
            last = s.activation(rex[:], lgt[:], AF.Exp, bias=rnegm[:])
            env.inc(last, "S")
            s.wait_ge(env.sems["V"], v_base + 8)
            last = s.activation(re2x[:], rm2[:], AF.Exp, bias=rnegm[:])
            env.inc(last, "S")
        env.phase([("vector", p_rm), ("scalar", p_rs)])

    base_t = env.cnt["T"]
    base_v = env.cnt["V"]

    def p_mt(t):
        for g in range(4):
            if g >= 1:
                t.wait_ge(env.sems["V"], base_v + g)
            for k in range(4):
                ps = g * 4 + k
                last = t.transpose(pd[0:8, k * 128:(k + 1) * 128]
                                   if False else pd[:, k * 128:(k + 1) * 128][0:8, :],
                                   mrowtok[:, ps * NEXP:(ps + 1) * NEXP], cst["identf"][:])
            env.inc(last, "T")

    def p_mv(v):
        for g in range(4):
            v.wait_ge(env.sems["T"], base_t + g + 1)
            for k in range(4):
                ps = g * 4 + k
                last = v.tensor_copy(mtr[:, ps * 128:(ps + 1) * 128],
                                     pd[:, k * 128:(k + 1) * 128][0:8, :])
            env.inc(last, "V")
    env.phase([("tensor", p_mt), ("vector", p_mv)])

    for e in range(NEXP):
        def p_wl(gp, e=e):
            r = w1_d[e].rearrange("(a p) j -> p a j", p=128)
            last = gp.dma_start(out=w1buf[:].rearrange("p (a j) -> p a j", j=HID), in_=r)
            env.inc(last, "LG", 16)
        env.phase([("gpsimd", p_wl)])

        for sg in range(L // MSEG):
            t0 = sg * MSEG

            base_t = env.cnt["T"]
            base_ld0 = env.cnt["LD"]

            def p_lx(sy, t0=t0):
                srcx = xln_dram.rearrange("(a p) t -> p a t", p=128)[:, :, t0:t0 + MSEG]
                last = sy.dma_start(out=xeseg[:].rearrange("p (a t) -> p a t", t=MSEG), in_=srcx)
                env.inc(last, "LD", 16)

            def p_bc(t, e=e, t0=t0):
                last = t.matmul(pg[:, 0:MSEG], cst["esel8"][:, e * 128:(e + 1) * 128],
                                mtr[0:8, t0:t0 + MSEG], start=True, stop=True)
                env.inc(last, "T")

            def p_bv(v, e=e, base_ld0=base_ld0):
                v.wait_ge(env.sems["T"], base_t + 1)
                inst = v.tensor_copy(mbc[:], pg[:, 0:MSEG])
                env.inc(inst, "V")
                v.wait_ge(env.sems["V"], env.cnt["V"])
                v.wait_ge(env.sems["LD"], base_ld0 + 16)
                for ct in range(NCT):
                    sl = slice(ct * MSEG, (ct + 1) * MSEG)
                    last = v.tensor_tensor(xeseg[:, sl], mbc[:], xeseg[:, sl], ALU.mult)
                env.inc(last, "V")
            env.phase([("sync", p_lx), ("tensor", p_bc), ("vector", p_bv)])

            base_t = env.cnt["T"]
            base_s = env.cnt["S"]
            base_v = env.cnt["V"]
            base_lg = env.cnt["LG"]

            def hid_ld(gp, e=e):
                for nt in range(NHT):
                    if nt >= 2:
                        gp.wait_ge(env.sems["T"], base_t + nt - 1)
                    src_ = w3q_dram[(e * NHT + nt) * 128:(e * NHT + nt + 1) * 128, :]
                    last = gp.dma_start(out=w3k[nt % 2][:], in_=src_)
                    env.inc(last, "LG", 16)

            def hid_t(t):
                for nt in range(NHT):
                    if nt >= 1:
                        t.wait_ge(env.sems["V"], base_v + nt)
                    t.wait_ge(env.sems["LG"], base_lg + 16 * (nt + 1))
                    for kc in range(NCT):
                        t.matmul(pg[:, 0:512],
                                 w1buf[:, kc * HID + nt * 128: kc * HID + (nt + 1) * 128],
                                 xeseg[:, kc * MSEG:(kc + 1) * MSEG],
                                 start=(kc == 0), stop=(kc == NCT - 1))
                    for kc in range(NCT):
                        last = t.matmul(pu[:, 0:512],
                                        w3k[nt % 2][:, kc * 128:(kc + 1) * 128],
                                        xeseg[:, kc * MSEG:(kc + 1) * MSEG],
                                        start=(kc == 0), stop=(kc == NCT - 1))
                    env.inc(last, "T")

            def hid_s(s):
                for nt in range(NHT):
                    s.wait_ge(env.sems["T"], base_t + nt + 1)
                    last = s.activation(sgt[:], pg[:, 0:MSEG], AF.Silu)
                    env.inc(last, "S")

            def hid_v(v):
                for nt in range(NHT):
                    v.wait_ge(env.sems["S"], base_s + nt + 1)
                    last = v.tensor_tensor(hbuf[:, nt * MSEG:(nt + 1) * MSEG],
                                           sgt[:], pu[:, 0:MSEG], ALU.mult)
                    env.inc(last, "V")
            env.phase([("gpsimd", hid_ld), ("tensor", hid_t), ("scalar", hid_s), ("vector", hid_v)])

            base_t = env.cnt["T"]
            base_v = env.cnt["V"]
            base_ld = env.cnt["LD"]

            def down_ld(sy, e=e):
                for jp in range(2):
                    bt = base_t + jp * (NHT + 2)
                    for nt in range(NHT):
                        if jp == 1 and nt == 0:
                            sy.wait_ge(env.sems["T"], base_t + NHT)
                        if nt >= 2:
                            sy.wait_ge(env.sems["T"], bt + nt - 1)
                        src_ = w2q_dram[((e * 2 + jp) * NHT + nt) * 128:
                                        ((e * 2 + jp) * NHT + nt + 1) * 128, :]
                        last = sy.dma_start(out=wdk[nt % 2][:], in_=src_)
                        env.inc(last, "LD", 16)

            def down_t(t, t0=t0):
                for jp in range(2):
                    bt = base_t + jp * (NHT + 2)
                    bl = base_ld + jp * 16 * NHT
                    if jp == 1:
                        t.wait_ge(env.sems["V"], base_v + 1)
                    for nt in range(NHT):
                        t.wait_ge(env.sems["LD"], bl + 16 * (nt + 1))
                        for tsub in range(MNTC):
                            last = t.matmul(
                                pd[:, tsub * 512:(tsub + 1) * 512],
                                hbuf[:, nt * MSEG + tsub * 128: nt * MSEG + (tsub + 1) * 128],
                                wdk[nt % 2][:],
                                start=(nt == 0), stop=(nt == NHT - 1))
                        env.inc(last, "T")
                    n = 0
                    for tsub in range(MNTC):
                        for ct in range(4):
                            if n % 8 == 0:
                                t.wait_ge(env.sems["V"], base_v + jp * 3 + 1 + (1 if n >= 8 else 0))
                            last = t.transpose(pT[:, (n % 8) * 128:(n % 8 + 1) * 128],
                                               dsbb[:, tsub * 512 + ct * 128: tsub * 512 + (ct + 1) * 128],
                                               cst["ident"][:])
                            n += 1
                            if n % 8 == 0:
                                env.inc(last, "T")

            def down_v(v, t0=t0):
                for jp in range(2):
                    bt = base_t + jp * (NHT + 2)
                    v.wait_ge(env.sems["T"], bt + NHT)
                    inst = v.tensor_copy(dsbb[:], pd[:, 0:MNTC * 512])
                    env.inc(inst, "V")
                    n = 0
                    for tsub in range(MNTC):
                        for ct in range(4):
                            if n % 8 == 0:
                                v.wait_ge(env.sems["T"], bt + NHT + n // 8 + 1)
                            ctg = jp * 4 + ct
                            sl = slice(ctg * L + t0 + tsub * 128, ctg * L + t0 + (tsub + 1) * 128)
                            last = v.tensor_tensor(xfull[:, sl], pT[:, (n % 8) * 128:(n % 8 + 1) * 128],
                                                   xfull[:, sl], ALU.add)
                            n += 1
                            if n % 8 == 0:
                                env.inc(last, "V")
            env.phase([("sync", down_ld), ("tensor", down_t), ("vector", down_v)])

def build_moe():
    nc = bass.Bass("TRN2", target_bir_lowering=False, detect_race_conditions=False)
    xin = nc.dram_tensor("xT", [DM, L], F32, kind="ExternalInput").ap()
    gate_wT_d = nc.dram_tensor("gate_wT", [DM, NEXP], F32, kind="ExternalInput").ap()
    w1_d = nc.dram_tensor("w1", [NEXP, DM, HID], F32, kind="ExternalInput").ap()
    w3_d = nc.dram_tensor("w3", [NEXP, DM, HID], F32, kind="ExternalInput").ap()
    w2_d = nc.dram_tensor("w2", [NEXP, HID, DM], F32, kind="ExternalInput").ap()
    MOE_CONSTS = [c for c in CONST_SPECS if c[0] in ("ones128", "ones128f", "ident", "identf", "esel8")]
    cst_d = {nm: nc.dram_tensor(nm, shp, dt, kind="ExternalInput").ap()
             for nm, shp, dt in MOE_CONSTS}
    xln_dram = nc.dram_tensor("m_xln_scratch", [DM, L], BF16, kind="Internal").ap()
    w2q_dram = nc.dram_tensor("m_w2q_scratch", [NEXP * 2 * NHT * 128, 512], BF16, kind="Internal").ap()
    w3q_dram = nc.dram_tensor("m_w3q_scratch", [NEXP * NHT * 128, NCT * 128], BF16, kind="Internal").ap()
    xout = nc.dram_tensor("xoutT", [DM, L], F32, kind="ExternalOutput").ap()

    with Env(nc) as env:
        cst = {nm: env.sb(nm + "_sb", shp, dt) for nm, shp, dt in MOE_CONSTS}
        for nm, val in (("epsc", EPS), ("neghc", -0.5)):
            cst[nm] = env.sb(nm, [128, 1], F32)
            cst[nm + "_val"] = val

        def p0_v(v):
            for nm in ("epsc", "neghc"):
                last = v.memset(cst[nm][:], cst[nm + "_val"])
            env.inc(last, "V")

        xfull = env.sb("m_xfull", [128, NCT * L], F32)

        def p0_sy(sy):
            for nm, shp, dt in MOE_CONSTS:
                sy.dma_start(out=cst[nm][:], in_=cst_d[nm]).then_inc(env.sems["LD"], 16)
                env.cnt["LD"] += 16
            last = sy.dma_start(out=xfull[:].rearrange("p (a t) -> p a t", t=L),
                                in_=xin.rearrange("(a p) t -> p a t", p=128))
            env.inc(last, "LD", 16)
        env.phase([("sync", p0_sy), ("vector", p0_v)])
        with env.scope():
            emit_moe(env, cst, xfull, xln_dram, w2q_dram, w3q_dram, gate_wT_d, w1_d, w3_d, w2_d)

        def pf_sy(sy):
            last = sy.dma_start(out=xout.rearrange("(a p) t -> p a t", p=128),
                                in_=xfull[:].rearrange("p (a t) -> p a t", t=L))
            env.inc(last, "ST", 16)
        env.phase([("sync", pf_sy)])

        @env.block.sync
        def _(sy):
            sy.wait_ge(env.sems["ST"], env.cnt["ST"])
    return nc


def run_moe(x_full, gate_w, w1, w2, w3):
    consts = host_consts()
    nc = build_moe()
    in_maps = []
    B = x_full.shape[0]
    for b in range(B):
        m = {"xT": np.ascontiguousarray(x_full[b].T),
             "gate_wT": np.ascontiguousarray(gate_w.T),
             "w1": w1, "w3": w3, "w2": w2}
        m.update({k: consts[k] for k in ("ones128", "ones128f", "ident", "identf", "esel8")})
        in_maps.append(m)
    res = run_bass_kernel_spmd(nc, in_maps, core_ids=list(range(B)))
    return np.stack([np.ascontiguousarray(res.results[b]["xoutT"]).T for b in range(B)])




def build_fused():
    nc = bass.Bass("TRN2", target_bir_lowering=False, detect_race_conditions=False)
    xin = nc.dram_tensor("xT", [DM, L], F32, kind="ExternalInput").ap()
    gdr = {}
    for nm, shp in (("wq", [DM, INNER]), ("wk", [DM, INNER]), ("wv", [DM, INNER]),
                    ("wg", [DM, INNER]), ("wo", [INNER, DM])):
        gdr[nm] = nc.dram_tensor(nm, shp, F32, kind="ExternalInput").ap()
    bg = nc.dram_tensor("bg", [1, INNER], F32, kind="ExternalInput").ap()
    gate_wT_d = nc.dram_tensor("gate_wT", [DM, NEXP], F32, kind="ExternalInput").ap()
    w1_d = nc.dram_tensor("w1", [NEXP, DM, HID], F32, kind="ExternalInput").ap()
    w3_d = nc.dram_tensor("w3", [NEXP, DM, HID], F32, kind="ExternalInput").ap()
    w2_d = nc.dram_tensor("w2", [NEXP, HID, DM], F32, kind="ExternalInput").ap()
    w_in_d = nc.dram_tensor("w_in", [DM, DI], F32, kind="ExternalInput").ap()
    w_gate_d = nc.dram_tensor("w_gate", [DM, DI], F32, kind="ExternalInput").ap()
    w_out_d = nc.dram_tensor("w_out", [DI, DM], F32, kind="ExternalInput").ap()
    b_in_d = nc.dram_tensor("b_in", [128, NTI], F32, kind="ExternalInput").ap()
    b_gate_d = nc.dram_tensor("b_gate", [128, NTI], F32, kind="ExternalInput").ap()
    b_out_d = nc.dram_tensor("b_out", [128, NCT], F32, kind="ExternalInput").ap()
    A_d = nc.dram_tensor("A", [DI, 256], F32, kind="ExternalInput").ap()
    B_d = nc.dram_tensor("B_ssm", [DI, DS], F32, kind="ExternalInput").ap()
    C_d = nc.dram_tensor("C_ssm", [DI, DS], F32, kind="ExternalInput").ap()
    fwg_d = nc.dram_tensor("f_w_gate", [HID, DM], F32, kind="ExternalInput").ap()
    fwu_d = nc.dram_tensor("f_w_up", [HID, DM], F32, kind="ExternalInput").ap()
    fwd_d = nc.dram_tensor("f_w_down", [DM, HID], F32, kind="ExternalInput").ap()
    cst_d = {nm: nc.dram_tensor(nm, shp, dt, kind="ExternalInput").ap()
             for nm, shp, dt in CONST_SPECS}
    xln_dram = nc.dram_tensor("m_xln_scratch", [DM, L], BF16, kind="Internal").ap()
    w2q_dram = nc.dram_tensor("m_w2q_scratch", [NEXP * 2 * NHT * 128, 512], BF16, kind="Internal").ap()
    w3q_dram = nc.dram_tensor("m_w3q_scratch", [NEXP * NHT * 128, NCT * 128], BF16, kind="Internal").ap()
    wdq_dram = nc.dram_tensor("f_wdq_scratch", [NHT * 128, DM], BF16, kind="Internal").ap()
    xout = nc.dram_tensor("xoutT", [DM, L], F32, kind="ExternalOutput").ap()

    with Env(nc) as env:
        cst = {nm: env.sb(nm + "_sb", shp, dt) for nm, shp, dt in CONST_SPECS}
        for nm, val in (("epsc", EPS), ("onec", 1.0), ("neg1c", -1.0), ("neghc", -0.5)):
            cst[nm] = env.sb(nm, [128, 1], F32)
            cst[nm + "_val"] = val
        xfull = env.sb("xfull", [128, NCT * L], F32)

        def p0_v(v):
            for nm in ("epsc", "onec", "neg1c", "neghc"):
                last = v.memset(cst[nm][:], cst[nm + "_val"])
            env.inc(last, "V")

        def p0_sy(sy):
            for nm, shp, dt in CONST_SPECS:
                sy.dma_start(out=cst[nm][:], in_=cst_d[nm]).then_inc(env.sems["LD"], 16)
                env.cnt["LD"] += 16
            last = sy.dma_start(out=xfull[:].rearrange("p (a t) -> p a t", t=L),
                                in_=xin.rearrange("(a p) t -> p a t", p=128))
            env.inc(last, "LD", 16)
        env.phase([("sync", p0_sy), ("vector", p0_v)])

        with env.scope():
            emit_gla(env, cst, xfull, gdr, bg)
        with env.scope():
            emit_moe(env, cst, xfull, xln_dram, w2q_dram, w3q_dram, gate_wT_d, w1_d, w3_d, w2_d)
        with env.scope():
            emit_ssm(env, cst, xfull, w_in_d, w_gate_d, w_out_d,
                     b_in_d, b_gate_d, b_out_d, A_d, B_d, C_d)
        with env.scope():
            emit_fp4ffn(env, cst, xfull, wdq_dram, fwg_d, fwu_d, fwd_d)

        def pf_sy(sy):
            last = sy.dma_start(out=xout.rearrange("(a p) t -> p a t", p=128),
                                in_=xfull[:].rearrange("p (a t) -> p a t", t=L))
            env.inc(last, "ST", 16)
        env.phase([("sync", pf_sy)])

        @env.block.sync
        def _(sy):
            sy.wait_ge(env.sems["ST"], env.cnt["ST"])
    return nc


def kernel(**inputs):
    ins = {k: np.asarray(v) for k, v in inputs.items()}
    x = np.asarray(ins["L0_attn_x"], np.float32)
    B = x.shape[0]
    cores = list(range(B))
    consts = host_consts()
    nc = build_fused()
    im = []
    for b in range(B):
        m = {"xT": np.ascontiguousarray(x[b].T),
             "wq": ins["L0_attn_Wq"], "wk": ins["L0_attn_Wk"],
             "wv": ins["L0_attn_Wv"], "wg": ins["L0_attn_Wg"],
             "bg": np.ascontiguousarray(np.asarray(ins["L0_attn_bg"]).reshape(1, -1)),
             "wo": permute_wo_rows(np.asarray(ins["L0_attn_Wo"])),
             "gate_wT": np.ascontiguousarray(np.asarray(ins["L0_ffn_gate_w"]).T),
             "w1": ins["L0_ffn_w1"], "w3": ins["L0_ffn_w3"], "w2": ins["L0_ffn_w2"],
             "w_in": ins["L1_attn_w_in"], "w_gate": ins["L1_attn_w_gate"],
             "w_out": ins["L1_attn_w_out"],
             "b_in": np.ascontiguousarray(np.asarray(ins["L1_attn_b_in"]).reshape(NTI, 128).T),
             "b_gate": np.ascontiguousarray(np.asarray(ins["L1_attn_b_gate"]).reshape(NTI, 128).T),
             "b_out": np.ascontiguousarray(np.asarray(ins["L1_attn_b_out"]).reshape(NCT, 128).T),
             "A": np.ascontiguousarray(np.asarray(ins["L1_attn_A"]).transpose(0, 2, 1).reshape(DI, 256)),
             "B_ssm": ins["L1_attn_B_ssm"], "C_ssm": ins["L1_attn_C_ssm"],
             "f_w_gate": ins["L1_ffn_w_gate"], "f_w_up": ins["L1_ffn_w_up"],
             "f_w_down": ins["L1_ffn_w_down"]}
        m.update({k: consts[k] for k, _s, _d in CONST_SPECS})
        im.append(m)
    res = run_bass_kernel_spmd(nc, im, core_ids=cores)
    out = np.stack([np.ascontiguousarray(res.results[b]["xoutT"]).T for b in range(B)])
    return out.astype(np.float32)


# version 06
# speedup vs baseline: 1.1651x; 1.1651x over previous
import numpy as np
from contextlib import ExitStack, contextmanager
import concourse.bass as bass
import concourse.mybir as mybir
from concourse.bass_utils import run_bass_kernel_spmd
from concourse.alu_op_type import AluOpType as ALU

F32 = mybir.dt.float32
BF16 = mybir.dt.bfloat16
U32 = mybir.dt.uint32
AF = mybir.ActivationFunctionType
AX = mybir.AxisListType

L = 2048
DM = 1024
NCT = DM // 128
SEG = 256
NSEG = L // SEG
GNTT = SEG // 128
C = 32
H, D = 16, 64
INNER = 1024
EPS = 1e-5
STOP_AFTER = None
SSM_DEBUG = False
P4_MODE = "full"

ENG_SEM = {"tensor": "T", "vector": "V", "scalar": "S", "gpsimd": "G", "sync": "Y"}
ALL_SEMS = ("T", "V", "S", "G", "Y", "LD", "LG", "ST")



def host_consts():
    import ml_dtypes
    BF = ml_dtypes.bfloat16
    c = {}
    c["ident"] = np.eye(128, dtype=np.float32).astype(BF)
    c["identf"] = np.eye(128, dtype=np.float32)
    c["ones128"] = np.ones((128, 128), np.float32).astype(BF)
    c["ones128f"] = np.ones((128, 128), np.float32)
    m = (np.arange(32)[:, None] <= np.arange(32)[None, :]).astype(np.float32)
    c["maskA"] = np.tile(m, (4, 16)).astype(BF)
    bt = np.zeros((128, 128), np.float32)
    for b in range(4):
        bt[b * 32:(b + 1) * 32, b * 32:(b + 1) * 32] = -m
    c["btrilneg"] = bt
    e8 = np.zeros((8, 8 * 128), np.float32)
    for e in range(8):
        e8[e, e * 128:(e + 1) * 128] = 1.0
    c["esel8"] = e8.astype(BF)
    e4 = np.zeros((128, 256), np.float32)
    for cc in range(4):
        e4[32 * cc + 31, cc * 64:(cc + 1) * 64] = 1.0
    c["esel4"] = e4.astype(BF)
    return c


CONST_SPECS = [("ident", [128, 128], BF16), ("identf", [128, 128], F32),
               ("esel8", [8, 1024], BF16), ("ones128", [128, 128], BF16),
               ("ones128f", [128, 128], F32), ("maskA", [128, 512], BF16),
               ("btrilneg", [128, 128], F32), ("esel4", [128, 256], BF16)]



class Env:
    def __init__(self, nc):
        self.nc = nc
        self.ctx = ExitStack()
        self.block = None
        self.sems = {}
        self.cnt = {}

    def __enter__(self):
        self.ctx.__enter__()
        self._scopes = []
        for name in ALL_SEMS:
            self.sems[name] = self.ctx.enter_context(self.nc.semaphore(name))
            self.cnt[name] = 0
        self.block = self.ctx.enter_context(self.nc.Block())
        return self

    @contextmanager
    def scope(self):
        st = ExitStack()
        self._scopes.append(st)
        try:
            yield
        finally:
            st.close()
            self._scopes.pop()

    def __exit__(self, *a):
        return self.ctx.__exit__(*a)

    def _actx(self):
        return self._scopes[-1] if getattr(self, "_scopes", None) else self.ctx

    def sb(self, name, shape, dtype=F32):
        return self._actx().enter_context(self.nc.sbuf_tensor(name, shape, dtype))

    def ps(self, name, shape, dtype=F32):
        return self._actx().enter_context(self.nc.psum_tensor(name, shape, dtype))

    def inc(self, inst, sem, n=1):
        inst.then_inc(self.sems[sem], n)
        self.cnt[sem] += n
        return inst

    def phase(self, parts):
        self.nphase = getattr(self, "nphase", 0) + 1
        if STOP_AFTER is not None and self.nphase > STOP_AFTER:
            return
        snap = {k: self.cnt[k] for k in ALL_SEMS}
        for name, fn in parts:
            dec = getattr(self.block, name)
            own = ENG_SEM[name]

            def body(eng, fn=fn, own=own):
                for k in ALL_SEMS:
                    if k != own and snap[k] > 0:
                        eng.wait_ge(self.sems[k], snap[k])
                before = sum(self.cnt.values())
                fn(eng)
                assert sum(self.cnt.values()) > before, "phase fn must inc a sem"
            dec(body)



def emit_ln(env, cst, xsrc, xln, scratch, nct=NCT, seg_w=SEG,
            src_stride=None, src_off=0, dst_stride=None, dst_off=0):
    ss = src_stride if src_stride is not None else seg_w
    ds = dst_stride if dst_stride is not None else seg_w
    sq, mu, var, rstd, lntmp = (scratch[k] for k in ("sqseg", "mu", "var", "rstd", "lntmp"))
    pS, pQ = scratch["pS"], scratch["pQ"]
    ones_sq = cst["ones128f"] if sq[:].dtype == F32 else cst["ones128"]
    bs = env.cnt["S"]
    bt = env.cnt["T"]
    bv = env.cnt["V"]

    def ssl(ct):
        return slice(ct * ss + src_off, ct * ss + src_off + seg_w)

    def f_s(s):
        for ct in range(nct):
            last = s.activation(sq[:, ct * seg_w:(ct + 1) * seg_w], xsrc[:, ssl(ct)], AF.Square)
        env.inc(last, "S")
        s.wait_ge(env.sems["V"], bv + 3)
        s.activation(lntmp[:], var[:], AF.Ln, bias=cst["epsc"][:])
        last = s.activation(rstd[:], lntmp[:], AF.Exp, scale=cst["neghc"][:])
        env.inc(last, "S")

    def f_t(t):
        for ct in range(nct):
            t.matmul(pS, cst["ones128f"][:], xsrc[:, ssl(ct)],
                     start=(ct == 0), stop=(ct == nct - 1))
        t.wait_ge(env.sems["S"], bs + 1)
        for ct in range(nct):
            last = t.matmul(pQ, ones_sq[:], sq[:, ct * seg_w:(ct + 1) * seg_w],
                            start=(ct == 0), stop=(ct == nct - 1))
        env.inc(last, "T")

    def f_v(v):
        v.wait_ge(env.sems["T"], bt + 1)
        inst = v.tensor_scalar(mu[:], pS, 1.0 / DM, None, ALU.mult)
        env.inc(inst, "V"); v.wait_ge(env.sems["V"], env.cnt["V"])
        inst = v.tensor_tensor(var[:], mu[:], mu[:], ALU.mult)
        env.inc(inst, "V"); v.wait_ge(env.sems["V"], env.cnt["V"])
        inst = v.scalar_tensor_tensor(var[:], pQ, 1.0 / DM, var[:], ALU.mult, ALU.subtract)
        env.inc(inst, "V")
        v.wait_ge(env.sems["S"], bs + 2)
        for ct in range(nct):
            v.tensor_tensor(lntmp[:], xsrc[:, ssl(ct)], mu[:], ALU.subtract)
            last = v.tensor_tensor(xln[:, ct * ds + dst_off: ct * ds + dst_off + seg_w],
                                   lntmp[:], rstd[:], ALU.mult)
        env.inc(last, "V")
    env.phase([("scalar", f_s), ("tensor", f_t), ("vector", f_v)])


def emit_proj_feat(env, dst, w_sb, src, pbank0, pbank1, njt=8, nct=NCT, seg_w=SEG,
                   evac="copy", evac_arg=None, dst_slicer=None, evac_custom=None):
    base_t = env.cnt["T"]
    jw = njt * 128
    ev_eng = "vector" if evac_custom is None else evac_custom[0]
    ev_sem = ENG_SEM[ev_eng]
    base_e = env.cnt[ev_sem]

    def _hw(h):
        return bass.AP(h.tensor, h.offset, [h.ap[0], [1, seg_w]])

    def f_t(t):
        for jt in range(njt):
            half = _hw(pbank0 if jt % 2 == 0 else pbank1)
            if jt >= 2:
                t.wait_ge(env.sems[ev_sem], base_e + jt - 1)
            for ct in range(nct):
                last = t.matmul(half, w_sb[:, ct * jw + jt * 128: ct * jw + (jt + 1) * 128],
                                src[:, ct * seg_w:(ct + 1) * seg_w],
                                start=(ct == 0), stop=(ct == nct - 1))
            env.inc(last, "T")

    def f_e(eng):
        for jt in range(njt):
            half = _hw(pbank0 if jt % 2 == 0 else pbank1)
            eng.wait_ge(env.sems["T"], base_t + jt + 1)
            if evac_custom is not None:
                last = evac_custom[1](eng, jt, half)
            else:
                dsl = dst_slicer(jt) if dst_slicer else dst[:, jt * seg_w:(jt + 1) * seg_w]
                if evac == "copy":
                    last = eng.tensor_copy(dsl, half)
                else:
                    last = eng.tensor_tensor(dsl, half, evac_arg[:, jt * seg_w:(jt + 1) * seg_w], ALU.add)
            env.inc(last, ev_sem)
    env.phase([("tensor", f_t), (ev_eng, f_e)])


def emit_proj_tok(env, dst, w_sb, xln, pb0, pb1, bias_row=None, ones128=None,
                  ntc=None, nct=NCT, seg_w=SEG, jw=1024):
    if ntc is None:
        ntc = seg_w // 128
    """Token-major projection: dst (128, ntc*jw) bf16, dst[t, j] = sum_c x[c,t] W[c,j]."""
    base_t = env.cnt["T"]
    base_v = env.cnt["V"]

    def f_t(t):
        for tc in range(ntc):
            pb = pb0 if tc % 2 == 0 else pb1
            if tc >= 2:
                t.wait_ge(env.sems["V"], base_v + tc - 1)
            nhalf = jw // 512
            for ct in range(nct):
                stat = xln[:, ct * seg_w + tc * 128: ct * seg_w + (tc + 1) * 128]
                mv = w_sb[:, ct * jw:(ct + 1) * jw]
                for hf in range(nhalf):
                    last = t.matmul(pb[:, hf * 512:(hf + 1) * 512], stat, mv[:, hf * 512:(hf + 1) * 512],
                                    start=(ct == 0), stop=(ct == nct - 1 and bias_row is None))
            if bias_row is not None:
                oc = ones128[0:1, 0:128]
                for hf in range(nhalf):
                    last = t.matmul(pb[:, hf * 512:(hf + 1) * 512], oc, bias_row[0:1, hf * 512:(hf + 1) * 512],
                                    start=False, stop=True)
            env.inc(last, "T")

    def f_v(v):
        for tc in range(ntc):
            pb = pb0 if tc % 2 == 0 else pb1
            v.wait_ge(env.sems["T"], base_t + tc + 1)
            last = v.tensor_copy(dst[:, tc * jw:(tc + 1) * jw], pb[:, 0:jw])
            env.inc(last, "V")
    env.phase([("tensor", f_t), ("vector", f_v)])



def emit_gla(env, cst, xfull, dr, bg_d):
    nc = env.nc

    w_sb = {nm: env.sb("g_" + nm + "_wsb", [128, NCT * 1024], BF16)
            for nm in ("wq", "wk", "wv", "wg", "wo")}
    bgrow = env.sb("g_bgrow", [1, INNER], BF16)

    def p0_gp(gp):
        for nm in ("wq", "wk", "wv", "wg", "wo"):
            r = dr[nm].rearrange("(a p) j -> p a j", p=128)
            last = gp.dma_start(out=w_sb[nm][:].rearrange("p (a j) -> p a j", j=1024), in_=r)
            env.inc(last, "LG", 16)
        last = gp.dma_start(out=bgrow[:], in_=bg_d)
        env.inc(last, "LG", 16)
    env.phase([("gpsimd", p0_gp)])

    xln = env.sb("g_xln", [128, NCT * SEG], BF16)
    scratch = {}
    for nm in ("mu", "var", "rstd"):
        scratch[nm] = env.sb("g_" + nm, [128, SEG], F32)
    scratch["msq"] = scratch["var"]
    scratch["lntmp"] = scratch["var"]

    QT = env.sb("g_QT", [128, NCT * SEG], BF16)
    KT = env.sb("g_KT", [128, NCT * SEG], BF16)
    Ktok = env.sb("g_Ktok", [128, GNTT * 1024], BF16)
    Vtok = env.sb("g_Vtok", [128, GNTT * 1024], BF16)
    cum = env.sb("g_cum", [128, GNTT * 1024], BF16)
    lgn = env.sb("g_lgn", [128, 512], F32)
    Am = env.sb("g_Am", [128, 512], BF16)
    S = env.sb("g_S", [128, 512], F32)
    Sbf = env.sb("g_Sbf", [128, 512], BF16)
    otok = env.sb("g_otok", [128, 1024], BF16)
    oT = env.sb("g_oT", [128, NCT * SEG], BF16)
    scratch["sqseg"] = oT
    expneg = otok

    pB = env.ps("g_pB", [128, 1024], F32)
    pI = env.ps("g_pI", [128, 1024], F32)
    pU = env.ps("g_pU", [128, 512], F32)
    pC = env.ps("g_pC", [128, 512], F32)
    pT = env.ps("g_pT", [128, 1024], BF16)
    pb0, pb1 = pB, pI
    scratch["pS"] = pU[:, 0:SEG]
    scratch["pQ"] = pC[:, 0:SEG]
    tmpB = env.sb("g_tmpB", [128, 512], F32)
    etile = tmpB
    Stmp = tmpB

    def p0_v(v):
        v.memset(S[:], 0.0)
        last = v.memset(Sbf[:], 0.0)
        env.inc(last, "V")
    env.phase([("vector", p0_v)])

    for seg in range(NSEG):
        t0 = seg * SEG

        emit_ln(env, cst, xfull, xln, scratch, src_stride=L, src_off=t0)

        emit_proj_feat(env, QT, w_sb["wq"], xln, pb0[:, 0:512], pb0[:, 512:1024])
        emit_proj_feat(env, KT, w_sb["wk"], xln, pb0[:, 0:512], pb0[:, 512:1024])
        emit_proj_tok(env, Vtok, w_sb["wv"], xln, pb0, pb1)
        emit_proj_tok(env, Ktok, w_sb["wk"], xln, pb0, pb1)

        for tc in range(GNTT):
            base_t = env.cnt["T"]
            base_s = env.cnt["S"]

            def g_t(t, tc=tc, base_t=base_t, base_s=base_s):
                for ct in range(NCT):
                    stat = xln[:, ct * SEG + tc * 128: ct * SEG + (tc + 1) * 128]
                    mv = w_sb["wg"][:, ct * 1024:(ct + 1) * 1024]
                    t.matmul(pb0[:, 0:512], stat, mv[:, 0:512], start=(ct == 0), stop=False)
                    t.matmul(pb0[:, 512:1024], stat, mv[:, 512:1024], start=(ct == 0), stop=False)
                oc = cst["ones128"][0:1, 0:128]
                t.matmul(pb0[:, 0:512], oc, bgrow[0:1, 0:512], start=False, stop=True)
                last = t.matmul(pb0[:, 512:1024], oc, bgrow[0:1, 512:1024], start=False, stop=True)
                env.inc(last, "T")
                for hf in range(2):
                    t.wait_ge(env.sems["S"], base_s + 1 + hf)
                    last = t.matmul(pb1[:, hf * 512:(hf + 1) * 512], cst["btrilneg"][:],
                                    lgn[:], start=True, stop=True)
                    env.inc(last, "T")

            def g_s(s, tc=tc, base_t=base_t):
                s.wait_ge(env.sems["T"], base_t + 1)
                for hf in range(2):
                    if hf == 1:
                        s.wait_ge(env.sems["T"], base_t + 2)
                    hsl = slice(hf * 512, (hf + 1) * 512)
                    s.activation(etile[:], pb0[:, hsl], AF.Exp, scale=cst["neg1c"][:])
                    last = s.activation(lgn[:], etile[:], AF.Ln, bias=cst["onec"][:])
                    env.inc(last, "S")
                s.wait_ge(env.sems["T"], base_t + 3)
                s.activation(expneg[:], pb1[:], AF.Exp, scale=cst["neg1c"][:])
                last = s.activation(cum[:, tc * 1024:(tc + 1) * 1024], pb1[:], AF.Exp)
                env.inc(last, "S")

            def g_v(v, tc=tc, base_s=base_s):
                v.wait_ge(env.sems["S"], base_s + 3)
                sl = slice(tc * 1024, (tc + 1) * 1024)
                last = v.tensor_tensor(Vtok[:, sl], expneg[:], Vtok[:, sl], ALU.mult)
                env.inc(last, "V")
            env.phase([("tensor", g_t), ("scalar", g_s), ("vector", g_v)])

        base_t = env.cnt["T"]
        base_v = env.cnt["V"]
        base_s = env.cnt["S"]

        def p4_t(t):
            for tt in range(GNTT):
                if tt >= 1:
                    t.wait_ge(env.sems["V"], base_v + (tt - 1) * 8 + 5)
                for cc in range(4):
                    base = 32 * cc
                    for h in [hh for hh in range(H) if hh % 2 == 0] + \
                             [hh for hh in range(H) if hh % 2 == 1]:
                        jt = h // 2
                        b64 = 64 * (h % 2)
                        dstp = pU if h % 2 == 0 else pC
                        tsl = slice(jt * SEG + tt * 128 + cc * 32,
                                    jt * SEG + tt * 128 + cc * 32 + 32)
                        last = t.matmul(dstp[base:base + 32, 32 * jt:32 * jt + 32],
                                        KT[b64:b64 + 64, tsl], QT[b64:b64 + 64, tsl],
                                        start=True, stop=True, tile_position=(b64, base))
                env.inc(last, "T")
                for cc in range(4):
                    base = 32 * cc
                    t.wait_ge(env.sems["V"], base_v + tt * 8 + 1 + cc)
                    vtile = Vtok[:, tt * 1024:(tt + 1) * 1024]
                    ktile = Ktok[:, tt * 1024:(tt + 1) * 1024]
                    for h in [hh for hh in range(H) if hh % 2 == 0] + \
                             [hh for hh in range(H) if hh % 2 == 1]:
                        jt = h // 2
                        b64 = 64 * (h % 2)
                        ocol = 64 * jt + 512 * (h % 2)
                        tsl = slice(jt * SEG + tt * 128 + cc * 32,
                                    jt * SEG + tt * 128 + cc * 32 + 32)
                        t.matmul(pB[base:base + 32, ocol:ocol + 64],
                                 QT[b64:b64 + 64, tsl], Sbf[b64:b64 + 64, 64 * jt:64 * jt + 64],
                                 start=True, stop=True, tile_position=(b64, base))
                    for h in range(H):
                        jt = h // 2
                        acol = 32 * jt + 256 * (h % 2)
                        ocol = 64 * jt + 512 * (h % 2)
                        t.matmul(pI[base:base + 32, ocol:ocol + 64],
                                 Am[base:base + 32, acol:acol + 32],
                                 vtile[base:base + 32, 64 * h:64 * h + 64],
                                 start=True, stop=True, tile_position=(base, base))
                    for h in range(H):
                        jt = h // 2
                        b64 = 64 * (h % 2)
                        last = t.matmul(pU[b64:b64 + 64, 64 * jt:64 * jt + 64],
                                        ktile[base:base + 32, 64 * h:64 * h + 64],
                                        vtile[base:base + 32, 64 * h:64 * h + 64],
                                        start=True, stop=True, tile_position=(base, b64))
                    ctile = cum[:, tt * 1024:(tt + 1) * 1024]
                    ceven = ctile[:, 0:1024]
                    ap_ev = bass.AP(ceven.tensor, ceven.offset, [ceven.ap[0], [128, 8], [1, 64]])
                    codd = ctile[:, 64:1024]
                    ap_od = bass.AP(codd.tensor, codd.offset, [codd.ap[0], [128, 8], [1, 64]])
                    t.matmul(pC[0:64, 0:512], esel_slice(cst, cc), ap_ev,
                             start=True, stop=True, tile_position=(0, 0))
                    last = t.matmul(pC[64:128, 0:512], esel_slice(cst, cc), ap_od,
                                    start=True, stop=True, tile_position=(0, 64))
                    env.inc(last, "T")
                t.wait_ge(env.sems["V"], base_v + tt * 8 + 7)
                for k in range(8):
                    last = t.transpose(pT[:, k * 128:(k + 1) * 128],
                                       otok[:, k * 128:(k + 1) * 128], cst["ident"][:])
                env.inc(last, "T")

        def p4_s(s):
            for tt in range(GNTT):
                s.wait_ge(env.sems["T"], base_t + tt * 6 + 5)
                s.wait_ge(env.sems["V"], base_v + tt * 8 + 5)
                last = s.activation(tmpB[:], pB[:, 0:512], AF.Copy)
                env.inc(last, "S")
                s.wait_ge(env.sems["V"], base_v + tt * 8 + 6)
                last = s.activation(tmpB[:], pB[:, 512:1024], AF.Copy)
                env.inc(last, "S")

        def p4_v(v):
            tv = base_t
            for tt in range(GNTT):
                v.wait_ge(env.sems["T"], tv + 1); tv += 1
                v.tensor_tensor(Am[:, 0:256], pU[:, 0:256], cst["maskA"][:, 0:256], ALU.mult)
                last = v.tensor_tensor(Am[:, 256:512], pC[:, 0:256], cst["maskA"][:, 0:256], ALU.mult)
                env.inc(last, "V")
                for cc in range(4):
                    v.wait_ge(env.sems["T"], tv + 1); tv += 1
                    v.tensor_tensor(Stmp[:], S[:], pU[:], ALU.add)
                    v.tensor_tensor(S[:], Stmp[:], pC[:], ALU.mult)
                    last = v.tensor_copy(Sbf[:], S[:])
                    env.inc(last, "V")
                for hf in range(2):
                    v.wait_ge(env.sems["S"], base_s + 2 * tt + 1 + hf)
                    hsl = slice(hf * 512, (hf + 1) * 512)
                    v.tensor_tensor(tmpB[:], pI[:, hsl], tmpB[:], ALU.add)
                    ctile = cum[:, tt * 1024:(tt + 1) * 1024]
                    cperm = bass.AP(ctile.tensor, ctile.offset + 64 * hf,
                                    [ctile.ap[0], [128, 8], [1, 64]])
                    otok3 = otok[:, hsl].rearrange("p (b c) -> p b c", c=64)
                    last = v.tensor_tensor(otok3, tmpB[:].rearrange("p (b c) -> p b c", c=64),
                                           cperm, ALU.mult)
                    env.inc(last, "V")
                v.wait_ge(env.sems["T"], tv + 1); tv += 1
                ot3 = oT[:].rearrange("p (a t) -> p a t", t=SEG)[:, :, tt * 128:(tt + 1) * 128]
                last = v.tensor_copy(ot3, pT[:].rearrange("p (a t) -> p a t", t=128))
                env.inc(last, "V")
        env.phase([("tensor", p4_t), ("scalar", p4_s), ("vector", p4_v)])

        def evac_res(eng, jt, half, t0=t0):
            dsl = xfull[:, jt * L + t0: jt * L + t0 + SEG]
            return eng.tensor_tensor(dsl, half, dsl, ALU.add)
        emit_proj_feat(env, None, w_sb["wo"], oT, pb0[:, 0:512], pb0[:, 512:1024],
                       evac_custom=("vector", evac_res))


def esel_slice(cst, cc):
    return cst["esel4"][:, cc * 64:(cc + 1) * 64]


def permute_wo_rows(wo):
    w = wo.reshape(2 * 8, 64, wo.shape[1])
    perm = [2 * jt + p for p in range(2) for jt in range(8)]
    return np.ascontiguousarray(w[perm].reshape(wo.shape))



def build_gla():
    nc = bass.Bass("TRN2", target_bir_lowering=False, detect_race_conditions=False)
    xin = nc.dram_tensor("xT", [DM, L], F32, kind="ExternalInput").ap()
    dr = {}
    for nm, shp in (("wq", [DM, INNER]), ("wk", [DM, INNER]), ("wv", [DM, INNER]),
                    ("wg", [DM, INNER]), ("wo", [INNER, DM])):
        dr[nm] = nc.dram_tensor(nm, shp, F32, kind="ExternalInput").ap()
    bg = nc.dram_tensor("bg", [1, INNER], F32, kind="ExternalInput").ap()
    GLA_CONSTS = [c for c in CONST_SPECS if c[0] != "esel8" and c[0] != "identf"]
    cst_d = {nm: nc.dram_tensor(nm, shp, dt, kind="ExternalInput").ap()
             for nm, shp, dt in GLA_CONSTS}
    xout = nc.dram_tensor("xoutT", [DM, L], F32, kind="ExternalOutput").ap()

    with Env(nc) as env:
        cst = {nm: env.sb(nm + "_sb", shp, dt) for nm, shp, dt in GLA_CONSTS}
        for nm, val in (("epsc", EPS), ("onec", 1.0), ("neg1c", -1.0), ("neghc", -0.5)):
            cst[nm] = env.sb(nm, [128, 1], F32)
            cst[nm + "_val"] = val
        xfull = env.sb("xfull", [128, NCT * L], F32)

        def p0_v(v):
            for nm in ("epsc", "onec", "neg1c", "neghc"):
                last = v.memset(cst[nm][:], cst[nm + "_val"])
            env.inc(last, "V")

        def p0_sy(sy):
            for nm, shp, dt in GLA_CONSTS:
                sy.dma_start(out=cst[nm][:], in_=cst_d[nm]).then_inc(env.sems["LD"], 16)
                env.cnt["LD"] += 16
            last = sy.dma_start(out=xfull[:].rearrange("p (a t) -> p a t", t=L),
                                in_=xin.rearrange("(a p) t -> p a t", p=128))
            env.inc(last, "LD", 16)
        env.phase([("sync", p0_sy), ("vector", p0_v)])

        with env.scope():
            emit_gla(env, cst, xfull, dr, bg)

        def pf_sy(sy):
            last = sy.dma_start(out=xout.rearrange("(a p) t -> p a t", p=128),
                                in_=xfull[:].rearrange("p (a t) -> p a t", t=L))
            env.inc(last, "ST", 16)
        env.phase([("sync", pf_sy)])

        @env.block.sync
        def _(sy):
            sy.wait_ge(env.sems["ST"], env.cnt["ST"])
    return nc


def run_gla(x_full, Wq, Wk, Wv, Wg, bgv, Wo):
    consts = host_consts()
    nc = build_gla()
    in_maps = []
    B = x_full.shape[0]
    for b in range(B):
        m = {"xT": np.ascontiguousarray(x_full[b].T),
             "wq": Wq, "wk": Wk, "wv": Wv, "wg": Wg,
             "bg": np.ascontiguousarray(bgv.reshape(1, -1)), "wo": permute_wo_rows(Wo)}
        m.update(consts)
        in_maps.append(m)
    res = run_bass_kernel_spmd(nc, in_maps, core_ids=list(range(B)))
    return np.stack([np.ascontiguousarray(res.results[b]["xoutT"]).T for b in range(B)])





DI = 2048
DS = 16
NTI = DI // 128
T0 = 32
SSEG = 512
XPW = SSEG + 32


def emit_ssm(env, cst, xfull, w_in_d, w_gate_d, w_out_d,
             b_in_d, b_gate_d, b_out_d, A_d, B_d, C_d):
    nc = env.nc

    xln = env.sb("s_xln", [128, NCT * SSEG], BF16)
    scratch = {}
    for nm in ("mu", "msq", "var", "rstd", "lntmp"):
        scratch[nm] = env.sb("s_" + nm, [128, SSEG], F32)
    wbuf = env.sb("s_wbuf", [128, 16384], BF16)
    xpseg = env.sb("s_xp", [128, NTI * XPW], BF16)
    y1 = env.sb("s_y1", [128, NTI * SSEG], F32)
    y2 = env.sb("s_y2", [128, NTI * SSEG], BF16)
    gate = env.sb("s_gate", [128, NTI * SSEG], BF16)
    scratch["sqseg"] = gate
    kap = env.sb("s_kap", [128, NTI * T0], F32)
    vA = env.sb("s_vA", [128, NTI * DS], F32)
    vB = env.sb("s_vB", [128, NTI * DS], F32)
    vtmp = env.sb("s_vtmp", [128, 256], F32)
    ktmp = env.sb("s_ktmp", [128, DS], F32)
    Bbuf = env.sb("s_B", [128, NTI * DS], F32)
    Cbuf = env.sb("s_C", [128, NTI * DS], F32)
    bincol = env.sb("s_bin", [128, NTI], F32)
    bgcol = env.sb("s_bg", [128, NTI], F32)
    bocol = env.sb("s_bo", [128, NCT], F32)

    pB = env.ps("s_pB", [128, 1024], F32)
    pI = env.ps("s_pI", [128, 1024], F32)
    pU = env.ps("s_pU", [128, 512], F32)
    pC = env.ps("s_pC", [128, 512], F32)
    scratch["pS"] = pU[:, 0:SSEG]
    scratch["pQ"] = pC[:, 0:SSEG]

    Abuf = y1

    def p_ld(sy):
        sy.dma_start(out=bincol[:], in_=b_in_d).then_inc(env.sems["LD"], 16)
        env.cnt["LD"] += 16
        sy.dma_start(out=bgcol[:], in_=b_gate_d).then_inc(env.sems["LD"], 16)
        env.cnt["LD"] += 16
        sy.dma_start(out=bocol[:], in_=b_out_d).then_inc(env.sems["LD"], 16)
        env.cnt["LD"] += 16
        sy.dma_start(out=Abuf[:, 0:NTI * 256].rearrange("p (a m) -> p a m", m=256),
                     in_=A_d.rearrange("(a p) m -> p a m", p=128)).then_inc(env.sems["LD"], 16)
        env.cnt["LD"] += 16
        sy.dma_start(out=Bbuf[:].rearrange("p (a n) -> p a n", n=DS),
                     in_=B_d.rearrange("(a p) n -> p a n", p=128)).then_inc(env.sems["LD"], 16)
        env.cnt["LD"] += 16
        last = sy.dma_start(out=Cbuf[:].rearrange("p (a n) -> p a n", n=DS),
                            in_=C_d.rearrange("(a p) n -> p a n", p=128))
        env.inc(last, "LD", 16)
    env.phase([("sync", p_ld)])

    def kprec(eng, tiles, vt, kt, own):
        def fn(e):
            def step(inst):
                env.inc(inst, own)
                e.wait_ge(env.sems[own], env.cnt[own])
            for ct in tiles:
                vsl = slice(ct * DS, (ct + 1) * DS)
                step(e.tensor_tensor(kt[:], Cbuf[:, vsl], Bbuf[:, vsl], ALU.mult))
                step(e.tensor_reduce(kap[:, ct * T0:ct * T0 + 1], kt[:], AX.X, ALU.add))
                vcur = Bbuf
                for tau in range(1, T0):
                    vnext = vA if vcur is not vA else vB
                    a3 = Abuf[:, ct * 256:(ct + 1) * 256].rearrange("p (m n) -> p m n", n=DS)
                    vc = vcur[:, vsl]
                    vbc = bass.AP(vc.tensor, vc.offset, [vc.ap[0], [1, DS], [0, DS]])
                    vt_mn = bass.AP(vt[:].tensor, vt[:].offset,
                                    [vt[:].ap[0], [1, DS], [DS, DS]])
                    step(e.tensor_tensor(vt_mn, a3, vbc, ALU.mult))
                    step(e.tensor_reduce(vnext[:, vsl], vt[:].rearrange("p (n m) -> p n m", m=DS),
                                         AX.X, ALU.add))
                    step(e.tensor_tensor(kt[:], Cbuf[:, vsl], vnext[:, vsl], ALU.mult))
                    step(e.tensor_reduce(kap[:, ct * T0 + tau:ct * T0 + tau + 1], kt[:],
                                         AX.X, ALU.add))
                    vcur = vnext
        return fn
    env.phase([("vector", kprec(None, range(0, NTI), vtmp, ktmp, "V"))])

    def p_ms(v):
        last = v.memset(xpseg[:], 0.0)
        env.inc(last, "V")
    env.phase([("vector", p_ms)])

    for s in range(L // SSEG):
        t0 = s * SSEG
        emit_ln(env, cst, xfull, xln, scratch, seg_w=SSEG,
                src_stride=L, src_off=t0)

        def ld_w(gp, wd=w_in_d, n=16):
            r = wd.rearrange("(a p) j -> p a j", p=128)
            last = gp.dma_start(out=wbuf[:].rearrange("p (a j) -> p a j", j=r.shape[2]), in_=r)
            env.inc(last, "LG", 16)
        env.phase([("gpsimd", ld_w)])

        def evac_xp(eng, jt, half):
            return eng.activation(xpseg[:, jt * XPW + 32: jt * XPW + 32 + SSEG], half,
                                  AF.Identity, bias=bincol[:, jt:jt + 1])
        emit_proj_feat(env, None, wbuf, xln, pB[:, 0:512], pB[:, 512:1024],
                       njt=16, nct=8, seg_w=SSEG, evac_custom=("scalar", evac_xp))

        def ld_wg(gp):
            r = w_gate_d.rearrange("(a p) j -> p a j", p=128)
            last = gp.dma_start(out=wbuf[:].rearrange("p (a j) -> p a j", j=2048), in_=r)
            env.inc(last, "LG", 16)
        env.phase([("gpsimd", ld_wg)])

        def evac_gate(eng, jt, half):
            return eng.activation(gate[:, jt * SSEG:(jt + 1) * SSEG], half,
                                  AF.Sigmoid, bias=bgcol[:, jt:jt + 1])
        emit_proj_feat(env, None, wbuf, xln, pB[:, 0:512], pB[:, 512:1024],
                       njt=16, nct=8, seg_w=SSEG, evac_custom=("scalar", evac_gate))

        def conv_v(v):
            for ct in range(NTI):
                ysl = y1[:, ct * SSEG:(ct + 1) * SSEG]
                for tau in range(0, T0):
                    xin = xpseg[:, ct * XPW + 32 - tau: ct * XPW + 32 - tau + SSEG]
                    kcol = kap[:, ct * T0 + tau: ct * T0 + tau + 1]
                    if tau == 0:
                        last = v.tensor_scalar(ysl, xin, kcol, None, ALU.mult)
                    else:
                        last = v.scalar_tensor_tensor(ysl, xin, kcol, ysl, ALU.mult, ALU.add)
            env.inc(last, "V")
        env.phase([("vector", conv_v)])

        def epi_v(v):
            for ct in range(NTI):
                ysl = slice(ct * SSEG, (ct + 1) * SSEG)
                last = v.tensor_tensor(y2[:, ysl], y1[:, ysl], gate[:, ysl], ALU.mult)
            srcap = xpseg[:].rearrange("p (a w) -> p a w", w=XPW)[:, :, SSEG:SSEG + 32]
            dstap = xpseg[:].rearrange("p (a w) -> p a w", w=XPW)[:, :, 0:32]
            last = v.tensor_copy(dstap, srcap)
            env.inc(last, "V")
        env.phase([("vector", epi_v)])

        if SSM_DEBUG and s == 0:
            dbg = {}
            for nm, buf, w in (("d_kap", kap, NTI * T0), ("d_xp", xpseg, NTI * XPW),
                               ("d_y1", y1, NTI * SSEG), ("d_gate", gate, NTI * SSEG),
                               ("d_xln", xln, NCT * SSEG), ("d_y2", y2, NTI * SSEG)):
                dt = {"d_kap": F32, "d_y1": F32}.get(nm, BF16)
                dbg[nm] = nc.dram_tensor(nm, [128, w], dt, kind="ExternalOutput").ap()

            def p_dbg(sy, dbg=dbg):
                for nm, buf in (("d_kap", kap), ("d_xp", xpseg), ("d_y1", y1),
                                ("d_gate", gate), ("d_xln", xln), ("d_y2", y2)):
                    last = sy.dma_start(out=dbg[nm], in_=buf[:])
                    env.inc(last, "ST", 16)
            env.phase([("sync", p_dbg)])

        def ld_wo(gp):
            r = w_out_d.rearrange("(a p) j -> p a j", p=128)
            last = gp.dma_start(out=wbuf[:].rearrange("p (a j) -> p a j", j=1024), in_=r)
            env.inc(last, "LG", 16)
        env.phase([("gpsimd", ld_wo)])

        def evac_out(eng, jt, half, t0=t0):
            dsl = xfull[:, jt * L + t0: jt * L + t0 + SSEG]
            return eng.scalar_tensor_tensor(dsl, half, bocol[:, jt:jt + 1], dsl,
                                            ALU.add, ALU.add)
        emit_proj_feat(env, None, wbuf, y2, pB[:, 0:512], pB[:, 512:1024],
                       njt=8, nct=16, seg_w=SSEG, evac_custom=("vector", evac_out))


def build_ssm():
    nc = bass.Bass("TRN2", target_bir_lowering=False, detect_race_conditions=False)
    xin = nc.dram_tensor("xT", [DM, L], F32, kind="ExternalInput").ap()
    w_in_d = nc.dram_tensor("w_in", [DM, DI], F32, kind="ExternalInput").ap()
    w_gate_d = nc.dram_tensor("w_gate", [DM, DI], F32, kind="ExternalInput").ap()
    w_out_d = nc.dram_tensor("w_out", [DI, DM], F32, kind="ExternalInput").ap()
    b_in_d = nc.dram_tensor("b_in", [128, NTI], F32, kind="ExternalInput").ap()
    b_gate_d = nc.dram_tensor("b_gate", [128, NTI], F32, kind="ExternalInput").ap()
    b_out_d = nc.dram_tensor("b_out", [128, NCT], F32, kind="ExternalInput").ap()
    A_d = nc.dram_tensor("A", [DI, 256], F32, kind="ExternalInput").ap()
    B_d = nc.dram_tensor("B_ssm", [DI, DS], F32, kind="ExternalInput").ap()
    C_d = nc.dram_tensor("C_ssm", [DI, DS], F32, kind="ExternalInput").ap()
    SSM_CONSTS = [c for c in CONST_SPECS if c[0] in ("ones128", "ones128f")]
    cst_d = {nm: nc.dram_tensor(nm, shp, dt, kind="ExternalInput").ap()
             for nm, shp, dt in SSM_CONSTS}
    xout = nc.dram_tensor("xoutT", [DM, L], F32, kind="ExternalOutput").ap()

    with Env(nc) as env:
        cst = {nm: env.sb(nm + "_sb", shp, dt) for nm, shp, dt in SSM_CONSTS}
        for nm, val in (("epsc", EPS), ("onec", 1.0), ("neg1c", -1.0), ("neghc", -0.5)):
            cst[nm] = env.sb(nm, [128, 1], F32)
            cst[nm + "_val"] = val
        xfull = env.sb("s_xfull", [128, NCT * L], F32)

        def p0_v(v):
            for nm in ("epsc", "onec", "neg1c", "neghc"):
                last = v.memset(cst[nm][:], cst[nm + "_val"])
            env.inc(last, "V")

        def p0_sy(sy):
            for nm, shp, dt in SSM_CONSTS:
                sy.dma_start(out=cst[nm][:], in_=cst_d[nm]).then_inc(env.sems["LD"], 16)
                env.cnt["LD"] += 16
            last = sy.dma_start(out=xfull[:].rearrange("p (a t) -> p a t", t=L),
                                in_=xin.rearrange("(a p) t -> p a t", p=128))
            env.inc(last, "LD", 16)
        env.phase([("sync", p0_sy), ("vector", p0_v)])
        emit_ssm(env, cst, xfull, w_in_d, w_gate_d, w_out_d,
                 b_in_d, b_gate_d, b_out_d, A_d, B_d, C_d)

        def pf_sy(sy):
            last = sy.dma_start(out=xout.rearrange("(a p) t -> p a t", p=128),
                                in_=xfull[:].rearrange("p (a t) -> p a t", t=L))
            env.inc(last, "ST", 16)
        env.phase([("sync", pf_sy)])

        @env.block.sync
        def _(sy):
            sy.wait_ge(env.sems["ST"], env.cnt["ST"])
    return nc


def run_ssm(x_full, w_in, b_in, w_gate, b_gate, A, B_ssm, C_ssm, w_out, b_out):
    consts = host_consts()
    nc = build_ssm()
    in_maps = []
    B = x_full.shape[0]
    for b in range(B):
        m = {"xT": np.ascontiguousarray(x_full[b].T),
             "w_in": w_in, "w_gate": w_gate, "w_out": w_out,
             "b_in": np.ascontiguousarray(b_in.reshape(NTI, 128).T),
             "b_gate": np.ascontiguousarray(b_gate.reshape(NTI, 128).T),
             "b_out": np.ascontiguousarray(b_out.reshape(NCT, 128).T),
             "A": np.ascontiguousarray(A.transpose(0, 2, 1).reshape(DI, 256)),
             "B_ssm": B_ssm, "C_ssm": C_ssm}
        m.update(consts)
        in_maps.append(m)
    res = run_bass_kernel_spmd(nc, in_maps, core_ids=list(range(B)))
    return np.stack([np.ascontiguousarray(res.results[b]["xoutT"]).T for b in range(B)])



HID = 2816
NHT = HID // 128
FSEG = 128
FNTC = FSEG // 128


def _q_slot(n, i):
    return (i % 2) if (i // 2) * 2 + 1 < n else 0


def quant_xn_counts(base, n):
    out = []
    for i in range(n):
        p, r = divmod(i, 2)
        if 2 * p + 1 < n:
            out.append(base + 10 * p + 7 + r)
        else:
            out.append(base + 10 * p + 4)
    return out


def quant_s_body(env, s, jobs, q, xn_counts):
    V = env.sems["V"]
    ab, l2 = q["ab"], q["l2"]
    n = len(jobs)
    for i, (srca, dsta, w) in enumerate(jobs):
        a0 = _q_slot(n, i) * w
        s.wait_ge(V, xn_counts[i])
        s.activation(l2[:, a0:a0 + w], ab[:, a0:a0 + w], AF.Sign)
        inst = s.activation(ab[:, a0:a0 + w], ab[:, a0:a0 + w], AF.Abs)
        env.inc(inst, "S")


def quant_v_body(env, v, jobs, q, s_counts):
    V, S_ = env.sems["V"], env.sems["S"]
    scl, rscl, ab, lv, l1, l2 = (q[k] for k in ("scl", "rscl", "ab", "lv", "l1", "l2"))
    n = len(jobs)
    rec = {}
    for g0 in range(0, n, 2):
        pair = list(range(g0, min(g0 + 2, n)))

        def sl(i):
            srca, dsta, w = jobs[i]
            return srca, dsta, w, w // 16, _q_slot(n, i)

        for i in pair:
            srca, dsta, w, nb, s_ = sl(i)
            inst = v.tensor_reduce(scl[:, s_ * nb:(s_ + 1) * nb],
                                   srca.rearrange("p (a b) -> p a b", b=16),
                                   AX.X, ALU.max, apply_absolute_value=True)
            env.inc(inst, "V"); rec[("r1", i)] = env.cnt["V"]
        for i in pair:
            srca, dsta, w, nb, s_ = sl(i)
            v.wait_ge(V, rec[("r1", i)])
            inst = v.tensor_scalar(scl[:, s_ * nb:(s_ + 1) * nb], scl[:, s_ * nb:(s_ + 1) * nb],
                                   1.0 / 6.0, 1e-12, ALU.mult, ALU.max)
            env.inc(inst, "V"); rec[("r2", i)] = env.cnt["V"]
        for i in pair:
            srca, dsta, w, nb, s_ = sl(i)
            v.wait_ge(V, rec[("r2", i)])
            inst = v.reciprocal(rscl[:, s_ * nb:(s_ + 1) * nb], scl[:, s_ * nb:(s_ + 1) * nb])
            env.inc(inst, "V"); rec[("r3", i)] = env.cnt["V"]
        for i in pair:
            srca, dsta, w, nb, s_ = sl(i)
            v.wait_ge(V, rec[("r3", i)])
            rb = rscl[:, s_ * nb:(s_ + 1) * nb]
            rbc = bass.AP(rb.tensor, rb.offset, [rb.ap[0], [1, nb], [0, 16]])
            a0 = s_ * w
            inst = v.tensor_tensor(ab[:, a0:a0 + w].rearrange("p (a b) -> p a b", b=16),
                                   srca.rearrange("p (a b) -> p a b", b=16), rbc, ALU.mult)
            env.inc(inst, "V")
        for i in pair:
            srca, dsta, w, nb, s_ = sl(i)
            a0 = s_ * w
            v.wait_ge(S_, s_counts[i])
            v.tensor_scalar(lv[:, a0:a0 + w], ab[:, a0:a0 + w], 0.25, None, ALU.is_gt)
            for t in (0.75, 1.25, 1.75):
                v.scalar_tensor_tensor(lv[:, a0:a0 + w], ab[:, a0:a0 + w], t, lv[:, a0:a0 + w],
                                       ALU.is_gt, ALU.add)
            v.tensor_scalar(l1[:, a0:a0 + w], ab[:, a0:a0 + w], 2.5, None, ALU.is_gt)
            v.scalar_tensor_tensor(l1[:, a0:a0 + w], ab[:, a0:a0 + w], 3.5, l1[:, a0:a0 + w],
                                   ALU.is_gt, ALU.add)
            v.scalar_tensor_tensor(lv[:, a0:a0 + w], lv[:, a0:a0 + w], 0.5, l1[:, a0:a0 + w],
                                   ALU.mult, ALU.add)
            v.tensor_scalar(l1[:, a0:a0 + w], ab[:, a0:a0 + w], 5.0, 2.0, ALU.is_gt, ALU.mult)
            v.tensor_tensor(lv[:, a0:a0 + w], lv[:, a0:a0 + w], l1[:, a0:a0 + w], ALU.add)
            v.tensor_tensor(lv[:, a0:a0 + w], lv[:, a0:a0 + w], l2[:, a0:a0 + w], ALU.mult)
            sb_ = scl[:, s_ * nb:(s_ + 1) * nb]
            sbc = bass.AP(sb_.tensor, sb_.offset, [sb_.ap[0], [1, nb], [0, 16]])
            inst = v.tensor_tensor(dsta.rearrange("p (a b) -> p a b", b=16),
                                   lv[:, a0:a0 + w].rearrange("p (a b) -> p a b", b=16), sbc, ALU.mult)
            env.inc(inst, "V")


def emit_quant_multi(env, jobs, q):
    bv = env.cnt["V"]
    bs = env.cnt["S"]
    xn_counts = quant_xn_counts(bv, len(jobs))
    s_counts = [bs + i + 1 for i in range(len(jobs))]

    def fs(s):
        quant_s_body(env, s, jobs, q, xn_counts)

    def fv(v):
        quant_v_body(env, v, jobs, q, s_counts)
    env.phase([("scalar", fs), ("vector", fv)])


def quant_chunks(env, src_ap_fn, dst_ap_fn, q, total, chunk=256):
    jobs = []
    off = 0
    while off < total:
        w = min(chunk, total - off)
        jobs.append((src_ap_fn(off, w), dst_ap_fn(off, w), w))
        off += w
    emit_quant_multi(env, jobs, q)


def emit_transpose_tiles(env, pairs, psum, ident, evac_dtype_note=None):
    ngrp = psum.shape[1] // 128 if hasattr(psum, "shape") else 4
    base_t = env.cnt["T"]
    base_v = env.cnt["V"]
    n = len(pairs)
    ngroups = (n + ngrp - 1) // ngrp

    def f_t(t):
        for g in range(ngroups):
            if g >= 1:
                t.wait_ge(env.sems["V"], base_v + g)
            lo, hi = g * ngrp, min((g + 1) * ngrp, n)
            for i in range(lo, hi):
                src, dstp = pairs[i]
                last = t.transpose(psum[:, (i - lo) * 128:(i - lo + 1) * 128], src, ident)
            env.inc(last, "T")

    def f_v(v):
        for g in range(ngroups):
            v.wait_ge(env.sems["T"], base_t + g + 1)
            lo, hi = g * ngrp, min((g + 1) * ngrp, n)
            for i in range(lo, hi):
                src, dstp = pairs[i]
                last = v.tensor_copy(dstp, psum[:, (i - lo) * 128:(i - lo + 1) * 128])
            env.inc(last, "V")
    env.phase([("tensor", f_t), ("vector", f_v)])


def emit_wprep_gu(env, cst, q, wdram, wdstT, wstage, wq_bf, pTb):
    bv, bs, bt, bld = env.cnt["V"], env.cnt["S"], env.cnt["T"], env.cnt["LD"]
    V, S_, T_, LD = env.sems["V"], env.sems["S"], env.sems["T"], env.sems["LD"]

    def jobs_for(wt):
        sl = (wt % 2) * 1024
        return [(wstage[:, sl + o: sl + o + 256], wq_bf[:, sl + o: sl + o + 256], 256)
                for o in range(0, 1024, 256)]

    def f_ld(sy):
        for wt in range(NHT):
            if wt >= 2:
                sy.wait_ge(V, bv + 21 * (wt - 2) + 20)
            last = sy.dma_start(out=wstage[:, (wt % 2) * 1024:(wt % 2) * 1024 + 1024],
                                in_=wdram[wt * 128:(wt + 1) * 128, :])
            env.inc(last, "LD", 16)

    def f_v(v):
        for wt in range(NHT):
            if wt >= 1:
                v.wait_ge(T_, bt + wt)
                k = wt - 1
                for kc in range(8):
                    last = v.tensor_copy(wdstT[:, kc * HID + k * 128:kc * HID + (k + 1) * 128],
                                         pTb[:, kc * 128:(kc + 1) * 128])
                env.inc(last, "V")
            v.wait_ge(LD, bld + 16 * (wt + 1))
            if wt >= 2:
                v.wait_ge(T_, bt + wt - 1)
            quant_v_body(env, v, jobs_for(wt), q,
                         [bs + 4 * wt + i + 1 for i in range(4)])
        v.wait_ge(T_, bt + NHT)
        k = NHT - 1
        for kc in range(8):
            last = v.tensor_copy(wdstT[:, kc * HID + k * 128:kc * HID + (k + 1) * 128],
                                 pTb[:, kc * 128:(kc + 1) * 128])
        env.inc(last, "V")

    def f_s(s):
        for wt in range(NHT):
            quant_s_body(env, s, jobs_for(wt), q,
                         quant_xn_counts(bv + 21 * wt, 4))

    def f_t(t):
        for wt in range(NHT):
            t.wait_ge(V, bv + 21 * wt + 20)
            sl = (wt % 2) * 1024
            for kc in range(8):
                last = t.transpose(pTb[:, kc * 128:(kc + 1) * 128],
                                   wq_bf[:, sl + kc * 128: sl + (kc + 1) * 128],
                                   cst["ident"][:])
            env.inc(last, "T")
    env.phase([("sync", f_ld), ("vector", f_v), ("scalar", f_s), ("tensor", f_t)])


def emit_wprep_wd(env, cst, q, wd_d, wdq_dram, wstage, wq_bf, wdk, pTb):
    bv, bs, bt, bld, bst = (env.cnt["V"], env.cnt["S"], env.cnt["T"],
                            env.cnt["LD"], env.cnt["ST"])
    V, S_, T_, LD, ST = (env.sems["V"], env.sems["S"], env.sems["T"],
                         env.sems["LD"], env.sems["ST"])
    NJ = HID // 256
    NG = (NHT + 7) // 8
    PTV = 5 * NJ + NG

    def jobs_for(wt):
        return [(wstage[:, o:o + 256], wq_bf[:, o:o + 256], 256)
                for o in range(0, HID, 256)]

    def stores_for(sy, wt):
        for g in range(NG):
            m = NG * wt + g
            sy.wait_ge(V, bv + PTV * wt + 5 * NJ + g + 1)
            lo, hi = g * 8, min(g * 8 + 8, NHT)
            dst = wdq_dram.rearrange("(a p) j -> p a j", p=128)[:, lo:hi,
                                                                wt * 128:(wt + 1) * 128]
            last = sy.dma_start(out=dst,
                                in_=wdk[m % 2][:, 0:(hi - lo) * 128].rearrange(
                                    "p (a j) -> p a j", j=128))
            env.inc(last, "ST", 16)

    def f_ld(sy):
        for wt in range(8):
            if wt >= 1:
                sy.wait_ge(V, bv + PTV * (wt - 1) + 5 * NJ)
            last = sy.dma_start(out=wstage[:], in_=wd_d[wt * 128:(wt + 1) * 128, :])
            env.inc(last, "LD", 16)
            if wt >= 1:
                stores_for(sy, wt - 1)
        stores_for(sy, 7)

    def f_v(v):
        for wt in range(8):
            v.wait_ge(LD, bld + 16 * (wt + 1))
            if wt >= 1:
                v.wait_ge(T_, bt + 3 * wt)
            quant_v_body(env, v, jobs_for(wt), q,
                         [bs + NJ * wt + i + 1 for i in range(NJ)])
            for g in range(NG):
                m = NG * wt + g
                v.wait_ge(T_, bt + m + 1)
                if m >= 2:
                    v.wait_ge(ST, bst + 16 * (m - 1))
                lo, hi = g * 8, min(g * 8 + 8, NHT)
                for kc in range(lo, hi):
                    last = v.tensor_copy(wdk[m % 2][:, (kc - lo) * 128:(kc - lo + 1) * 128],
                                         pTb[:, (kc - lo) * 128:(kc - lo + 1) * 128])
                env.inc(last, "V")

    def f_s(s):
        for wt in range(8):
            quant_s_body(env, s, jobs_for(wt), q,
                         quant_xn_counts(bv + PTV * wt, NJ))

    def f_t(t):
        for wt in range(8):
            for g in range(NG):
                m = NG * wt + g
                if g == 0:
                    t.wait_ge(V, bv + PTV * wt + 5 * NJ)
                else:
                    t.wait_ge(V, bv + PTV * wt + 5 * NJ + g)
                lo, hi = g * 8, min(g * 8 + 8, NHT)
                for kc in range(lo, hi):
                    last = t.transpose(pTb[:, (kc - lo) * 128:(kc - lo + 1) * 128],
                                       wq_bf[:, kc * 128:(kc + 1) * 128], cst["ident"][:])
                env.inc(last, "T")

    env.phase([("sync", f_ld), ("vector", f_v), ("scalar", f_s), ("tensor", f_t)])


def emit_fp4ffn(env, cst, xfull, wdq_dram, wg_d, wu_d, wd_d):
    nc = env.nc

    wgqT = env.sb("f_wgqT", [128, 8 * HID], BF16)
    wuqT = env.sb("f_wuqT", [128, 8 * HID], BF16)
    wdk = [env.sb("f_wdk0", [128, DM], BF16), env.sb("f_wdk1", [128, DM], BF16)]

    q = {"scl": env.sb("f_scl", [128, 64], F32),
         "rscl": env.sb("f_rscl", [128, 64], F32),
         "ab": env.sb("f_qab", [128, 512], F32),
         "lv": env.sb("f_qlv", [128, 512], F32),
         "l1": env.sb("f_ql1", [128, 512], F32),
         "l2": env.sb("f_ql2", [128, 512], F32)}

    xlnf = env.sb("f_xlnf", [128, NCT * FSEG], F32)
    scratch = {}
    for nm in ("mu", "msq", "var", "rstd", "lntmp"):
        scratch[nm] = env.sb("f_" + nm, [128, FSEG], F32)
    xtok = env.sb("f_xtok", [128, FNTC * DM], F32)
    dsb = xtok

    xqT = env.sb("f_xqT", [128, NCT * FSEG], BF16)
    hmt = env.sb("f_hmt", [128, 2 * FSEG], F32)
    hmtok = env.sb("f_hmtok", [128, HID], F32)
    wstage = hmtok
    hqtok = env.sb("f_hqtok", [128, HID], BF16)
    xqtok = hqtok
    wq_bf = hqtok
    hqT = env.sb("f_hqT", [128, NHT * FSEG], BF16)
    scratch["sqseg"] = hqT
    sgt = env.sb("f_sgt", [128, 2 * FSEG], F32)

    pg = env.ps("f_pg", [128, 512], F32)
    pu = env.ps("f_pu", [128, 512], F32)
    pTf = env.ps("f_pTf", [128, 512], F32)
    pTb = env.ps("f_pTb", [128, 1024], BF16)
    pd = env.ps("f_pd", [128, NCT * FSEG], F32)
    scratch["pS"] = pg[:, 0:FSEG]
    scratch["pQ"] = pu[:, 0:FSEG]

    for wdram, wdstT in ((wg_d, wgqT), (wu_d, wuqT)):
        emit_wprep_gu(env, cst, q, wdram, wdstT, wstage, wq_bf, pTb)
    emit_wprep_wd(env, cst, q, wd_d, wdq_dram, wstage, wq_bf, wdk, pTb)

    for s_ in range(L // FSEG):
        t0 = s_ * FSEG

        emit_ln(env, cst, xfull, xlnf, scratch, seg_w=FSEG, src_stride=L, src_off=t0)

        pairs = [(xlnf[:, ct * FSEG + tc * 128: ct * FSEG + (tc + 1) * 128],
                  xtok[:, tc * DM + ct * 128: tc * DM + (ct + 1) * 128])
                 for ct in range(NCT) for tc in range(FNTC)]
        emit_transpose_tiles(env, pairs, pTf[:, 0:512], cst["identf"][:])

        jobs = []
        for tc in range(FNTC):
            o = 0
            while o < DM:
                w = min(256, DM - o)
                jobs.append((xtok[:, tc * DM + o: tc * DM + o + w],
                             xqtok[:, tc * DM + o: tc * DM + o + w], w))
                o += w
        emit_quant_multi(env, jobs, q)

        pairs = [(xqtok[:, tc * DM + ct * 128: tc * DM + (ct + 1) * 128],
                  xqT[:, ct * FSEG + tc * 128: ct * FSEG + (tc + 1) * 128])
                 for tc in range(FNTC) for ct in range(NCT)]
        emit_transpose_tiles(env, pairs, pTb[:, 0:1024], cst["ident"][:])

        for tsub in range(FNTC):
            base_s = env.cnt["S"]
            base_v = env.cnt["V"]
            incs = {}

            def hid_t(t, tsub=tsub):
                def tr(k):
                    t.wait_ge(env.sems["V"], base_v + 2 * k + 1)
                    last = t.transpose(pTf[:, (k % 2) * 128:(k % 2) * 128 + 128],
                                       hmt[:, (k % 2) * 128:(k % 2) * 128 + 128],
                                       cst["identf"][:])
                    env.inc(last, "T")
                    incs[("tr", k)] = env.cnt["T"]
                for nt in range(NHT):
                    if nt >= 2:
                        t.wait_ge(env.sems["V"], base_v + 2 * nt - 3)
                        t.wait_ge(env.sems["S"], base_s + nt - 1)
                    pga = pg if nt % 2 == 0 else pd[:, 0:512]
                    pua = pu if nt % 2 == 0 else pd[:, 512:1024]
                    for kc in range(8):
                        t.matmul(pga[:, 0:FSEG],
                                 wgqT[:, kc * HID + nt * 128: kc * HID + (nt + 1) * 128],
                                 xqT[:, kc * FSEG:(kc + 1) * FSEG],
                                 start=(kc == 0), stop=(kc == 7))
                    for kc in range(8):
                        last = t.matmul(pua[:, 0:FSEG],
                                        wuqT[:, kc * HID + nt * 128: kc * HID + (nt + 1) * 128],
                                        xqT[:, kc * FSEG:(kc + 1) * FSEG],
                                        start=(kc == 0), stop=(kc == 7))
                    env.inc(last, "T")
                    incs[("mm", nt)] = env.cnt["T"]
                    if nt >= 1:
                        tr(nt - 1)
                tr(NHT - 1)

            def hid_s(s, tsub=tsub):
                for nt in range(NHT):
                    pga = pg if nt % 2 == 0 else pd[:, 0:512]
                    if nt >= 2:
                        s.wait_ge(env.sems["V"], base_v + 2 * nt - 3)
                    s.wait_ge(env.sems["T"], incs[("mm", nt)])
                    last = s.activation(sgt[:, (nt % 2) * FSEG:(nt % 2) * FSEG + FSEG],
                                        pga[:, 0:FSEG], AF.Silu)
                    env.inc(last, "S")

            def hid_v(v, tsub=tsub):
                for nt in range(NHT):
                    pua = pu if nt % 2 == 0 else pd[:, 512:1024]
                    v.wait_ge(env.sems["S"], base_s + nt + 1)
                    if nt >= 2:
                        v.wait_ge(env.sems["T"], incs[("tr", nt - 2)])
                    last = v.tensor_tensor(hmt[:, (nt % 2) * 128:(nt % 2) * 128 + 128],
                                           sgt[:, (nt % 2) * FSEG:(nt % 2) * FSEG + FSEG],
                                           pua[:, 0:FSEG], ALU.mult)
                    env.inc(last, "V")
                    v.wait_ge(env.sems["T"], incs[("tr", nt)])
                    last = v.tensor_copy(hmtok[:, nt * 128:(nt + 1) * 128],
                                         pTf[:, (nt % 2) * 128:(nt % 2) * 128 + 128])
                    env.inc(last, "V")
            env.phase([("tensor", hid_t), ("scalar", hid_s), ("vector", hid_v)])

            quant_chunks(env, lambda o, w: hmtok[:, o:o + w],
                         lambda o, w: hqtok[:, o:o + w], q, HID)

            pairs = [(hqtok[:, nt * 128:(nt + 1) * 128],
                      hqT[:, nt * FSEG + tsub * 128: nt * FSEG + tsub * 128 + 128])
                     for nt in range(NHT)]
            emit_transpose_tiles(env, pairs, pTb[:, 0:1024], cst["ident"][:])

        base_t = env.cnt["T"]
        base_ld = env.cnt["LD"]

        def down_ld(sy):
            for kc in range(NHT):
                if kc >= 2:
                    sy.wait_ge(env.sems["T"], base_t + kc - 1)
                src_ = wdq_dram[kc * 128:(kc + 1) * 128, :]
                last = sy.dma_start(out=wdk[kc % 2][:], in_=src_)
                env.inc(last, "LD", 16)

        def down_t(t):
            for kc in range(NHT):
                t.wait_ge(env.sems["LD"], base_ld + 16 * (kc + 1))
                for tsub in range(FNTC):
                    for hf in range(2):
                        last = t.matmul(pd[:, tsub * DM + hf * 512: tsub * DM + (hf + 1) * 512],
                                        hqT[:, kc * FSEG + tsub * 128: kc * FSEG + (tsub + 1) * 128],
                                        wdk[kc % 2][:, hf * 512:(hf + 1) * 512],
                                        start=(kc == 0), stop=(kc == NHT - 1))
                env.inc(last, "T")

        def down_ev(v):
            v.wait_ge(env.sems["T"], base_t + NHT)
            for tsub in range(FNTC):
                last = v.tensor_copy(dsb[:, tsub * DM:(tsub + 1) * DM],
                                     pd[:, tsub * DM:(tsub + 1) * DM])
            env.inc(last, "V")
        env.phase([("sync", down_ld), ("tensor", down_t), ("vector", down_ev)])

        base_t = env.cnt["T"]
        base_v = env.cnt["V"]

        def dtr_t(t):
            n = 0
            for tsub in range(FNTC):
                for ct in range(NCT):
                    if n % 4 == 0 and n >= 4:
                        t.wait_ge(env.sems["V"], base_v + n // 4)
                    last = t.transpose(pTf[:, (n % 4) * 128:(n % 4 + 1) * 128],
                                       dsb[:, tsub * DM + ct * 128: tsub * DM + (ct + 1) * 128],
                                       cst["identf"][:])
                    n += 1
                    if n % 4 == 0:
                        env.inc(last, "T")

        def dtr_v(v, t0=t0):
            n = 0
            for tsub in range(FNTC):
                for ct in range(NCT):
                    if n % 4 == 0:
                        v.wait_ge(env.sems["T"], base_t + n // 4 + 1)
                    sl = slice(ct * L + t0 + tsub * 128, ct * L + t0 + (tsub + 1) * 128)
                    last = v.tensor_tensor(xfull[:, sl], pTf[:, (n % 4) * 128:(n % 4 + 1) * 128],
                                           xfull[:, sl], ALU.add)
                    n += 1
                    if n % 4 == 0:
                        env.inc(last, "V")
        env.phase([("tensor", dtr_t), ("vector", dtr_v)])


def build_fp4():
    nc = bass.Bass("TRN2", target_bir_lowering=False, detect_race_conditions=False)
    xin = nc.dram_tensor("xT", [DM, L], F32, kind="ExternalInput").ap()
    wg_d = nc.dram_tensor("w_gate", [HID, DM], F32, kind="ExternalInput").ap()
    wu_d = nc.dram_tensor("w_up", [HID, DM], F32, kind="ExternalInput").ap()
    wd_d = nc.dram_tensor("w_down", [DM, HID], F32, kind="ExternalInput").ap()
    FP4_CONSTS = [c for c in CONST_SPECS if c[0] in ("ones128", "ones128f", "ident", "identf")]
    cst_d = {nm: nc.dram_tensor(nm, shp, dt, kind="ExternalInput").ap()
             for nm, shp, dt in FP4_CONSTS}
    wdq_dram = nc.dram_tensor("f_wdq_scratch", [NHT * 128, DM], BF16, kind="Internal").ap()
    xout = nc.dram_tensor("xoutT", [DM, L], F32, kind="ExternalOutput").ap()

    with Env(nc) as env:
        cst = {nm: env.sb(nm + "_sb", shp, dt) for nm, shp, dt in FP4_CONSTS}
        for nm, val in (("epsc", EPS), ("neghc", -0.5)):
            cst[nm] = env.sb(nm, [128, 1], F32)
            cst[nm + "_val"] = val
        xfull = env.sb("f_xfull", [128, NCT * L], F32)

        def p0_v(v):
            for nm in ("epsc", "neghc"):
                last = v.memset(cst[nm][:], cst[nm + "_val"])
            env.inc(last, "V")

        def p0_sy(sy):
            for nm, shp, dt in FP4_CONSTS:
                sy.dma_start(out=cst[nm][:], in_=cst_d[nm]).then_inc(env.sems["LD"], 16)
                env.cnt["LD"] += 16
            last = sy.dma_start(out=xfull[:].rearrange("p (a t) -> p a t", t=L),
                                in_=xin.rearrange("(a p) t -> p a t", p=128))
            env.inc(last, "LD", 16)
        env.phase([("sync", p0_sy), ("vector", p0_v)])
        with env.scope():
            emit_fp4ffn(env, cst, xfull, wdq_dram, wg_d, wu_d, wd_d)

        def pf_sy(sy):
            last = sy.dma_start(out=xout.rearrange("(a p) t -> p a t", p=128),
                                in_=xfull[:].rearrange("p (a t) -> p a t", t=L))
            env.inc(last, "ST", 16)
        env.phase([("sync", pf_sy)])

        @env.block.sync
        def _(sy):
            sy.wait_ge(env.sems["ST"], env.cnt["ST"])
    return nc


def run_fp4(x_full, w_gate, w_up, w_down):
    consts = host_consts()
    nc = build_fp4()
    in_maps = []
    B = x_full.shape[0]
    for b in range(B):
        m = {"xT": np.ascontiguousarray(x_full[b].T),
             "w_gate": w_gate, "w_up": w_up, "w_down": w_down}
        m.update(consts)
        in_maps.append(m)
    res = run_bass_kernel_spmd(nc, in_maps, core_ids=list(range(B)))
    return np.stack([np.ascontiguousarray(res.results[b]["xoutT"]).T for b in range(B)])





NEXP = 8
MSEG = 512
MNTC = MSEG // 128
PSEG = 128


def emit_moe(env, cst, xfull, xln_dram, w2q_dram, w3q_dram, gate_wT_d, w1_d, w3_d, w2_d):
    nc = env.nc

    with env.scope():
        w2stage = env.sb("m_w2stage", [128, NHT * DM], BF16)
        for e in range(NEXP):
            def cst_gp(gp, e=e):
                r = w2_d[e].rearrange("(a p) j -> p a j", p=128)
                last = gp.dma_start(out=w2stage[:].rearrange("p (a j) -> p a j", j=DM), in_=r)
                env.inc(last, "LG", 16)
            env.phase([("gpsimd", cst_gp)])

            def cst_st(sy, e=e):
                for jp in range(2):
                    dst = w2q_dram[(e * 2 + jp) * NHT * 128:(e * 2 + jp + 1) * NHT * 128,
                                   :].rearrange("(nt p) j -> p nt j", p=128)
                    srcs = w2stage[:].rearrange("p (nt j) -> p nt j", j=DM)[
                        :, :, jp * 512:(jp + 1) * 512]
                    last = sy.dma_start(out=dst, in_=srcs)
                    env.inc(last, "ST", 16)
            env.phase([("sync", cst_st)])
            def cst3_gp(gp, e=e):
                for a in range(NCT):
                    srca = w3_d[e, a * 128:(a + 1) * 128, :].rearrange(
                        "p (nt h) -> p nt h", h=128)
                    dsta = w2stage[:, 0:NHT * 1024].rearrange(
                        "p (nt c) -> p nt c", c=1024)[:, :, a * 128:(a + 1) * 128]
                    last = gp.dma_start(out=dsta, in_=srca)
                    env.inc(last, "LG", 16)
            env.phase([("gpsimd", cst3_gp)])

            def cst3_st(sy, e=e):
                dst = w3q_dram[e * NHT * 128:(e + 1) * NHT * 128, :].rearrange(
                    "(nt p) c -> p nt c", p=128)
                last = sy.dma_start(out=dst, in_=w2stage[:, 0:NHT * 1024].rearrange(
                    "p (nt c) -> p nt c", c=1024))
                env.inc(last, "ST", 16)
            env.phase([("sync", cst3_st)])

    w1buf = env.sb("m_w1", [128, NCT * HID], BF16)
    w3k = [env.sb("m_w3k0", [128, NCT * 128], BF16), env.sb("m_w3k1", [128, NCT * 128], BF16)]
    wdk = [env.sb("m_wdk0", [128, 512], BF16), env.sb("m_wdk1", [128, 512], BF16)]
    xlnf = env.sb("m_xlnf", [128, NCT * PSEG], F32)
    xlnbf = env.sb("m_xlnbf", [128, NCT * PSEG], BF16)
    scratch = {}
    for nm in ("mu", "msq", "var", "rstd", "lntmp"):
        scratch[nm] = env.sb("m_" + nm, [128, PSEG], F32)
    gwT = env.sb("m_gwT", [128, NCT * NEXP], F32)
    lgt = env.sb("m_lgt", [128, NEXP], F32)
    rmax = env.sb("m_rmax", [128, 1], F32)
    rnegm = env.sb("m_rnegm", [128, 1], F32)
    rex = env.sb("m_rex", [128, NEXP], F32)
    rsum = env.sb("m_rsum", [128, 1], F32)
    rrs = env.sb("m_rrs", [128, 1], F32)
    req1 = env.sb("m_req1", [128, NEXP], F32)
    rtmp = env.sb("m_rtmp", [128, NEXP], F32)
    rm2 = env.sb("m_rm2", [128, 1], F32)
    re2x = env.sb("m_re2x", [128, 1], F32)
    rs2 = env.sb("m_rs2", [128, 1], F32)
    req2 = env.sb("m_req2", [128, NEXP], F32)
    rt1 = env.sb("m_rt1", [128, NEXP], F32)
    mrowtok = env.sb("m_mrowtok", [128, (L // 128) * NEXP], F32)
    mtr = env.sb("m_mtr", [8, L], BF16)
    mbc = env.sb("m_mbc", [128, MSEG], BF16)
    xeseg = env.sb("m_xeseg", [128, NCT * MSEG], BF16)
    scratch["sqseg"] = env.sb("m_sqf", [128, NCT * PSEG], F32)
    hbuf = env.sb("m_hbuf", [128, NHT * MSEG], BF16)
    sgt = env.sb("m_sgt", [128, 2 * MSEG], F32)
    dsbb = env.sb("m_dsbb", [128, MNTC * 512], BF16)

    pg = env.ps("m_pg", [128, 512], F32)
    pu = env.ps("m_pu", [128, 512], F32)
    pd = env.ps("m_pd", [128, MNTC * 512], F32)
    pT = env.ps("m_pT", [128, 1024], BF16)
    scratch["pS"] = pg[:, 0:PSEG]
    scratch["pQ"] = pu[:, 0:PSEG]

    def p_gw(sy):
        last = sy.dma_start(out=gwT[:].rearrange("p (a j) -> p a j", j=NEXP),
                            in_=gate_wT_d.rearrange("(a p) j -> p a j", p=128))
        env.inc(last, "LD", 16)
    env.phase([("sync", p_gw)])

    for ps in range(L // PSEG):
        t0 = ps * PSEG
        emit_ln(env, cst, xfull, xlnf, scratch, seg_w=PSEG, src_stride=L, src_off=t0)

        def p_cp(v, ps=ps):
            for ct in range(NCT):
                last = v.tensor_copy(xlnbf[:, ct * PSEG:(ct + 1) * PSEG],
                                     xlnf[:, ct * PSEG:(ct + 1) * PSEG])
            env.inc(last, "V")
        env.phase([("vector", p_cp)])

        def p_st(sy, t0=t0):
            dst = xln_dram.rearrange("(a p) t -> p a t", p=128)[:, :, t0:t0 + PSEG]
            last = sy.dma_start(out=dst, in_=xlnbf[:].rearrange("p (a t) -> p a t", t=PSEG))
            env.inc(last, "ST", 16)
        env.phase([("sync", p_st)])

        def p_rt(t, ps=ps):
            for ct in range(NCT):
                last = t.matmul(pd[:, 0:NEXP], xlnf[:, ct * PSEG:(ct + 1) * PSEG],
                                gwT[:, ct * NEXP:(ct + 1) * NEXP],
                                start=(ct == 0), stop=(ct == NCT - 1))
            env.inc(last, "T")
        env.phase([("tensor", p_rt)])

        s_base = env.cnt["S"]
        v_base = env.cnt["V"]

        def p_rm(v, ps=ps, s_base=s_base):
            V = env.sems["V"]

            def stp(i):
                env.inc(i, "V"); v.wait_ge(V, env.cnt["V"])
            stp(v.tensor_copy(lgt[:], pd[:, 0:NEXP]))
            stp(v.tensor_reduce(rmax[:], lgt[:], AX.X, ALU.max))
            stp(v.tensor_scalar(rnegm[:], rmax[:], -1.0, None, ALU.mult))
            v.wait_ge(env.sems["S"], s_base + 1)
            stp(v.tensor_reduce(rsum[:], rex[:], AX.X, ALU.add))
            stp(v.reciprocal(rrs[:], rsum[:]))
            mb = bass.AP(rmax[:].tensor, rmax[:].offset, [rmax[:].ap[0], [0, NEXP]])
            stp(v.tensor_tensor(req1[:], lgt[:], mb, ALU.is_equal))
            stp(v.scalar_tensor_tensor(rtmp[:], req1[:], -1e30, lgt[:], ALU.mult, ALU.add))
            stp(v.tensor_reduce(rm2[:], rtmp[:], AX.X, ALU.max))
            v.wait_ge(env.sems["S"], s_base + 2)
            stp(v.tensor_tensor(rs2[:], re2x[:], rrs[:], ALU.mult))
            m2b = bass.AP(rm2[:].tensor, rm2[:].offset, [rm2[:].ap[0], [0, NEXP]])
            stp(v.tensor_tensor(req2[:], rtmp[:], m2b, ALU.is_equal))
            stp(v.tensor_scalar(rt1[:], req1[:], rrs[:], None, ALU.mult))
            stp(v.scalar_tensor_tensor(mrowtok[:, ps * NEXP:(ps + 1) * NEXP],
                                       req2[:], rs2[:], rt1[:], ALU.mult, ALU.add))

        def p_rs(s, ps=ps, v_base=v_base):
            s.wait_ge(env.sems["V"], v_base + 3)
            last = s.activation(rex[:], lgt[:], AF.Exp, bias=rnegm[:])
            env.inc(last, "S")
            s.wait_ge(env.sems["V"], v_base + 8)
            last = s.activation(re2x[:], rm2[:], AF.Exp, bias=rnegm[:])
            env.inc(last, "S")
        env.phase([("vector", p_rm), ("scalar", p_rs)])

    base_t = env.cnt["T"]
    base_v = env.cnt["V"]

    def p_mt(t):
        for g in range(4):
            if g >= 1:
                t.wait_ge(env.sems["V"], base_v + g)
            for k in range(4):
                ps = g * 4 + k
                last = t.transpose(pd[0:8, k * 128:(k + 1) * 128]
                                   if False else pd[:, k * 128:(k + 1) * 128][0:8, :],
                                   mrowtok[:, ps * NEXP:(ps + 1) * NEXP], cst["identf"][:])
            env.inc(last, "T")

    def p_mv(v):
        for g in range(4):
            v.wait_ge(env.sems["T"], base_t + g + 1)
            for k in range(4):
                ps = g * 4 + k
                last = v.tensor_copy(mtr[:, ps * 128:(ps + 1) * 128],
                                     pd[:, k * 128:(k + 1) * 128][0:8, :])
            env.inc(last, "V")
    env.phase([("tensor", p_mt), ("vector", p_mv)])

    for e in range(NEXP):
        def p_wl(gp, e=e):
            r = w1_d[e].rearrange("(a p) j -> p a j", p=128)
            last = gp.dma_start(out=w1buf[:].rearrange("p (a j) -> p a j", j=HID), in_=r)
            env.inc(last, "LG", 16)
        env.phase([("gpsimd", p_wl)])

        for sg in range(L // MSEG):
            t0 = sg * MSEG

            base_t = env.cnt["T"]
            base_ld0 = env.cnt["LD"]

            def p_lx(sy, t0=t0):
                srcx = xln_dram.rearrange("(a p) t -> p a t", p=128)[:, :, t0:t0 + MSEG]
                last = sy.dma_start(out=xeseg[:].rearrange("p (a t) -> p a t", t=MSEG), in_=srcx)
                env.inc(last, "LD", 16)

            def p_bc(t, e=e, t0=t0):
                last = t.matmul(pg[:, 0:MSEG], cst["esel8"][:, e * 128:(e + 1) * 128],
                                mtr[0:8, t0:t0 + MSEG], start=True, stop=True)
                env.inc(last, "T")

            def p_bv(v, e=e, base_ld0=base_ld0):
                v.wait_ge(env.sems["T"], base_t + 1)
                inst = v.tensor_copy(mbc[:], pg[:, 0:MSEG])
                env.inc(inst, "V")
                v.wait_ge(env.sems["V"], env.cnt["V"])
                v.wait_ge(env.sems["LD"], base_ld0 + 16)
                for ct in range(NCT):
                    sl = slice(ct * MSEG, (ct + 1) * MSEG)
                    last = v.tensor_tensor(xeseg[:, sl], mbc[:], xeseg[:, sl], ALU.mult)
                env.inc(last, "V")
            env.phase([("sync", p_lx), ("tensor", p_bc), ("vector", p_bv)])

            base_t = env.cnt["T"]
            base_s = env.cnt["S"]
            base_v = env.cnt["V"]
            base_lg = env.cnt["LG"]

            def hid_ld(gp, e=e):
                for nt in range(NHT):
                    if nt >= 2:
                        gp.wait_ge(env.sems["T"], base_t + nt - 1)
                    src_ = w3q_dram[(e * NHT + nt) * 128:(e * NHT + nt + 1) * 128, :]
                    last = gp.dma_start(out=w3k[nt % 2][:], in_=src_)
                    env.inc(last, "LG", 16)

            def hid_t(t):
                for nt in range(NHT):
                    pga = pg if nt % 2 == 0 else pd[:, 0:512]
                    pua = pu if nt % 2 == 0 else pd[:, 512:1024]
                    if nt >= 2:
                        t.wait_ge(env.sems["V"], base_v + nt - 1)
                    t.wait_ge(env.sems["LG"], base_lg + 16 * (nt + 1))
                    for kc in range(NCT):
                        t.matmul(pga[:, 0:512],
                                 w1buf[:, kc * HID + nt * 128: kc * HID + (nt + 1) * 128],
                                 xeseg[:, kc * MSEG:(kc + 1) * MSEG],
                                 start=(kc == 0), stop=(kc == NCT - 1))
                    for kc in range(NCT):
                        last = t.matmul(pua[:, 0:512],
                                        w3k[nt % 2][:, kc * 128:(kc + 1) * 128],
                                        xeseg[:, kc * MSEG:(kc + 1) * MSEG],
                                        start=(kc == 0), stop=(kc == NCT - 1))
                    env.inc(last, "T")

            def hid_s(s):
                for nt in range(NHT):
                    pga = pg if nt % 2 == 0 else pd[:, 0:512]
                    if nt >= 2:
                        s.wait_ge(env.sems["V"], base_v + nt - 1)
                    s.wait_ge(env.sems["T"], base_t + nt + 1)
                    last = s.activation(sgt[:, (nt % 2) * MSEG:(nt % 2) * MSEG + MSEG],
                                        pga[:, 0:MSEG], AF.Silu)
                    env.inc(last, "S")

            def hid_v(v):
                for nt in range(NHT):
                    pua = pu if nt % 2 == 0 else pd[:, 512:1024]
                    v.wait_ge(env.sems["S"], base_s + nt + 1)
                    last = v.tensor_tensor(hbuf[:, nt * MSEG:(nt + 1) * MSEG],
                                           sgt[:, (nt % 2) * MSEG:(nt % 2) * MSEG + MSEG],
                                           pua[:, 0:MSEG], ALU.mult)
                    env.inc(last, "V")
            env.phase([("gpsimd", hid_ld), ("tensor", hid_t), ("scalar", hid_s), ("vector", hid_v)])

            base_t = env.cnt["T"]
            base_v = env.cnt["V"]
            base_ld = env.cnt["LD"]

            def down_ld(sy, e=e):
                for jp in range(2):
                    bt = base_t + jp * (NHT + 2)
                    for nt in range(NHT):
                        if jp == 1 and nt == 0:
                            sy.wait_ge(env.sems["T"], base_t + NHT)
                        if nt >= 2:
                            sy.wait_ge(env.sems["T"], bt + nt - 1)
                        src_ = w2q_dram[((e * 2 + jp) * NHT + nt) * 128:
                                        ((e * 2 + jp) * NHT + nt + 1) * 128, :]
                        last = sy.dma_start(out=wdk[nt % 2][:], in_=src_)
                        env.inc(last, "LD", 16)

            def down_t(t, t0=t0):
                for jp in range(2):
                    bt = base_t + jp * (NHT + 2)
                    bl = base_ld + jp * 16 * NHT
                    if jp == 1:
                        t.wait_ge(env.sems["V"], base_v + 1)
                    for nt in range(NHT):
                        t.wait_ge(env.sems["LD"], bl + 16 * (nt + 1))
                        for tsub in range(MNTC):
                            last = t.matmul(
                                pd[:, tsub * 512:(tsub + 1) * 512],
                                hbuf[:, nt * MSEG + tsub * 128: nt * MSEG + (tsub + 1) * 128],
                                wdk[nt % 2][:],
                                start=(nt == 0), stop=(nt == NHT - 1))
                        env.inc(last, "T")
                    n = 0
                    for tsub in range(MNTC):
                        for ct in range(4):
                            if n % 8 == 0:
                                t.wait_ge(env.sems["V"], base_v + jp * 3 + 1 + (1 if n >= 8 else 0))
                            last = t.transpose(pT[:, (n % 8) * 128:(n % 8 + 1) * 128],
                                               dsbb[:, tsub * 512 + ct * 128: tsub * 512 + (ct + 1) * 128],
                                               cst["ident"][:])
                            n += 1
                            if n % 8 == 0:
                                env.inc(last, "T")

            def down_v(v, t0=t0):
                for jp in range(2):
                    bt = base_t + jp * (NHT + 2)
                    v.wait_ge(env.sems["T"], bt + NHT)
                    inst = v.tensor_copy(dsbb[:], pd[:, 0:MNTC * 512])
                    env.inc(inst, "V")
                    n = 0
                    for tsub in range(MNTC):
                        for ct in range(4):
                            if n % 8 == 0:
                                v.wait_ge(env.sems["T"], bt + NHT + n // 8 + 1)
                            ctg = jp * 4 + ct
                            sl = slice(ctg * L + t0 + tsub * 128, ctg * L + t0 + (tsub + 1) * 128)
                            last = v.tensor_tensor(xfull[:, sl], pT[:, (n % 8) * 128:(n % 8 + 1) * 128],
                                                   xfull[:, sl], ALU.add)
                            n += 1
                            if n % 8 == 0:
                                env.inc(last, "V")
            env.phase([("sync", down_ld), ("tensor", down_t), ("vector", down_v)])

def build_moe():
    nc = bass.Bass("TRN2", target_bir_lowering=False, detect_race_conditions=False)
    xin = nc.dram_tensor("xT", [DM, L], F32, kind="ExternalInput").ap()
    gate_wT_d = nc.dram_tensor("gate_wT", [DM, NEXP], F32, kind="ExternalInput").ap()
    w1_d = nc.dram_tensor("w1", [NEXP, DM, HID], F32, kind="ExternalInput").ap()
    w3_d = nc.dram_tensor("w3", [NEXP, DM, HID], F32, kind="ExternalInput").ap()
    w2_d = nc.dram_tensor("w2", [NEXP, HID, DM], F32, kind="ExternalInput").ap()
    MOE_CONSTS = [c for c in CONST_SPECS if c[0] in ("ones128", "ones128f", "ident", "identf", "esel8")]
    cst_d = {nm: nc.dram_tensor(nm, shp, dt, kind="ExternalInput").ap()
             for nm, shp, dt in MOE_CONSTS}
    xln_dram = nc.dram_tensor("m_xln_scratch", [DM, L], BF16, kind="Internal").ap()
    w2q_dram = nc.dram_tensor("m_w2q_scratch", [NEXP * 2 * NHT * 128, 512], BF16, kind="Internal").ap()
    w3q_dram = nc.dram_tensor("m_w3q_scratch", [NEXP * NHT * 128, NCT * 128], BF16, kind="Internal").ap()
    xout = nc.dram_tensor("xoutT", [DM, L], F32, kind="ExternalOutput").ap()

    with Env(nc) as env:
        cst = {nm: env.sb(nm + "_sb", shp, dt) for nm, shp, dt in MOE_CONSTS}
        for nm, val in (("epsc", EPS), ("neghc", -0.5)):
            cst[nm] = env.sb(nm, [128, 1], F32)
            cst[nm + "_val"] = val

        def p0_v(v):
            for nm in ("epsc", "neghc"):
                last = v.memset(cst[nm][:], cst[nm + "_val"])
            env.inc(last, "V")

        xfull = env.sb("m_xfull", [128, NCT * L], F32)

        def p0_sy(sy):
            for nm, shp, dt in MOE_CONSTS:
                sy.dma_start(out=cst[nm][:], in_=cst_d[nm]).then_inc(env.sems["LD"], 16)
                env.cnt["LD"] += 16
            last = sy.dma_start(out=xfull[:].rearrange("p (a t) -> p a t", t=L),
                                in_=xin.rearrange("(a p) t -> p a t", p=128))
            env.inc(last, "LD", 16)
        env.phase([("sync", p0_sy), ("vector", p0_v)])
        with env.scope():
            emit_moe(env, cst, xfull, xln_dram, w2q_dram, w3q_dram, gate_wT_d, w1_d, w3_d, w2_d)

        def pf_sy(sy):
            last = sy.dma_start(out=xout.rearrange("(a p) t -> p a t", p=128),
                                in_=xfull[:].rearrange("p (a t) -> p a t", t=L))
            env.inc(last, "ST", 16)
        env.phase([("sync", pf_sy)])

        @env.block.sync
        def _(sy):
            sy.wait_ge(env.sems["ST"], env.cnt["ST"])
    return nc


def run_moe(x_full, gate_w, w1, w2, w3):
    consts = host_consts()
    nc = build_moe()
    in_maps = []
    B = x_full.shape[0]
    for b in range(B):
        m = {"xT": np.ascontiguousarray(x_full[b].T),
             "gate_wT": np.ascontiguousarray(gate_w.T),
             "w1": w1, "w3": w3, "w2": w2}
        m.update({k: consts[k] for k in ("ones128", "ones128f", "ident", "identf", "esel8")})
        in_maps.append(m)
    res = run_bass_kernel_spmd(nc, in_maps, core_ids=list(range(B)))
    return np.stack([np.ascontiguousarray(res.results[b]["xoutT"]).T for b in range(B)])




def build_fused():
    nc = bass.Bass("TRN2", target_bir_lowering=False, detect_race_conditions=False)
    xin = nc.dram_tensor("xT", [DM, L], F32, kind="ExternalInput").ap()
    gdr = {}
    for nm, shp in (("wq", [DM, INNER]), ("wk", [DM, INNER]), ("wv", [DM, INNER]),
                    ("wg", [DM, INNER]), ("wo", [INNER, DM])):
        gdr[nm] = nc.dram_tensor(nm, shp, F32, kind="ExternalInput").ap()
    bg = nc.dram_tensor("bg", [1, INNER], F32, kind="ExternalInput").ap()
    gate_wT_d = nc.dram_tensor("gate_wT", [DM, NEXP], F32, kind="ExternalInput").ap()
    w1_d = nc.dram_tensor("w1", [NEXP, DM, HID], F32, kind="ExternalInput").ap()
    w3_d = nc.dram_tensor("w3", [NEXP, DM, HID], F32, kind="ExternalInput").ap()
    w2_d = nc.dram_tensor("w2", [NEXP, HID, DM], F32, kind="ExternalInput").ap()
    w_in_d = nc.dram_tensor("w_in", [DM, DI], F32, kind="ExternalInput").ap()
    w_gate_d = nc.dram_tensor("w_gate", [DM, DI], F32, kind="ExternalInput").ap()
    w_out_d = nc.dram_tensor("w_out", [DI, DM], F32, kind="ExternalInput").ap()
    b_in_d = nc.dram_tensor("b_in", [128, NTI], F32, kind="ExternalInput").ap()
    b_gate_d = nc.dram_tensor("b_gate", [128, NTI], F32, kind="ExternalInput").ap()
    b_out_d = nc.dram_tensor("b_out", [128, NCT], F32, kind="ExternalInput").ap()
    A_d = nc.dram_tensor("A", [DI, 256], F32, kind="ExternalInput").ap()
    B_d = nc.dram_tensor("B_ssm", [DI, DS], F32, kind="ExternalInput").ap()
    C_d = nc.dram_tensor("C_ssm", [DI, DS], F32, kind="ExternalInput").ap()
    fwg_d = nc.dram_tensor("f_w_gate", [HID, DM], F32, kind="ExternalInput").ap()
    fwu_d = nc.dram_tensor("f_w_up", [HID, DM], F32, kind="ExternalInput").ap()
    fwd_d = nc.dram_tensor("f_w_down", [DM, HID], F32, kind="ExternalInput").ap()
    cst_d = {nm: nc.dram_tensor(nm, shp, dt, kind="ExternalInput").ap()
             for nm, shp, dt in CONST_SPECS}
    xln_dram = nc.dram_tensor("m_xln_scratch", [DM, L], BF16, kind="Internal").ap()
    w2q_dram = nc.dram_tensor("m_w2q_scratch", [NEXP * 2 * NHT * 128, 512], BF16, kind="Internal").ap()
    w3q_dram = nc.dram_tensor("m_w3q_scratch", [NEXP * NHT * 128, NCT * 128], BF16, kind="Internal").ap()
    wdq_dram = nc.dram_tensor("f_wdq_scratch", [NHT * 128, DM], BF16, kind="Internal").ap()
    xout = nc.dram_tensor("xoutT", [DM, L], F32, kind="ExternalOutput").ap()

    with Env(nc) as env:
        cst = {nm: env.sb(nm + "_sb", shp, dt) for nm, shp, dt in CONST_SPECS}
        for nm, val in (("epsc", EPS), ("onec", 1.0), ("neg1c", -1.0), ("neghc", -0.5)):
            cst[nm] = env.sb(nm, [128, 1], F32)
            cst[nm + "_val"] = val
        xfull = env.sb("xfull", [128, NCT * L], F32)

        def p0_v(v):
            for nm in ("epsc", "onec", "neg1c", "neghc"):
                last = v.memset(cst[nm][:], cst[nm + "_val"])
            env.inc(last, "V")

        def p0_sy(sy):
            for nm, shp, dt in CONST_SPECS:
                sy.dma_start(out=cst[nm][:], in_=cst_d[nm]).then_inc(env.sems["LD"], 16)
                env.cnt["LD"] += 16
            last = sy.dma_start(out=xfull[:].rearrange("p (a t) -> p a t", t=L),
                                in_=xin.rearrange("(a p) t -> p a t", p=128))
            env.inc(last, "LD", 16)
        env.phase([("sync", p0_sy), ("vector", p0_v)])

        with env.scope():
            emit_gla(env, cst, xfull, gdr, bg)
        with env.scope():
            emit_moe(env, cst, xfull, xln_dram, w2q_dram, w3q_dram, gate_wT_d, w1_d, w3_d, w2_d)
        with env.scope():
            emit_ssm(env, cst, xfull, w_in_d, w_gate_d, w_out_d,
                     b_in_d, b_gate_d, b_out_d, A_d, B_d, C_d)
        with env.scope():
            emit_fp4ffn(env, cst, xfull, wdq_dram, fwg_d, fwu_d, fwd_d)

        def pf_sy(sy):
            last = sy.dma_start(out=xout.rearrange("(a p) t -> p a t", p=128),
                                in_=xfull[:].rearrange("p (a t) -> p a t", t=L))
            env.inc(last, "ST", 16)
        env.phase([("sync", pf_sy)])

        @env.block.sync
        def _(sy):
            sy.wait_ge(env.sems["ST"], env.cnt["ST"])
    return nc


def kernel(**inputs):
    ins = {k: np.asarray(v) for k, v in inputs.items()}
    x = np.asarray(ins["L0_attn_x"], np.float32)
    B = x.shape[0]
    cores = list(range(B))
    consts = host_consts()
    nc = build_fused()
    im = []
    for b in range(B):
        m = {"xT": np.ascontiguousarray(x[b].T),
             "wq": ins["L0_attn_Wq"], "wk": ins["L0_attn_Wk"],
             "wv": ins["L0_attn_Wv"], "wg": ins["L0_attn_Wg"],
             "bg": np.ascontiguousarray(np.asarray(ins["L0_attn_bg"]).reshape(1, -1)),
             "wo": permute_wo_rows(np.asarray(ins["L0_attn_Wo"])),
             "gate_wT": np.ascontiguousarray(np.asarray(ins["L0_ffn_gate_w"]).T),
             "w1": ins["L0_ffn_w1"], "w3": ins["L0_ffn_w3"], "w2": ins["L0_ffn_w2"],
             "w_in": ins["L1_attn_w_in"], "w_gate": ins["L1_attn_w_gate"],
             "w_out": ins["L1_attn_w_out"],
             "b_in": np.ascontiguousarray(np.asarray(ins["L1_attn_b_in"]).reshape(NTI, 128).T),
             "b_gate": np.ascontiguousarray(np.asarray(ins["L1_attn_b_gate"]).reshape(NTI, 128).T),
             "b_out": np.ascontiguousarray(np.asarray(ins["L1_attn_b_out"]).reshape(NCT, 128).T),
             "A": np.ascontiguousarray(np.asarray(ins["L1_attn_A"]).transpose(0, 2, 1).reshape(DI, 256)),
             "B_ssm": ins["L1_attn_B_ssm"], "C_ssm": ins["L1_attn_C_ssm"],
             "f_w_gate": ins["L1_ffn_w_gate"], "f_w_up": ins["L1_ffn_w_up"],
             "f_w_down": ins["L1_ffn_w_down"]}
        m.update({k: consts[k] for k, _s, _d in CONST_SPECS})
        im.append(m)
    res = run_bass_kernel_spmd(nc, im, core_ids=cores)
    out = np.stack([np.ascontiguousarray(res.results[b]["xoutT"]).T for b in range(B)])
    return out.astype(np.float32)


# version 07
# speedup vs baseline: 1.1689x; 1.0033x over previous
import numpy as np
from contextlib import ExitStack, contextmanager
import concourse.bass as bass
import concourse.mybir as mybir
from concourse.bass_utils import run_bass_kernel_spmd
from concourse.alu_op_type import AluOpType as ALU

F32 = mybir.dt.float32
BF16 = mybir.dt.bfloat16
U32 = mybir.dt.uint32
AF = mybir.ActivationFunctionType
AX = mybir.AxisListType

L = 2048
DM = 1024
NCT = DM // 128
SEG = 256
NSEG = L // SEG
GNTT = SEG // 128
C = 32
H, D = 16, 64
INNER = 1024
EPS = 1e-5
STOP_AFTER = None
SSM_DEBUG = False
P4_MODE = "full"

ENG_SEM = {"tensor": "T", "vector": "V", "scalar": "S", "gpsimd": "G", "sync": "Y"}
ALL_SEMS = ("T", "V", "S", "G", "Y", "LD", "LG", "ST")



def host_consts():
    import ml_dtypes
    BF = ml_dtypes.bfloat16
    c = {}
    c["ident"] = np.eye(128, dtype=np.float32).astype(BF)
    c["identf"] = np.eye(128, dtype=np.float32)
    c["ones128"] = np.ones((128, 128), np.float32).astype(BF)
    c["ones128f"] = np.ones((128, 128), np.float32)
    m = (np.arange(32)[:, None] <= np.arange(32)[None, :]).astype(np.float32)
    c["maskA"] = np.tile(m, (4, 16)).astype(BF)
    bt = np.zeros((128, 128), np.float32)
    for b in range(4):
        bt[b * 32:(b + 1) * 32, b * 32:(b + 1) * 32] = -m
    c["btrilneg"] = bt
    e8 = np.zeros((8, 8 * 128), np.float32)
    for e in range(8):
        e8[e, e * 128:(e + 1) * 128] = 1.0
    c["esel8"] = e8.astype(BF)
    e4 = np.zeros((128, 256), np.float32)
    for cc in range(4):
        e4[32 * cc + 31, cc * 64:(cc + 1) * 64] = 1.0
    c["esel4"] = e4.astype(BF)
    return c


CONST_SPECS = [("ident", [128, 128], BF16), ("identf", [128, 128], F32),
               ("esel8", [8, 1024], BF16), ("ones128", [128, 128], BF16),
               ("ones128f", [128, 128], F32), ("maskA", [128, 512], BF16),
               ("btrilneg", [128, 128], F32), ("esel4", [128, 256], BF16)]



class Env:
    def __init__(self, nc):
        self.nc = nc
        self.ctx = ExitStack()
        self.block = None
        self.sems = {}
        self.cnt = {}

    def __enter__(self):
        self.ctx.__enter__()
        self._scopes = []
        for name in ALL_SEMS:
            self.sems[name] = self.ctx.enter_context(self.nc.semaphore(name))
            self.cnt[name] = 0
        self.block = self.ctx.enter_context(self.nc.Block())
        return self

    @contextmanager
    def scope(self):
        st = ExitStack()
        self._scopes.append(st)
        try:
            yield
        finally:
            st.close()
            self._scopes.pop()

    def __exit__(self, *a):
        return self.ctx.__exit__(*a)

    def _actx(self):
        return self._scopes[-1] if getattr(self, "_scopes", None) else self.ctx

    def sb(self, name, shape, dtype=F32):
        return self._actx().enter_context(self.nc.sbuf_tensor(name, shape, dtype))

    def ps(self, name, shape, dtype=F32):
        return self._actx().enter_context(self.nc.psum_tensor(name, shape, dtype))

    def inc(self, inst, sem, n=1):
        inst.then_inc(self.sems[sem], n)
        self.cnt[sem] += n
        return inst

    def phase(self, parts):
        self.nphase = getattr(self, "nphase", 0) + 1
        if STOP_AFTER is not None and self.nphase > STOP_AFTER:
            return
        snap = {k: self.cnt[k] for k in ALL_SEMS}
        for name, fn in parts:
            dec = getattr(self.block, name)
            own = ENG_SEM[name]

            def body(eng, fn=fn, own=own):
                for k in ALL_SEMS:
                    if k != own and snap[k] > 0:
                        eng.wait_ge(self.sems[k], snap[k])
                before = sum(self.cnt.values())
                fn(eng)
                assert sum(self.cnt.values()) > before, "phase fn must inc a sem"
            dec(body)



def emit_ln(env, cst, xsrc, xln, scratch, nct=NCT, seg_w=SEG,
            src_stride=None, src_off=0, dst_stride=None, dst_off=0):
    ss = src_stride if src_stride is not None else seg_w
    ds = dst_stride if dst_stride is not None else seg_w
    sq, mu, var, rstd, lntmp = (scratch[k] for k in ("sqseg", "mu", "var", "rstd", "lntmp"))
    pS, pQ = scratch["pS"], scratch["pQ"]
    ones_sq = cst["ones128f"] if sq[:].dtype == F32 else cst["ones128"]
    bs = env.cnt["S"]
    bt = env.cnt["T"]
    bv = env.cnt["V"]

    def ssl(ct):
        return slice(ct * ss + src_off, ct * ss + src_off + seg_w)

    def f_s(s):
        for ct in range(nct):
            last = s.activation(sq[:, ct * seg_w:(ct + 1) * seg_w], xsrc[:, ssl(ct)], AF.Square)
        env.inc(last, "S")
        s.wait_ge(env.sems["V"], bv + 3)
        s.activation(lntmp[:], var[:], AF.Ln, bias=cst["epsc"][:])
        last = s.activation(rstd[:], lntmp[:], AF.Exp, scale=cst["neghc"][:])
        env.inc(last, "S")

    def f_t(t):
        for ct in range(nct):
            t.matmul(pS, cst["ones128f"][:], xsrc[:, ssl(ct)],
                     start=(ct == 0), stop=(ct == nct - 1))
        t.wait_ge(env.sems["S"], bs + 1)
        for ct in range(nct):
            last = t.matmul(pQ, ones_sq[:], sq[:, ct * seg_w:(ct + 1) * seg_w],
                            start=(ct == 0), stop=(ct == nct - 1))
        env.inc(last, "T")

    def f_v(v):
        v.wait_ge(env.sems["T"], bt + 1)
        inst = v.tensor_scalar(mu[:], pS, 1.0 / DM, None, ALU.mult)
        env.inc(inst, "V"); v.wait_ge(env.sems["V"], env.cnt["V"])
        inst = v.tensor_tensor(var[:], mu[:], mu[:], ALU.mult)
        env.inc(inst, "V"); v.wait_ge(env.sems["V"], env.cnt["V"])
        inst = v.scalar_tensor_tensor(var[:], pQ, 1.0 / DM, var[:], ALU.mult, ALU.subtract)
        env.inc(inst, "V")
        v.wait_ge(env.sems["S"], bs + 2)
        for ct in range(nct):
            v.tensor_tensor(lntmp[:], xsrc[:, ssl(ct)], mu[:], ALU.subtract)
            last = v.tensor_tensor(xln[:, ct * ds + dst_off: ct * ds + dst_off + seg_w],
                                   lntmp[:], rstd[:], ALU.mult)
        env.inc(last, "V")
    env.phase([("scalar", f_s), ("tensor", f_t), ("vector", f_v)])


def emit_proj_feat(env, dst, w_sb, src, pbank0, pbank1, njt=8, nct=NCT, seg_w=SEG,
                   evac="copy", evac_arg=None, dst_slicer=None, evac_custom=None):
    base_t = env.cnt["T"]
    jw = njt * 128
    ev_eng = "vector" if evac_custom is None else evac_custom[0]
    ev_sem = ENG_SEM[ev_eng]
    base_e = env.cnt[ev_sem]

    def _hw(h):
        return bass.AP(h.tensor, h.offset, [h.ap[0], [1, seg_w]])

    def f_t(t):
        for jt in range(njt):
            half = _hw(pbank0 if jt % 2 == 0 else pbank1)
            if jt >= 2:
                t.wait_ge(env.sems[ev_sem], base_e + jt - 1)
            for ct in range(nct):
                last = t.matmul(half, w_sb[:, ct * jw + jt * 128: ct * jw + (jt + 1) * 128],
                                src[:, ct * seg_w:(ct + 1) * seg_w],
                                start=(ct == 0), stop=(ct == nct - 1))
            env.inc(last, "T")

    def f_e(eng):
        for jt in range(njt):
            half = _hw(pbank0 if jt % 2 == 0 else pbank1)
            eng.wait_ge(env.sems["T"], base_t + jt + 1)
            if evac_custom is not None:
                last = evac_custom[1](eng, jt, half)
            else:
                dsl = dst_slicer(jt) if dst_slicer else dst[:, jt * seg_w:(jt + 1) * seg_w]
                if evac == "copy":
                    last = eng.tensor_copy(dsl, half)
                else:
                    last = eng.tensor_tensor(dsl, half, evac_arg[:, jt * seg_w:(jt + 1) * seg_w], ALU.add)
            env.inc(last, ev_sem)
    env.phase([("tensor", f_t), (ev_eng, f_e)])


def emit_proj_tok(env, dst, w_sb, xln, pb0, pb1, bias_row=None, ones128=None,
                  ntc=None, nct=NCT, seg_w=SEG, jw=1024):
    if ntc is None:
        ntc = seg_w // 128
    """Token-major projection: dst (128, ntc*jw) bf16, dst[t, j] = sum_c x[c,t] W[c,j]."""
    base_t = env.cnt["T"]
    base_v = env.cnt["V"]

    def f_t(t):
        for tc in range(ntc):
            pb = pb0 if tc % 2 == 0 else pb1
            if tc >= 2:
                t.wait_ge(env.sems["V"], base_v + tc - 1)
            nhalf = jw // 512
            for ct in range(nct):
                stat = xln[:, ct * seg_w + tc * 128: ct * seg_w + (tc + 1) * 128]
                mv = w_sb[:, ct * jw:(ct + 1) * jw]
                for hf in range(nhalf):
                    last = t.matmul(pb[:, hf * 512:(hf + 1) * 512], stat, mv[:, hf * 512:(hf + 1) * 512],
                                    start=(ct == 0), stop=(ct == nct - 1 and bias_row is None))
            if bias_row is not None:
                oc = ones128[0:1, 0:128]
                for hf in range(nhalf):
                    last = t.matmul(pb[:, hf * 512:(hf + 1) * 512], oc, bias_row[0:1, hf * 512:(hf + 1) * 512],
                                    start=False, stop=True)
            env.inc(last, "T")

    def f_v(v):
        for tc in range(ntc):
            pb = pb0 if tc % 2 == 0 else pb1
            v.wait_ge(env.sems["T"], base_t + tc + 1)
            last = v.tensor_copy(dst[:, tc * jw:(tc + 1) * jw], pb[:, 0:jw])
            env.inc(last, "V")
    env.phase([("tensor", f_t), ("vector", f_v)])



def emit_gla(env, cst, xfull, dr, bg_d):
    nc = env.nc

    w_sb = {nm: env.sb("g_" + nm + "_wsb", [128, NCT * 1024], BF16)
            for nm in ("wq", "wk", "wv", "wg", "wo")}
    bgrow = env.sb("g_bgrow", [1, INNER], BF16)

    def p0_gp(gp):
        for nm in ("wq", "wk", "wv", "wg", "wo"):
            r = dr[nm].rearrange("(a p) j -> p a j", p=128)
            last = gp.dma_start(out=w_sb[nm][:].rearrange("p (a j) -> p a j", j=1024), in_=r)
            env.inc(last, "LG", 16)
        last = gp.dma_start(out=bgrow[:], in_=bg_d)
        env.inc(last, "LG", 16)
    env.phase([("gpsimd", p0_gp)])

    xln = env.sb("g_xln", [128, NCT * SEG], BF16)
    scratch = {}
    for nm in ("mu", "var", "rstd"):
        scratch[nm] = env.sb("g_" + nm, [128, SEG], F32)
    scratch["msq"] = scratch["var"]
    scratch["lntmp"] = scratch["var"]

    QT = env.sb("g_QT", [128, NCT * SEG], BF16)
    KT = env.sb("g_KT", [128, NCT * SEG], BF16)
    Ktok = env.sb("g_Ktok", [128, GNTT * 1024], BF16)
    Vtok = env.sb("g_Vtok", [128, GNTT * 1024], BF16)
    cum = env.sb("g_cum", [128, GNTT * 1024], BF16)
    lgn = env.sb("g_lgn", [128, 512], F32)
    Am = env.sb("g_Am", [128, 512], BF16)
    S = env.sb("g_S", [128, 512], F32)
    Sbf = env.sb("g_Sbf", [128, 512], BF16)
    otok = env.sb("g_otok", [128, 1024], BF16)
    oT = env.sb("g_oT", [128, NCT * SEG], BF16)
    scratch["sqseg"] = oT
    expneg = otok

    pB = env.ps("g_pB", [128, 1024], F32)
    pI = env.ps("g_pI", [128, 1024], F32)
    pU = env.ps("g_pU", [128, 512], F32)
    pC = env.ps("g_pC", [128, 512], F32)
    pT = env.ps("g_pT", [128, 1024], BF16)
    pb0, pb1 = pB, pI
    scratch["pS"] = pU[:, 0:SEG]
    scratch["pQ"] = pC[:, 0:SEG]
    tmpB = env.sb("g_tmpB", [128, 512], F32)
    etile = tmpB
    Stmp = tmpB

    def p0_v(v):
        v.memset(S[:], 0.0)
        last = v.memset(Sbf[:], 0.0)
        env.inc(last, "V")
    env.phase([("vector", p0_v)])

    for seg in range(NSEG):
        t0 = seg * SEG

        emit_ln(env, cst, xfull, xln, scratch, src_stride=L, src_off=t0)

        emit_proj_feat(env, QT, w_sb["wq"], xln, pb0[:, 0:512], pb0[:, 512:1024])
        emit_proj_feat(env, KT, w_sb["wk"], xln, pb0[:, 0:512], pb0[:, 512:1024])
        emit_proj_tok(env, Vtok, w_sb["wv"], xln, pb0, pb1)
        emit_proj_tok(env, Ktok, w_sb["wk"], xln, pb0, pb1)

        for tc in range(GNTT):
            base_t = env.cnt["T"]
            base_s = env.cnt["S"]

            def g_t(t, tc=tc, base_t=base_t, base_s=base_s):
                for ct in range(NCT):
                    stat = xln[:, ct * SEG + tc * 128: ct * SEG + (tc + 1) * 128]
                    mv = w_sb["wg"][:, ct * 1024:(ct + 1) * 1024]
                    t.matmul(pb0[:, 0:512], stat, mv[:, 0:512], start=(ct == 0), stop=False)
                    t.matmul(pb0[:, 512:1024], stat, mv[:, 512:1024], start=(ct == 0), stop=False)
                oc = cst["ones128"][0:1, 0:128]
                t.matmul(pb0[:, 0:512], oc, bgrow[0:1, 0:512], start=False, stop=True)
                last = t.matmul(pb0[:, 512:1024], oc, bgrow[0:1, 512:1024], start=False, stop=True)
                env.inc(last, "T")
                for hf in range(2):
                    t.wait_ge(env.sems["S"], base_s + 1 + hf)
                    last = t.matmul(pb1[:, hf * 512:(hf + 1) * 512], cst["btrilneg"][:],
                                    lgn[:], start=True, stop=True)
                    env.inc(last, "T")

            def g_s(s, tc=tc, base_t=base_t):
                s.wait_ge(env.sems["T"], base_t + 1)
                for hf in range(2):
                    if hf == 1:
                        s.wait_ge(env.sems["T"], base_t + 2)
                    hsl = slice(hf * 512, (hf + 1) * 512)
                    s.activation(etile[:], pb0[:, hsl], AF.Exp, scale=cst["neg1c"][:])
                    last = s.activation(lgn[:], etile[:], AF.Ln, bias=cst["onec"][:])
                    env.inc(last, "S")
                s.wait_ge(env.sems["T"], base_t + 3)
                s.activation(expneg[:], pb1[:], AF.Exp, scale=cst["neg1c"][:])
                last = s.activation(cum[:, tc * 1024:(tc + 1) * 1024], pb1[:], AF.Exp)
                env.inc(last, "S")

            def g_v(v, tc=tc, base_s=base_s):
                v.wait_ge(env.sems["S"], base_s + 3)
                sl = slice(tc * 1024, (tc + 1) * 1024)
                last = v.tensor_tensor(Vtok[:, sl], expneg[:], Vtok[:, sl], ALU.mult)
                env.inc(last, "V")
            env.phase([("tensor", g_t), ("scalar", g_s), ("vector", g_v)])

        base_t = env.cnt["T"]
        base_v = env.cnt["V"]
        base_s = env.cnt["S"]

        def p4_t(t):
            for tt in range(GNTT):
                if tt >= 1:
                    t.wait_ge(env.sems["V"], base_v + (tt - 1) * 8 + 5)
                for cc in range(4):
                    base = 32 * cc
                    for h in [hh for hh in range(H) if hh % 2 == 0] + \
                             [hh for hh in range(H) if hh % 2 == 1]:
                        jt = h // 2
                        b64 = 64 * (h % 2)
                        dstp = pU if h % 2 == 0 else pC
                        tsl = slice(jt * SEG + tt * 128 + cc * 32,
                                    jt * SEG + tt * 128 + cc * 32 + 32)
                        last = t.matmul(dstp[base:base + 32, 32 * jt:32 * jt + 32],
                                        KT[b64:b64 + 64, tsl], QT[b64:b64 + 64, tsl],
                                        start=True, stop=True, tile_position=(b64, base))
                env.inc(last, "T")
                for cc in range(4):
                    base = 32 * cc
                    t.wait_ge(env.sems["V"], base_v + tt * 8 + 1 + cc)
                    vtile = Vtok[:, tt * 1024:(tt + 1) * 1024]
                    ktile = Ktok[:, tt * 1024:(tt + 1) * 1024]
                    for h in [hh for hh in range(H) if hh % 2 == 0] + \
                             [hh for hh in range(H) if hh % 2 == 1]:
                        jt = h // 2
                        b64 = 64 * (h % 2)
                        ocol = 64 * jt + 512 * (h % 2)
                        tsl = slice(jt * SEG + tt * 128 + cc * 32,
                                    jt * SEG + tt * 128 + cc * 32 + 32)
                        t.matmul(pB[base:base + 32, ocol:ocol + 64],
                                 QT[b64:b64 + 64, tsl], Sbf[b64:b64 + 64, 64 * jt:64 * jt + 64],
                                 start=True, stop=True, tile_position=(b64, base))
                    for h in range(H):
                        jt = h // 2
                        acol = 32 * jt + 256 * (h % 2)
                        ocol = 64 * jt + 512 * (h % 2)
                        t.matmul(pI[base:base + 32, ocol:ocol + 64],
                                 Am[base:base + 32, acol:acol + 32],
                                 vtile[base:base + 32, 64 * h:64 * h + 64],
                                 start=True, stop=True, tile_position=(base, base))
                    for h in range(H):
                        jt = h // 2
                        b64 = 64 * (h % 2)
                        last = t.matmul(pU[b64:b64 + 64, 64 * jt:64 * jt + 64],
                                        ktile[base:base + 32, 64 * h:64 * h + 64],
                                        vtile[base:base + 32, 64 * h:64 * h + 64],
                                        start=True, stop=True, tile_position=(base, b64))
                    ctile = cum[:, tt * 1024:(tt + 1) * 1024]
                    ceven = ctile[:, 0:1024]
                    ap_ev = bass.AP(ceven.tensor, ceven.offset, [ceven.ap[0], [128, 8], [1, 64]])
                    codd = ctile[:, 64:1024]
                    ap_od = bass.AP(codd.tensor, codd.offset, [codd.ap[0], [128, 8], [1, 64]])
                    t.matmul(pC[0:64, 0:512], esel_slice(cst, cc), ap_ev,
                             start=True, stop=True, tile_position=(0, 0))
                    last = t.matmul(pC[64:128, 0:512], esel_slice(cst, cc), ap_od,
                                    start=True, stop=True, tile_position=(0, 64))
                    env.inc(last, "T")
                t.wait_ge(env.sems["V"], base_v + tt * 8 + 7)
                for k in range(8):
                    last = t.transpose(pT[:, k * 128:(k + 1) * 128],
                                       otok[:, k * 128:(k + 1) * 128], cst["ident"][:])
                env.inc(last, "T")

        def p4_s(s):
            for tt in range(GNTT):
                s.wait_ge(env.sems["T"], base_t + tt * 6 + 5)
                s.wait_ge(env.sems["V"], base_v + tt * 8 + 5)
                last = s.activation(tmpB[:], pB[:, 0:512], AF.Copy)
                env.inc(last, "S")
                s.wait_ge(env.sems["V"], base_v + tt * 8 + 6)
                last = s.activation(tmpB[:], pB[:, 512:1024], AF.Copy)
                env.inc(last, "S")

        def p4_v(v):
            tv = base_t
            for tt in range(GNTT):
                v.wait_ge(env.sems["T"], tv + 1); tv += 1
                v.tensor_tensor(Am[:, 0:256], pU[:, 0:256], cst["maskA"][:, 0:256], ALU.mult)
                last = v.tensor_tensor(Am[:, 256:512], pC[:, 0:256], cst["maskA"][:, 0:256], ALU.mult)
                env.inc(last, "V")
                for cc in range(4):
                    v.wait_ge(env.sems["T"], tv + 1); tv += 1
                    v.tensor_tensor(Stmp[:], S[:], pU[:], ALU.add)
                    v.tensor_tensor(S[:], Stmp[:], pC[:], ALU.mult)
                    last = v.tensor_copy(Sbf[:], S[:])
                    env.inc(last, "V")
                for hf in range(2):
                    v.wait_ge(env.sems["S"], base_s + 2 * tt + 1 + hf)
                    hsl = slice(hf * 512, (hf + 1) * 512)
                    v.tensor_tensor(tmpB[:], pI[:, hsl], tmpB[:], ALU.add)
                    ctile = cum[:, tt * 1024:(tt + 1) * 1024]
                    cperm = bass.AP(ctile.tensor, ctile.offset + 64 * hf,
                                    [ctile.ap[0], [128, 8], [1, 64]])
                    otok3 = otok[:, hsl].rearrange("p (b c) -> p b c", c=64)
                    last = v.tensor_tensor(otok3, tmpB[:].rearrange("p (b c) -> p b c", c=64),
                                           cperm, ALU.mult)
                    env.inc(last, "V")
                v.wait_ge(env.sems["T"], tv + 1); tv += 1
                ot3 = oT[:].rearrange("p (a t) -> p a t", t=SEG)[:, :, tt * 128:(tt + 1) * 128]
                last = v.tensor_copy(ot3, pT[:].rearrange("p (a t) -> p a t", t=128))
                env.inc(last, "V")
        env.phase([("tensor", p4_t), ("scalar", p4_s), ("vector", p4_v)])

        def evac_res(eng, jt, half, t0=t0):
            dsl = xfull[:, jt * L + t0: jt * L + t0 + SEG]
            return eng.tensor_tensor(dsl, half, dsl, ALU.add)
        emit_proj_feat(env, None, w_sb["wo"], oT, pb0[:, 0:512], pb0[:, 512:1024],
                       evac_custom=("vector", evac_res))


def esel_slice(cst, cc):
    return cst["esel4"][:, cc * 64:(cc + 1) * 64]


def permute_wo_rows(wo):
    w = wo.reshape(2 * 8, 64, wo.shape[1])
    perm = [2 * jt + p for p in range(2) for jt in range(8)]
    return np.ascontiguousarray(w[perm].reshape(wo.shape))



def build_gla():
    nc = bass.Bass("TRN2", target_bir_lowering=False, detect_race_conditions=False)
    xin = nc.dram_tensor("xT", [DM, L], F32, kind="ExternalInput").ap()
    dr = {}
    for nm, shp in (("wq", [DM, INNER]), ("wk", [DM, INNER]), ("wv", [DM, INNER]),
                    ("wg", [DM, INNER]), ("wo", [INNER, DM])):
        dr[nm] = nc.dram_tensor(nm, shp, F32, kind="ExternalInput").ap()
    bg = nc.dram_tensor("bg", [1, INNER], F32, kind="ExternalInput").ap()
    GLA_CONSTS = [c for c in CONST_SPECS if c[0] != "esel8" and c[0] != "identf"]
    cst_d = {nm: nc.dram_tensor(nm, shp, dt, kind="ExternalInput").ap()
             for nm, shp, dt in GLA_CONSTS}
    xout = nc.dram_tensor("xoutT", [DM, L], F32, kind="ExternalOutput").ap()

    with Env(nc) as env:
        cst = {nm: env.sb(nm + "_sb", shp, dt) for nm, shp, dt in GLA_CONSTS}
        for nm, val in (("epsc", EPS), ("onec", 1.0), ("neg1c", -1.0), ("neghc", -0.5)):
            cst[nm] = env.sb(nm, [128, 1], F32)
            cst[nm + "_val"] = val
        xfull = env.sb("xfull", [128, NCT * L], F32)

        def p0_v(v):
            for nm in ("epsc", "onec", "neg1c", "neghc"):
                last = v.memset(cst[nm][:], cst[nm + "_val"])
            env.inc(last, "V")

        def p0_sy(sy):
            for nm, shp, dt in GLA_CONSTS:
                sy.dma_start(out=cst[nm][:], in_=cst_d[nm]).then_inc(env.sems["LD"], 16)
                env.cnt["LD"] += 16
            last = sy.dma_start(out=xfull[:].rearrange("p (a t) -> p a t", t=L),
                                in_=xin.rearrange("(a p) t -> p a t", p=128))
            env.inc(last, "LD", 16)
        env.phase([("sync", p0_sy), ("vector", p0_v)])

        with env.scope():
            emit_gla(env, cst, xfull, dr, bg)

        def pf_sy(sy):
            last = sy.dma_start(out=xout.rearrange("(a p) t -> p a t", p=128),
                                in_=xfull[:].rearrange("p (a t) -> p a t", t=L))
            env.inc(last, "ST", 16)
        env.phase([("sync", pf_sy)])

        @env.block.sync
        def _(sy):
            sy.wait_ge(env.sems["ST"], env.cnt["ST"])
    return nc


def run_gla(x_full, Wq, Wk, Wv, Wg, bgv, Wo):
    consts = host_consts()
    nc = build_gla()
    in_maps = []
    B = x_full.shape[0]
    for b in range(B):
        m = {"xT": np.ascontiguousarray(x_full[b].T),
             "wq": Wq, "wk": Wk, "wv": Wv, "wg": Wg,
             "bg": np.ascontiguousarray(bgv.reshape(1, -1)), "wo": permute_wo_rows(Wo)}
        m.update(consts)
        in_maps.append(m)
    res = run_bass_kernel_spmd(nc, in_maps, core_ids=list(range(B)))
    return np.stack([np.ascontiguousarray(res.results[b]["xoutT"]).T for b in range(B)])





DI = 2048
DS = 16
NTI = DI // 128
T0 = 32
SSEG = 512
XPW = SSEG + 32


def emit_ssm(env, cst, xfull, w_in_d, w_gate_d, w_out_d,
             b_in_d, b_gate_d, b_out_d, A_d, B_d, C_d):
    nc = env.nc

    xln = env.sb("s_xln", [128, NCT * SSEG], BF16)
    scratch = {}
    for nm in ("mu", "msq", "var", "rstd", "lntmp"):
        scratch[nm] = env.sb("s_" + nm, [128, SSEG], F32)
    wbuf = env.sb("s_wbuf", [128, 16384], BF16)
    xpseg = env.sb("s_xp", [128, NTI * XPW], BF16)
    y1 = env.sb("s_y1", [128, NTI * SSEG], F32)
    y2 = env.sb("s_y2", [128, NTI * SSEG], BF16)
    gate = env.sb("s_gate", [128, NTI * SSEG], BF16)
    scratch["sqseg"] = gate
    kap = env.sb("s_kap", [128, NTI * T0], F32)
    vA = env.sb("s_vA", [128, NTI * DS], F32)
    vB = env.sb("s_vB", [128, NTI * DS], F32)
    vtmp = env.sb("s_vtmp", [128, 256], F32)
    ktmp = env.sb("s_ktmp", [128, DS], F32)
    Bbuf = env.sb("s_B", [128, NTI * DS], F32)
    Cbuf = env.sb("s_C", [128, NTI * DS], F32)
    bincol = env.sb("s_bin", [128, NTI], F32)
    bgcol = env.sb("s_bg", [128, NTI], F32)
    bocol = env.sb("s_bo", [128, NCT], F32)

    pB = env.ps("s_pB", [128, 1024], F32)
    pI = env.ps("s_pI", [128, 1024], F32)
    pU = env.ps("s_pU", [128, 512], F32)
    pC = env.ps("s_pC", [128, 512], F32)
    scratch["pS"] = pU[:, 0:SSEG]
    scratch["pQ"] = pC[:, 0:SSEG]

    Abuf = y1

    def p_ld(sy):
        sy.dma_start(out=bincol[:], in_=b_in_d).then_inc(env.sems["LD"], 16)
        env.cnt["LD"] += 16
        sy.dma_start(out=bgcol[:], in_=b_gate_d).then_inc(env.sems["LD"], 16)
        env.cnt["LD"] += 16
        sy.dma_start(out=bocol[:], in_=b_out_d).then_inc(env.sems["LD"], 16)
        env.cnt["LD"] += 16
        sy.dma_start(out=Abuf[:, 0:NTI * 256].rearrange("p (a m) -> p a m", m=256),
                     in_=A_d.rearrange("(a p) m -> p a m", p=128)).then_inc(env.sems["LD"], 16)
        env.cnt["LD"] += 16
        sy.dma_start(out=Bbuf[:].rearrange("p (a n) -> p a n", n=DS),
                     in_=B_d.rearrange("(a p) n -> p a n", p=128)).then_inc(env.sems["LD"], 16)
        env.cnt["LD"] += 16
        last = sy.dma_start(out=Cbuf[:].rearrange("p (a n) -> p a n", n=DS),
                            in_=C_d.rearrange("(a p) n -> p a n", p=128))
        env.inc(last, "LD", 16)
    env.phase([("sync", p_ld)])

    def kprec(eng, tiles, vt, kt, own):
        def fn(e):
            def step(inst):
                env.inc(inst, own)
                e.wait_ge(env.sems[own], env.cnt[own])
            for ct in tiles:
                vsl = slice(ct * DS, (ct + 1) * DS)
                step(e.tensor_tensor(kt[:], Cbuf[:, vsl], Bbuf[:, vsl], ALU.mult))
                step(e.tensor_reduce(kap[:, ct * T0:ct * T0 + 1], kt[:], AX.X, ALU.add))
                vcur = Bbuf
                for tau in range(1, T0):
                    vnext = vA if vcur is not vA else vB
                    a3 = Abuf[:, ct * 256:(ct + 1) * 256].rearrange("p (m n) -> p m n", n=DS)
                    vc = vcur[:, vsl]
                    vbc = bass.AP(vc.tensor, vc.offset, [vc.ap[0], [1, DS], [0, DS]])
                    vt_mn = bass.AP(vt[:].tensor, vt[:].offset,
                                    [vt[:].ap[0], [1, DS], [DS, DS]])
                    step(e.tensor_tensor(vt_mn, a3, vbc, ALU.mult))
                    step(e.tensor_reduce(vnext[:, vsl], vt[:].rearrange("p (n m) -> p n m", m=DS),
                                         AX.X, ALU.add))
                    step(e.tensor_tensor(kt[:], Cbuf[:, vsl], vnext[:, vsl], ALU.mult))
                    step(e.tensor_reduce(kap[:, ct * T0 + tau:ct * T0 + tau + 1], kt[:],
                                         AX.X, ALU.add))
                    vcur = vnext
        return fn
    env.phase([("vector", kprec(None, range(0, NTI), vtmp, ktmp, "V"))])

    def p_ms(v):
        last = v.memset(xpseg[:], 0.0)
        env.inc(last, "V")
    env.phase([("vector", p_ms)])

    for s in range(L // SSEG):
        t0 = s * SSEG
        emit_ln(env, cst, xfull, xln, scratch, seg_w=SSEG,
                src_stride=L, src_off=t0)

        def ld_w(gp, wd=w_in_d, n=16):
            r = wd.rearrange("(a p) j -> p a j", p=128)
            last = gp.dma_start(out=wbuf[:].rearrange("p (a j) -> p a j", j=r.shape[2]), in_=r)
            env.inc(last, "LG", 16)
        env.phase([("gpsimd", ld_w)])

        def evac_xp(eng, jt, half):
            return eng.activation(xpseg[:, jt * XPW + 32: jt * XPW + 32 + SSEG], half,
                                  AF.Identity, bias=bincol[:, jt:jt + 1])
        emit_proj_feat(env, None, wbuf, xln, pB[:, 0:512], pB[:, 512:1024],
                       njt=16, nct=8, seg_w=SSEG, evac_custom=("scalar", evac_xp))

        def ld_wg(gp):
            r = w_gate_d.rearrange("(a p) j -> p a j", p=128)
            last = gp.dma_start(out=wbuf[:].rearrange("p (a j) -> p a j", j=2048), in_=r)
            env.inc(last, "LG", 16)
        env.phase([("gpsimd", ld_wg)])

        def evac_gate(eng, jt, half):
            return eng.activation(gate[:, jt * SSEG:(jt + 1) * SSEG], half,
                                  AF.Sigmoid, bias=bgcol[:, jt:jt + 1])
        emit_proj_feat(env, None, wbuf, xln, pB[:, 0:512], pB[:, 512:1024],
                       njt=16, nct=8, seg_w=SSEG, evac_custom=("scalar", evac_gate))

        def conv_v(v):
            for ct in range(NTI):
                ysl = y1[:, ct * SSEG:(ct + 1) * SSEG]
                for tau in range(0, T0):
                    xin = xpseg[:, ct * XPW + 32 - tau: ct * XPW + 32 - tau + SSEG]
                    kcol = kap[:, ct * T0 + tau: ct * T0 + tau + 1]
                    if tau == 0:
                        last = v.tensor_scalar(ysl, xin, kcol, None, ALU.mult)
                    else:
                        last = v.scalar_tensor_tensor(ysl, xin, kcol, ysl, ALU.mult, ALU.add)
            env.inc(last, "V")
        env.phase([("vector", conv_v)])

        def epi_v(v):
            for ct in range(NTI):
                ysl = slice(ct * SSEG, (ct + 1) * SSEG)
                last = v.tensor_tensor(y2[:, ysl], y1[:, ysl], gate[:, ysl], ALU.mult)
            srcap = xpseg[:].rearrange("p (a w) -> p a w", w=XPW)[:, :, SSEG:SSEG + 32]
            dstap = xpseg[:].rearrange("p (a w) -> p a w", w=XPW)[:, :, 0:32]
            last = v.tensor_copy(dstap, srcap)
            env.inc(last, "V")
        env.phase([("vector", epi_v)])

        if SSM_DEBUG and s == 0:
            dbg = {}
            for nm, buf, w in (("d_kap", kap, NTI * T0), ("d_xp", xpseg, NTI * XPW),
                               ("d_y1", y1, NTI * SSEG), ("d_gate", gate, NTI * SSEG),
                               ("d_xln", xln, NCT * SSEG), ("d_y2", y2, NTI * SSEG)):
                dt = {"d_kap": F32, "d_y1": F32}.get(nm, BF16)
                dbg[nm] = nc.dram_tensor(nm, [128, w], dt, kind="ExternalOutput").ap()

            def p_dbg(sy, dbg=dbg):
                for nm, buf in (("d_kap", kap), ("d_xp", xpseg), ("d_y1", y1),
                                ("d_gate", gate), ("d_xln", xln), ("d_y2", y2)):
                    last = sy.dma_start(out=dbg[nm], in_=buf[:])
                    env.inc(last, "ST", 16)
            env.phase([("sync", p_dbg)])

        def ld_wo(gp):
            r = w_out_d.rearrange("(a p) j -> p a j", p=128)
            last = gp.dma_start(out=wbuf[:].rearrange("p (a j) -> p a j", j=1024), in_=r)
            env.inc(last, "LG", 16)
        env.phase([("gpsimd", ld_wo)])

        def evac_out(eng, jt, half, t0=t0):
            dsl = xfull[:, jt * L + t0: jt * L + t0 + SSEG]
            return eng.scalar_tensor_tensor(dsl, half, bocol[:, jt:jt + 1], dsl,
                                            ALU.add, ALU.add)
        emit_proj_feat(env, None, wbuf, y2, pB[:, 0:512], pB[:, 512:1024],
                       njt=8, nct=16, seg_w=SSEG, evac_custom=("vector", evac_out))


def build_ssm():
    nc = bass.Bass("TRN2", target_bir_lowering=False, detect_race_conditions=False)
    xin = nc.dram_tensor("xT", [DM, L], F32, kind="ExternalInput").ap()
    w_in_d = nc.dram_tensor("w_in", [DM, DI], F32, kind="ExternalInput").ap()
    w_gate_d = nc.dram_tensor("w_gate", [DM, DI], F32, kind="ExternalInput").ap()
    w_out_d = nc.dram_tensor("w_out", [DI, DM], F32, kind="ExternalInput").ap()
    b_in_d = nc.dram_tensor("b_in", [128, NTI], F32, kind="ExternalInput").ap()
    b_gate_d = nc.dram_tensor("b_gate", [128, NTI], F32, kind="ExternalInput").ap()
    b_out_d = nc.dram_tensor("b_out", [128, NCT], F32, kind="ExternalInput").ap()
    A_d = nc.dram_tensor("A", [DI, 256], F32, kind="ExternalInput").ap()
    B_d = nc.dram_tensor("B_ssm", [DI, DS], F32, kind="ExternalInput").ap()
    C_d = nc.dram_tensor("C_ssm", [DI, DS], F32, kind="ExternalInput").ap()
    SSM_CONSTS = [c for c in CONST_SPECS if c[0] in ("ones128", "ones128f")]
    cst_d = {nm: nc.dram_tensor(nm, shp, dt, kind="ExternalInput").ap()
             for nm, shp, dt in SSM_CONSTS}
    xout = nc.dram_tensor("xoutT", [DM, L], F32, kind="ExternalOutput").ap()

    with Env(nc) as env:
        cst = {nm: env.sb(nm + "_sb", shp, dt) for nm, shp, dt in SSM_CONSTS}
        for nm, val in (("epsc", EPS), ("onec", 1.0), ("neg1c", -1.0), ("neghc", -0.5)):
            cst[nm] = env.sb(nm, [128, 1], F32)
            cst[nm + "_val"] = val
        xfull = env.sb("s_xfull", [128, NCT * L], F32)

        def p0_v(v):
            for nm in ("epsc", "onec", "neg1c", "neghc"):
                last = v.memset(cst[nm][:], cst[nm + "_val"])
            env.inc(last, "V")

        def p0_sy(sy):
            for nm, shp, dt in SSM_CONSTS:
                sy.dma_start(out=cst[nm][:], in_=cst_d[nm]).then_inc(env.sems["LD"], 16)
                env.cnt["LD"] += 16
            last = sy.dma_start(out=xfull[:].rearrange("p (a t) -> p a t", t=L),
                                in_=xin.rearrange("(a p) t -> p a t", p=128))
            env.inc(last, "LD", 16)
        env.phase([("sync", p0_sy), ("vector", p0_v)])
        emit_ssm(env, cst, xfull, w_in_d, w_gate_d, w_out_d,
                 b_in_d, b_gate_d, b_out_d, A_d, B_d, C_d)

        def pf_sy(sy):
            last = sy.dma_start(out=xout.rearrange("(a p) t -> p a t", p=128),
                                in_=xfull[:].rearrange("p (a t) -> p a t", t=L))
            env.inc(last, "ST", 16)
        env.phase([("sync", pf_sy)])

        @env.block.sync
        def _(sy):
            sy.wait_ge(env.sems["ST"], env.cnt["ST"])
    return nc


def run_ssm(x_full, w_in, b_in, w_gate, b_gate, A, B_ssm, C_ssm, w_out, b_out):
    consts = host_consts()
    nc = build_ssm()
    in_maps = []
    B = x_full.shape[0]
    for b in range(B):
        m = {"xT": np.ascontiguousarray(x_full[b].T),
             "w_in": w_in, "w_gate": w_gate, "w_out": w_out,
             "b_in": np.ascontiguousarray(b_in.reshape(NTI, 128).T),
             "b_gate": np.ascontiguousarray(b_gate.reshape(NTI, 128).T),
             "b_out": np.ascontiguousarray(b_out.reshape(NCT, 128).T),
             "A": np.ascontiguousarray(A.transpose(0, 2, 1).reshape(DI, 256)),
             "B_ssm": B_ssm, "C_ssm": C_ssm}
        m.update(consts)
        in_maps.append(m)
    res = run_bass_kernel_spmd(nc, in_maps, core_ids=list(range(B)))
    return np.stack([np.ascontiguousarray(res.results[b]["xoutT"]).T for b in range(B)])



HID = 2816
NHT = HID // 128
FSEG = 128
FNTC = FSEG // 128


def _q_slot(n, i):
    return (i % 2) if (i // 2) * 2 + 1 < n else 0


def quant_xn_counts(base, n):
    out = []
    for i in range(n):
        p, r = divmod(i, 2)
        if 2 * p + 1 < n:
            out.append(base + 10 * p + 7 + r)
        else:
            out.append(base + 10 * p + 4)
    return out


def quant_s_body(env, s, jobs, q, xn_counts):
    V = env.sems["V"]
    ab, l2 = q["ab"], q["l2"]
    n = len(jobs)
    for i, (srca, dsta, w) in enumerate(jobs):
        a0 = _q_slot(n, i) * w
        s.wait_ge(V, xn_counts[i])
        s.activation(l2[:, a0:a0 + w], ab[:, a0:a0 + w], AF.Sign)
        inst = s.activation(ab[:, a0:a0 + w], ab[:, a0:a0 + w], AF.Abs)
        env.inc(inst, "S")


def quant_v_body(env, v, jobs, q, s_counts):
    V, S_ = env.sems["V"], env.sems["S"]
    scl, rscl, ab, lv, l1, l2 = (q[k] for k in ("scl", "rscl", "ab", "lv", "l1", "l2"))
    n = len(jobs)
    rec = {}
    for g0 in range(0, n, 2):
        pair = list(range(g0, min(g0 + 2, n)))

        def sl(i):
            srca, dsta, w = jobs[i]
            return srca, dsta, w, w // 16, _q_slot(n, i)

        for i in pair:
            srca, dsta, w, nb, s_ = sl(i)
            inst = v.tensor_reduce(scl[:, s_ * nb:(s_ + 1) * nb],
                                   srca.rearrange("p (a b) -> p a b", b=16),
                                   AX.X, ALU.max, apply_absolute_value=True)
            env.inc(inst, "V"); rec[("r1", i)] = env.cnt["V"]
        for i in pair:
            srca, dsta, w, nb, s_ = sl(i)
            v.wait_ge(V, rec[("r1", i)])
            inst = v.tensor_scalar(scl[:, s_ * nb:(s_ + 1) * nb], scl[:, s_ * nb:(s_ + 1) * nb],
                                   1.0 / 6.0, 1e-12, ALU.mult, ALU.max)
            env.inc(inst, "V"); rec[("r2", i)] = env.cnt["V"]
        for i in pair:
            srca, dsta, w, nb, s_ = sl(i)
            v.wait_ge(V, rec[("r2", i)])
            inst = v.reciprocal(rscl[:, s_ * nb:(s_ + 1) * nb], scl[:, s_ * nb:(s_ + 1) * nb])
            env.inc(inst, "V"); rec[("r3", i)] = env.cnt["V"]
        for i in pair:
            srca, dsta, w, nb, s_ = sl(i)
            v.wait_ge(V, rec[("r3", i)])
            rb = rscl[:, s_ * nb:(s_ + 1) * nb]
            rbc = bass.AP(rb.tensor, rb.offset, [rb.ap[0], [1, nb], [0, 16]])
            a0 = s_ * w
            inst = v.tensor_tensor(ab[:, a0:a0 + w].rearrange("p (a b) -> p a b", b=16),
                                   srca.rearrange("p (a b) -> p a b", b=16), rbc, ALU.mult)
            env.inc(inst, "V")
        for i in pair:
            srca, dsta, w, nb, s_ = sl(i)
            a0 = s_ * w
            v.wait_ge(S_, s_counts[i])
            v.tensor_scalar(lv[:, a0:a0 + w], ab[:, a0:a0 + w], 0.25, None, ALU.is_gt)
            for t in (0.75, 1.25, 1.75):
                v.scalar_tensor_tensor(lv[:, a0:a0 + w], ab[:, a0:a0 + w], t, lv[:, a0:a0 + w],
                                       ALU.is_gt, ALU.add)
            v.tensor_scalar(l1[:, a0:a0 + w], ab[:, a0:a0 + w], 2.5, None, ALU.is_gt)
            v.scalar_tensor_tensor(l1[:, a0:a0 + w], ab[:, a0:a0 + w], 3.5, l1[:, a0:a0 + w],
                                   ALU.is_gt, ALU.add)
            v.scalar_tensor_tensor(lv[:, a0:a0 + w], lv[:, a0:a0 + w], 0.5, l1[:, a0:a0 + w],
                                   ALU.mult, ALU.add)
            v.tensor_scalar(l1[:, a0:a0 + w], ab[:, a0:a0 + w], 5.0, 2.0, ALU.is_gt, ALU.mult)
            v.tensor_tensor(lv[:, a0:a0 + w], lv[:, a0:a0 + w], l1[:, a0:a0 + w], ALU.add)
            v.tensor_tensor(lv[:, a0:a0 + w], lv[:, a0:a0 + w], l2[:, a0:a0 + w], ALU.mult)
            sb_ = scl[:, s_ * nb:(s_ + 1) * nb]
            sbc = bass.AP(sb_.tensor, sb_.offset, [sb_.ap[0], [1, nb], [0, 16]])
            inst = v.tensor_tensor(dsta.rearrange("p (a b) -> p a b", b=16),
                                   lv[:, a0:a0 + w].rearrange("p (a b) -> p a b", b=16), sbc, ALU.mult)
            env.inc(inst, "V")


def emit_quant_multi(env, jobs, q):
    bv = env.cnt["V"]
    bs = env.cnt["S"]
    xn_counts = quant_xn_counts(bv, len(jobs))
    s_counts = [bs + i + 1 for i in range(len(jobs))]

    def fs(s):
        quant_s_body(env, s, jobs, q, xn_counts)

    def fv(v):
        quant_v_body(env, v, jobs, q, s_counts)
    env.phase([("scalar", fs), ("vector", fv)])


def quant_chunks(env, src_ap_fn, dst_ap_fn, q, total, chunk=256):
    jobs = []
    off = 0
    while off < total:
        w = min(chunk, total - off)
        jobs.append((src_ap_fn(off, w), dst_ap_fn(off, w), w))
        off += w
    emit_quant_multi(env, jobs, q)


def emit_transpose_tiles(env, pairs, psum, ident, evac_dtype_note=None):
    ngrp = psum.shape[1] // 128 if hasattr(psum, "shape") else 4
    base_t = env.cnt["T"]
    base_v = env.cnt["V"]
    n = len(pairs)
    ngroups = (n + ngrp - 1) // ngrp

    def f_t(t):
        for g in range(ngroups):
            if g >= 1:
                t.wait_ge(env.sems["V"], base_v + g)
            lo, hi = g * ngrp, min((g + 1) * ngrp, n)
            for i in range(lo, hi):
                src, dstp = pairs[i]
                last = t.transpose(psum[:, (i - lo) * 128:(i - lo + 1) * 128], src, ident)
            env.inc(last, "T")

    def f_v(v):
        for g in range(ngroups):
            v.wait_ge(env.sems["T"], base_t + g + 1)
            lo, hi = g * ngrp, min((g + 1) * ngrp, n)
            for i in range(lo, hi):
                src, dstp = pairs[i]
                last = v.tensor_copy(dstp, psum[:, (i - lo) * 128:(i - lo + 1) * 128])
            env.inc(last, "V")
    env.phase([("tensor", f_t), ("vector", f_v)])


def emit_wprep_gu(env, cst, q, wdram, wdstT, wstage, wq_bf, pTb):
    bv, bs, bt, bld = env.cnt["V"], env.cnt["S"], env.cnt["T"], env.cnt["LD"]
    V, S_, T_, LD = env.sems["V"], env.sems["S"], env.sems["T"], env.sems["LD"]

    def jobs_for(wt):
        sl = (wt % 2) * 1024
        return [(wstage[:, sl + o: sl + o + 256], wq_bf[:, sl + o: sl + o + 256], 256)
                for o in range(0, 1024, 256)]

    def f_ld(sy):
        for wt in range(NHT):
            if wt >= 2:
                sy.wait_ge(V, bv + 21 * (wt - 2) + 20)
            last = sy.dma_start(out=wstage[:, (wt % 2) * 1024:(wt % 2) * 1024 + 1024],
                                in_=wdram[wt * 128:(wt + 1) * 128, :])
            env.inc(last, "LD", 16)

    def f_v(v):
        for wt in range(NHT):
            if wt >= 1:
                v.wait_ge(T_, bt + wt)
                k = wt - 1
                for kc in range(8):
                    last = v.tensor_copy(wdstT[:, kc * HID + k * 128:kc * HID + (k + 1) * 128],
                                         pTb[:, kc * 128:(kc + 1) * 128])
                env.inc(last, "V")
            v.wait_ge(LD, bld + 16 * (wt + 1))
            if wt >= 2:
                v.wait_ge(T_, bt + wt - 1)
            quant_v_body(env, v, jobs_for(wt), q,
                         [bs + 4 * wt + i + 1 for i in range(4)])
        v.wait_ge(T_, bt + NHT)
        k = NHT - 1
        for kc in range(8):
            last = v.tensor_copy(wdstT[:, kc * HID + k * 128:kc * HID + (k + 1) * 128],
                                 pTb[:, kc * 128:(kc + 1) * 128])
        env.inc(last, "V")

    def f_s(s):
        for wt in range(NHT):
            quant_s_body(env, s, jobs_for(wt), q,
                         quant_xn_counts(bv + 21 * wt, 4))

    def f_t(t):
        for wt in range(NHT):
            t.wait_ge(V, bv + 21 * wt + 20)
            sl = (wt % 2) * 1024
            for kc in range(8):
                last = t.transpose(pTb[:, kc * 128:(kc + 1) * 128],
                                   wq_bf[:, sl + kc * 128: sl + (kc + 1) * 128],
                                   cst["ident"][:])
            env.inc(last, "T")
    env.phase([("sync", f_ld), ("vector", f_v), ("scalar", f_s), ("tensor", f_t)])


def emit_wprep_wd(env, cst, q, wd_d, wdq_dram, wstage, wq_bf, wdk, pTb):
    bv, bs, bt, bld, bst = (env.cnt["V"], env.cnt["S"], env.cnt["T"],
                            env.cnt["LD"], env.cnt["ST"])
    V, S_, T_, LD, ST = (env.sems["V"], env.sems["S"], env.sems["T"],
                         env.sems["LD"], env.sems["ST"])
    NJ = HID // 256
    NG = (NHT + 7) // 8
    PTV = 5 * NJ + NG

    def jobs_for(wt):
        return [(wstage[:, o:o + 256], wq_bf[:, o:o + 256], 256)
                for o in range(0, HID, 256)]

    def stores_for(sy, wt):
        for g in range(NG):
            m = NG * wt + g
            sy.wait_ge(V, bv + PTV * wt + 5 * NJ + g + 1)
            lo, hi = g * 8, min(g * 8 + 8, NHT)
            dst = wdq_dram.rearrange("(a p) j -> p a j", p=128)[:, lo:hi,
                                                                wt * 128:(wt + 1) * 128]
            last = sy.dma_start(out=dst,
                                in_=wdk[m % 2][:, 0:(hi - lo) * 128].rearrange(
                                    "p (a j) -> p a j", j=128))
            env.inc(last, "ST", 16)

    def f_ld(sy):
        for wt in range(8):
            if wt >= 1:
                sy.wait_ge(V, bv + PTV * (wt - 1) + 5 * NJ)
            last = sy.dma_start(out=wstage[:], in_=wd_d[wt * 128:(wt + 1) * 128, :])
            env.inc(last, "LD", 16)
            if wt >= 1:
                stores_for(sy, wt - 1)
        stores_for(sy, 7)

    def f_v(v):
        for wt in range(8):
            v.wait_ge(LD, bld + 16 * (wt + 1))
            if wt >= 1:
                v.wait_ge(T_, bt + 3 * wt)
            quant_v_body(env, v, jobs_for(wt), q,
                         [bs + NJ * wt + i + 1 for i in range(NJ)])
            for g in range(NG):
                m = NG * wt + g
                v.wait_ge(T_, bt + m + 1)
                if m >= 2:
                    v.wait_ge(ST, bst + 16 * (m - 1))
                lo, hi = g * 8, min(g * 8 + 8, NHT)
                for kc in range(lo, hi):
                    last = v.tensor_copy(wdk[m % 2][:, (kc - lo) * 128:(kc - lo + 1) * 128],
                                         pTb[:, (kc - lo) * 128:(kc - lo + 1) * 128])
                env.inc(last, "V")

    def f_s(s):
        for wt in range(8):
            quant_s_body(env, s, jobs_for(wt), q,
                         quant_xn_counts(bv + PTV * wt, NJ))

    def f_t(t):
        for wt in range(8):
            for g in range(NG):
                m = NG * wt + g
                if g == 0:
                    t.wait_ge(V, bv + PTV * wt + 5 * NJ)
                else:
                    t.wait_ge(V, bv + PTV * wt + 5 * NJ + g)
                lo, hi = g * 8, min(g * 8 + 8, NHT)
                for kc in range(lo, hi):
                    last = t.transpose(pTb[:, (kc - lo) * 128:(kc - lo + 1) * 128],
                                       wq_bf[:, kc * 128:(kc + 1) * 128], cst["ident"][:])
                env.inc(last, "T")

    env.phase([("sync", f_ld), ("vector", f_v), ("scalar", f_s), ("tensor", f_t)])


def emit_fp4ffn(env, cst, xfull, wdq_dram, wg_d, wu_d, wd_d):
    nc = env.nc

    wgqT = env.sb("f_wgqT", [128, 8 * HID], BF16)
    wuqT = env.sb("f_wuqT", [128, 8 * HID], BF16)
    wdk = [env.sb("f_wdk0", [128, DM], BF16), env.sb("f_wdk1", [128, DM], BF16)]

    q = {"scl": env.sb("f_scl", [128, 64], F32),
         "rscl": env.sb("f_rscl", [128, 64], F32),
         "ab": env.sb("f_qab", [128, 512], F32),
         "lv": env.sb("f_qlv", [128, 512], F32),
         "l1": env.sb("f_ql1", [128, 512], F32),
         "l2": env.sb("f_ql2", [128, 512], F32)}

    xlnf = env.sb("f_xlnf", [128, NCT * FSEG], F32)
    scratch = {}
    for nm in ("mu", "msq", "var", "rstd", "lntmp"):
        scratch[nm] = env.sb("f_" + nm, [128, FSEG], F32)
    xtok = env.sb("f_xtok", [128, FNTC * DM], F32)
    dsb = xtok

    xqT = env.sb("f_xqT", [128, NCT * FSEG], BF16)
    hmt = env.sb("f_hmt", [128, 2 * FSEG], F32)
    hmtok = env.sb("f_hmtok", [128, HID], F32)
    wstage = hmtok
    hqtok = env.sb("f_hqtok", [128, HID], BF16)
    xqtok = hqtok
    wq_bf = hqtok
    hqT = env.sb("f_hqT", [128, NHT * FSEG], BF16)
    scratch["sqseg"] = hqT
    sgt = env.sb("f_sgt", [128, 2 * FSEG], F32)

    pg = env.ps("f_pg", [128, 512], F32)
    pu = env.ps("f_pu", [128, 512], F32)
    pTf = env.ps("f_pTf", [128, 512], F32)
    pTb = env.ps("f_pTb", [128, 1024], BF16)
    pd = env.ps("f_pd", [128, NCT * FSEG], F32)
    scratch["pS"] = pg[:, 0:FSEG]
    scratch["pQ"] = pu[:, 0:FSEG]

    for wdram, wdstT in ((wg_d, wgqT), (wu_d, wuqT)):
        emit_wprep_gu(env, cst, q, wdram, wdstT, wstage, wq_bf, pTb)
    emit_wprep_wd(env, cst, q, wd_d, wdq_dram, wstage, wq_bf, wdk, pTb)

    for s_ in range(L // FSEG):
        t0 = s_ * FSEG

        emit_ln(env, cst, xfull, xlnf, scratch, seg_w=FSEG, src_stride=L, src_off=t0)

        pairs = [(xlnf[:, ct * FSEG + tc * 128: ct * FSEG + (tc + 1) * 128],
                  xtok[:, tc * DM + ct * 128: tc * DM + (ct + 1) * 128])
                 for ct in range(NCT) for tc in range(FNTC)]
        emit_transpose_tiles(env, pairs, pTf[:, 0:512], cst["identf"][:])

        jobs = []
        for tc in range(FNTC):
            o = 0
            while o < DM:
                w = min(256, DM - o)
                jobs.append((xtok[:, tc * DM + o: tc * DM + o + w],
                             xqtok[:, tc * DM + o: tc * DM + o + w], w))
                o += w
        emit_quant_multi(env, jobs, q)

        pairs = [(xqtok[:, tc * DM + ct * 128: tc * DM + (ct + 1) * 128],
                  xqT[:, ct * FSEG + tc * 128: ct * FSEG + (tc + 1) * 128])
                 for tc in range(FNTC) for ct in range(NCT)]
        emit_transpose_tiles(env, pairs, pTb[:, 0:1024], cst["ident"][:])

        for tsub in range(FNTC):
            base_s = env.cnt["S"]
            base_v = env.cnt["V"]
            incs = {}

            def hid_t(t, tsub=tsub):
                def tr(k):
                    t.wait_ge(env.sems["V"], base_v + 2 * k + 1)
                    last = t.transpose(pTf[:, (k % 2) * 128:(k % 2) * 128 + 128],
                                       hmt[:, (k % 2) * 128:(k % 2) * 128 + 128],
                                       cst["identf"][:])
                    env.inc(last, "T")
                    incs[("tr", k)] = env.cnt["T"]
                for nt in range(NHT):
                    if nt >= 2:
                        t.wait_ge(env.sems["V"], base_v + 2 * nt - 3)
                        t.wait_ge(env.sems["S"], base_s + nt - 1)
                    pga = pg if nt % 2 == 0 else pd[:, 0:512]
                    pua = pu if nt % 2 == 0 else pd[:, 512:1024]
                    for kc in range(8):
                        t.matmul(pga[:, 0:FSEG],
                                 wgqT[:, kc * HID + nt * 128: kc * HID + (nt + 1) * 128],
                                 xqT[:, kc * FSEG:(kc + 1) * FSEG],
                                 start=(kc == 0), stop=(kc == 7))
                    for kc in range(8):
                        last = t.matmul(pua[:, 0:FSEG],
                                        wuqT[:, kc * HID + nt * 128: kc * HID + (nt + 1) * 128],
                                        xqT[:, kc * FSEG:(kc + 1) * FSEG],
                                        start=(kc == 0), stop=(kc == 7))
                    env.inc(last, "T")
                    incs[("mm", nt)] = env.cnt["T"]
                    if nt >= 1:
                        tr(nt - 1)
                tr(NHT - 1)

            def hid_s(s, tsub=tsub):
                for nt in range(NHT):
                    pga = pg if nt % 2 == 0 else pd[:, 0:512]
                    if nt >= 2:
                        s.wait_ge(env.sems["V"], base_v + 2 * nt - 3)
                    s.wait_ge(env.sems["T"], incs[("mm", nt)])
                    last = s.activation(sgt[:, (nt % 2) * FSEG:(nt % 2) * FSEG + FSEG],
                                        pga[:, 0:FSEG], AF.Silu)
                    env.inc(last, "S")

            def hid_v(v, tsub=tsub):
                for nt in range(NHT):
                    pua = pu if nt % 2 == 0 else pd[:, 512:1024]
                    v.wait_ge(env.sems["S"], base_s + nt + 1)
                    if nt >= 2:
                        v.wait_ge(env.sems["T"], incs[("tr", nt - 2)])
                    last = v.tensor_tensor(hmt[:, (nt % 2) * 128:(nt % 2) * 128 + 128],
                                           sgt[:, (nt % 2) * FSEG:(nt % 2) * FSEG + FSEG],
                                           pua[:, 0:FSEG], ALU.mult)
                    env.inc(last, "V")
                    v.wait_ge(env.sems["T"], incs[("tr", nt)])
                    last = v.tensor_copy(hmtok[:, nt * 128:(nt + 1) * 128],
                                         pTf[:, (nt % 2) * 128:(nt % 2) * 128 + 128])
                    env.inc(last, "V")
            env.phase([("tensor", hid_t), ("scalar", hid_s), ("vector", hid_v)])

            quant_chunks(env, lambda o, w: hmtok[:, o:o + w],
                         lambda o, w: hqtok[:, o:o + w], q, HID)

            pairs = [(hqtok[:, nt * 128:(nt + 1) * 128],
                      hqT[:, nt * FSEG + tsub * 128: nt * FSEG + tsub * 128 + 128])
                     for nt in range(NHT)]
            emit_transpose_tiles(env, pairs, pTb[:, 0:1024], cst["ident"][:])

        base_t = env.cnt["T"]
        base_ld = env.cnt["LD"]

        def down_ld(sy):
            for kc in range(NHT):
                if kc >= 2:
                    sy.wait_ge(env.sems["T"], base_t + kc - 1)
                src_ = wdq_dram[kc * 128:(kc + 1) * 128, :]
                last = sy.dma_start(out=wdk[kc % 2][:], in_=src_)
                env.inc(last, "LD", 16)

        def down_t(t):
            for kc in range(NHT):
                t.wait_ge(env.sems["LD"], base_ld + 16 * (kc + 1))
                for tsub in range(FNTC):
                    for hf in range(2):
                        last = t.matmul(pd[:, tsub * DM + hf * 512: tsub * DM + (hf + 1) * 512],
                                        hqT[:, kc * FSEG + tsub * 128: kc * FSEG + (tsub + 1) * 128],
                                        wdk[kc % 2][:, hf * 512:(hf + 1) * 512],
                                        start=(kc == 0), stop=(kc == NHT - 1))
                env.inc(last, "T")

        def down_ev(v):
            v.wait_ge(env.sems["T"], base_t + NHT)
            for tsub in range(FNTC):
                last = v.tensor_copy(dsb[:, tsub * DM:(tsub + 1) * DM],
                                     pd[:, tsub * DM:(tsub + 1) * DM])
            env.inc(last, "V")
        env.phase([("sync", down_ld), ("tensor", down_t), ("vector", down_ev)])

        base_t = env.cnt["T"]
        base_v = env.cnt["V"]

        def dtr_t(t):
            n = 0
            for tsub in range(FNTC):
                for ct in range(NCT):
                    if n % 4 == 0 and n >= 4:
                        t.wait_ge(env.sems["V"], base_v + n // 4)
                    last = t.transpose(pTf[:, (n % 4) * 128:(n % 4 + 1) * 128],
                                       dsb[:, tsub * DM + ct * 128: tsub * DM + (ct + 1) * 128],
                                       cst["identf"][:])
                    n += 1
                    if n % 4 == 0:
                        env.inc(last, "T")

        def dtr_v(v, t0=t0):
            n = 0
            for tsub in range(FNTC):
                for ct in range(NCT):
                    if n % 4 == 0:
                        v.wait_ge(env.sems["T"], base_t + n // 4 + 1)
                    sl = slice(ct * L + t0 + tsub * 128, ct * L + t0 + (tsub + 1) * 128)
                    last = v.tensor_tensor(xfull[:, sl], pTf[:, (n % 4) * 128:(n % 4 + 1) * 128],
                                           xfull[:, sl], ALU.add)
                    n += 1
                    if n % 4 == 0:
                        env.inc(last, "V")
        env.phase([("tensor", dtr_t), ("vector", dtr_v)])


def build_fp4():
    nc = bass.Bass("TRN2", target_bir_lowering=False, detect_race_conditions=False)
    xin = nc.dram_tensor("xT", [DM, L], F32, kind="ExternalInput").ap()
    wg_d = nc.dram_tensor("w_gate", [HID, DM], F32, kind="ExternalInput").ap()
    wu_d = nc.dram_tensor("w_up", [HID, DM], F32, kind="ExternalInput").ap()
    wd_d = nc.dram_tensor("w_down", [DM, HID], F32, kind="ExternalInput").ap()
    FP4_CONSTS = [c for c in CONST_SPECS if c[0] in ("ones128", "ones128f", "ident", "identf")]
    cst_d = {nm: nc.dram_tensor(nm, shp, dt, kind="ExternalInput").ap()
             for nm, shp, dt in FP4_CONSTS}
    wdq_dram = nc.dram_tensor("f_wdq_scratch", [NHT * 128, DM], BF16, kind="Internal").ap()
    xout = nc.dram_tensor("xoutT", [DM, L], F32, kind="ExternalOutput").ap()

    with Env(nc) as env:
        cst = {nm: env.sb(nm + "_sb", shp, dt) for nm, shp, dt in FP4_CONSTS}
        for nm, val in (("epsc", EPS), ("neghc", -0.5)):
            cst[nm] = env.sb(nm, [128, 1], F32)
            cst[nm + "_val"] = val
        xfull = env.sb("f_xfull", [128, NCT * L], F32)

        def p0_v(v):
            for nm in ("epsc", "neghc"):
                last = v.memset(cst[nm][:], cst[nm + "_val"])
            env.inc(last, "V")

        def p0_sy(sy):
            for nm, shp, dt in FP4_CONSTS:
                sy.dma_start(out=cst[nm][:], in_=cst_d[nm]).then_inc(env.sems["LD"], 16)
                env.cnt["LD"] += 16
            last = sy.dma_start(out=xfull[:].rearrange("p (a t) -> p a t", t=L),
                                in_=xin.rearrange("(a p) t -> p a t", p=128))
            env.inc(last, "LD", 16)
        env.phase([("sync", p0_sy), ("vector", p0_v)])
        with env.scope():
            emit_fp4ffn(env, cst, xfull, wdq_dram, wg_d, wu_d, wd_d)

        def pf_sy(sy):
            last = sy.dma_start(out=xout.rearrange("(a p) t -> p a t", p=128),
                                in_=xfull[:].rearrange("p (a t) -> p a t", t=L))
            env.inc(last, "ST", 16)
        env.phase([("sync", pf_sy)])

        @env.block.sync
        def _(sy):
            sy.wait_ge(env.sems["ST"], env.cnt["ST"])
    return nc


def run_fp4(x_full, w_gate, w_up, w_down):
    consts = host_consts()
    nc = build_fp4()
    in_maps = []
    B = x_full.shape[0]
    for b in range(B):
        m = {"xT": np.ascontiguousarray(x_full[b].T),
             "w_gate": w_gate, "w_up": w_up, "w_down": w_down}
        m.update(consts)
        in_maps.append(m)
    res = run_bass_kernel_spmd(nc, in_maps, core_ids=list(range(B)))
    return np.stack([np.ascontiguousarray(res.results[b]["xoutT"]).T for b in range(B)])





NEXP = 8
MSEG = 512
MNTC = MSEG // 128
PSEG = 128


def emit_moe(env, cst, xfull, xln_dram, w2q_dram, w3q_dram, gate_wT_d, w1_d, w3_d, w2_d):
    nc = env.nc

    with env.scope():
        w2stage = env.sb("m_w2stage", [128, NHT * DM], BF16)
        for e in range(NEXP):
            def cst_gp(gp, e=e):
                r = w2_d[e].rearrange("(a p) j -> p a j", p=128)
                last = gp.dma_start(out=w2stage[:].rearrange("p (a j) -> p a j", j=DM), in_=r)
                env.inc(last, "LG", 16)
            env.phase([("gpsimd", cst_gp)])

            def cst_st(sy, e=e):
                for jp in range(2):
                    dst = w2q_dram[(e * 2 + jp) * NHT * 128:(e * 2 + jp + 1) * NHT * 128,
                                   :].rearrange("(nt p) j -> p nt j", p=128)
                    srcs = w2stage[:].rearrange("p (nt j) -> p nt j", j=DM)[
                        :, :, jp * 512:(jp + 1) * 512]
                    last = sy.dma_start(out=dst, in_=srcs)
                    env.inc(last, "ST", 16)
            env.phase([("sync", cst_st)])
            def cst3_gp(gp, e=e):
                for a in range(NCT):
                    srca = w3_d[e, a * 128:(a + 1) * 128, :].rearrange(
                        "p (nt h) -> p nt h", h=128)
                    dsta = w2stage[:, 0:NHT * 1024].rearrange(
                        "p (nt c) -> p nt c", c=1024)[:, :, a * 128:(a + 1) * 128]
                    last = gp.dma_start(out=dsta, in_=srca)
                    env.inc(last, "LG", 16)
            env.phase([("gpsimd", cst3_gp)])

            def cst3_st(sy, e=e):
                dst = w3q_dram[e * NHT * 128:(e + 1) * NHT * 128, :].rearrange(
                    "(nt p) c -> p nt c", p=128)
                last = sy.dma_start(out=dst, in_=w2stage[:, 0:NHT * 1024].rearrange(
                    "p (nt c) -> p nt c", c=1024))
                env.inc(last, "ST", 16)
            env.phase([("sync", cst3_st)])

    w1buf = env.sb("m_w1", [128, NCT * HID], BF16)
    w3k = [env.sb("m_w3k0", [128, NCT * 128], BF16), env.sb("m_w3k1", [128, NCT * 128], BF16)]
    wdk = [env.sb("m_wdk0", [128, 512], BF16), env.sb("m_wdk1", [128, 512], BF16)]
    xlnf = env.sb("m_xlnf", [128, NCT * PSEG], F32)
    xlnbf = env.sb("m_xlnbf", [128, NCT * PSEG], BF16)
    scratch = {}
    for nm in ("mu", "msq", "var", "rstd", "lntmp"):
        scratch[nm] = env.sb("m_" + nm, [128, PSEG], F32)
    gwT = env.sb("m_gwT", [128, NCT * NEXP], F32)
    lgt = env.sb("m_lgt", [128, NEXP], F32)
    rmax = env.sb("m_rmax", [128, 1], F32)
    rnegm = env.sb("m_rnegm", [128, 1], F32)
    rex = env.sb("m_rex", [128, NEXP], F32)
    rsum = env.sb("m_rsum", [128, 1], F32)
    rrs = env.sb("m_rrs", [128, 1], F32)
    req1 = env.sb("m_req1", [128, NEXP], F32)
    rtmp = env.sb("m_rtmp", [128, NEXP], F32)
    rm2 = env.sb("m_rm2", [128, 1], F32)
    re2x = env.sb("m_re2x", [128, 1], F32)
    rs2 = env.sb("m_rs2", [128, 1], F32)
    req2 = env.sb("m_req2", [128, NEXP], F32)
    rt1 = env.sb("m_rt1", [128, NEXP], F32)
    mrowtok = env.sb("m_mrowtok", [128, (L // 128) * NEXP], F32)
    mtr = env.sb("m_mtr", [8, L], BF16)
    mbc = env.sb("m_mbc", [128, MSEG], BF16)
    xeseg = env.sb("m_xeseg", [128, NCT * MSEG], BF16)
    scratch["sqseg"] = env.sb("m_sqf", [128, NCT * PSEG], F32)
    hbuf = env.sb("m_hbuf", [128, NHT * MSEG], BF16)
    sgt = env.sb("m_sgt", [128, 3 * MSEG], F32)
    dsbb = env.sb("m_dsbb", [128, MNTC * 512], BF16)

    pg = env.ps("m_pg", [128, 512], F32)
    pu = env.ps("m_pu", [128, 512], F32)
    pd = env.ps("m_pd", [128, MNTC * 512], F32)
    pT = env.ps("m_pT", [128, 1024], BF16)
    scratch["pS"] = pg[:, 0:PSEG]
    scratch["pQ"] = pu[:, 0:PSEG]

    def p_gw(sy):
        last = sy.dma_start(out=gwT[:].rearrange("p (a j) -> p a j", j=NEXP),
                            in_=gate_wT_d.rearrange("(a p) j -> p a j", p=128))
        env.inc(last, "LD", 16)
    env.phase([("sync", p_gw)])

    for ps in range(L // PSEG):
        t0 = ps * PSEG
        emit_ln(env, cst, xfull, xlnf, scratch, seg_w=PSEG, src_stride=L, src_off=t0)

        def p_cp(v, ps=ps):
            for ct in range(NCT):
                last = v.tensor_copy(xlnbf[:, ct * PSEG:(ct + 1) * PSEG],
                                     xlnf[:, ct * PSEG:(ct + 1) * PSEG])
            env.inc(last, "V")
        env.phase([("vector", p_cp)])

        def p_st(sy, t0=t0):
            dst = xln_dram.rearrange("(a p) t -> p a t", p=128)[:, :, t0:t0 + PSEG]
            last = sy.dma_start(out=dst, in_=xlnbf[:].rearrange("p (a t) -> p a t", t=PSEG))
            env.inc(last, "ST", 16)
        env.phase([("sync", p_st)])

        def p_rt(t, ps=ps):
            for ct in range(NCT):
                last = t.matmul(pd[:, 0:NEXP], xlnf[:, ct * PSEG:(ct + 1) * PSEG],
                                gwT[:, ct * NEXP:(ct + 1) * NEXP],
                                start=(ct == 0), stop=(ct == NCT - 1))
            env.inc(last, "T")
        env.phase([("tensor", p_rt)])

        s_base = env.cnt["S"]
        v_base = env.cnt["V"]

        def p_rm(v, ps=ps, s_base=s_base):
            V = env.sems["V"]

            def stp(i):
                env.inc(i, "V"); v.wait_ge(V, env.cnt["V"])
            stp(v.tensor_copy(lgt[:], pd[:, 0:NEXP]))
            stp(v.tensor_reduce(rmax[:], lgt[:], AX.X, ALU.max))
            stp(v.tensor_scalar(rnegm[:], rmax[:], -1.0, None, ALU.mult))
            v.wait_ge(env.sems["S"], s_base + 1)
            stp(v.tensor_reduce(rsum[:], rex[:], AX.X, ALU.add))
            stp(v.reciprocal(rrs[:], rsum[:]))
            mb = bass.AP(rmax[:].tensor, rmax[:].offset, [rmax[:].ap[0], [0, NEXP]])
            stp(v.tensor_tensor(req1[:], lgt[:], mb, ALU.is_equal))
            stp(v.scalar_tensor_tensor(rtmp[:], req1[:], -1e30, lgt[:], ALU.mult, ALU.add))
            stp(v.tensor_reduce(rm2[:], rtmp[:], AX.X, ALU.max))
            v.wait_ge(env.sems["S"], s_base + 2)
            stp(v.tensor_tensor(rs2[:], re2x[:], rrs[:], ALU.mult))
            m2b = bass.AP(rm2[:].tensor, rm2[:].offset, [rm2[:].ap[0], [0, NEXP]])
            stp(v.tensor_tensor(req2[:], rtmp[:], m2b, ALU.is_equal))
            stp(v.tensor_scalar(rt1[:], req1[:], rrs[:], None, ALU.mult))
            stp(v.scalar_tensor_tensor(mrowtok[:, ps * NEXP:(ps + 1) * NEXP],
                                       req2[:], rs2[:], rt1[:], ALU.mult, ALU.add))

        def p_rs(s, ps=ps, v_base=v_base):
            s.wait_ge(env.sems["V"], v_base + 3)
            last = s.activation(rex[:], lgt[:], AF.Exp, bias=rnegm[:])
            env.inc(last, "S")
            s.wait_ge(env.sems["V"], v_base + 8)
            last = s.activation(re2x[:], rm2[:], AF.Exp, bias=rnegm[:])
            env.inc(last, "S")
        env.phase([("vector", p_rm), ("scalar", p_rs)])

    base_t = env.cnt["T"]
    base_v = env.cnt["V"]

    def p_mt(t):
        for g in range(4):
            if g >= 1:
                t.wait_ge(env.sems["V"], base_v + g)
            for k in range(4):
                ps = g * 4 + k
                last = t.transpose(pd[0:8, k * 128:(k + 1) * 128]
                                   if False else pd[:, k * 128:(k + 1) * 128][0:8, :],
                                   mrowtok[:, ps * NEXP:(ps + 1) * NEXP], cst["identf"][:])
            env.inc(last, "T")

    def p_mv(v):
        for g in range(4):
            v.wait_ge(env.sems["T"], base_t + g + 1)
            for k in range(4):
                ps = g * 4 + k
                last = v.tensor_copy(mtr[:, ps * 128:(ps + 1) * 128],
                                     pd[:, k * 128:(k + 1) * 128][0:8, :])
            env.inc(last, "V")
    env.phase([("tensor", p_mt), ("vector", p_mv)])

    for e in range(NEXP):
        def p_wl(gp, e=e):
            r = w1_d[e].rearrange("(a p) j -> p a j", p=128)
            last = gp.dma_start(out=w1buf[:].rearrange("p (a j) -> p a j", j=HID), in_=r)
            env.inc(last, "LG", 16)
        env.phase([("gpsimd", p_wl)])

        for sg in range(L // MSEG):
            t0 = sg * MSEG

            base_t = env.cnt["T"]
            base_ld0 = env.cnt["LD"]

            def p_lx(sy, t0=t0):
                srcx = xln_dram.rearrange("(a p) t -> p a t", p=128)[:, :, t0:t0 + MSEG]
                last = sy.dma_start(out=xeseg[:].rearrange("p (a t) -> p a t", t=MSEG), in_=srcx)
                env.inc(last, "LD", 16)

            def p_bc(t, e=e, t0=t0):
                last = t.matmul(pg[:, 0:MSEG], cst["esel8"][:, e * 128:(e + 1) * 128],
                                mtr[0:8, t0:t0 + MSEG], start=True, stop=True)
                env.inc(last, "T")

            def p_bv(v, e=e, base_ld0=base_ld0):
                v.wait_ge(env.sems["T"], base_t + 1)
                inst = v.tensor_copy(mbc[:], pg[:, 0:MSEG])
                env.inc(inst, "V")
                v.wait_ge(env.sems["V"], env.cnt["V"])
                v.wait_ge(env.sems["LD"], base_ld0 + 16)
                for ct in range(NCT):
                    sl = slice(ct * MSEG, (ct + 1) * MSEG)
                    last = v.tensor_tensor(xeseg[:, sl], mbc[:], xeseg[:, sl], ALU.mult)
                env.inc(last, "V")
            env.phase([("sync", p_lx), ("tensor", p_bc), ("vector", p_bv)])

            base_t = env.cnt["T"]
            base_s = env.cnt["S"]
            base_v = env.cnt["V"]
            base_lg = env.cnt["LG"]

            def hid_ld(gp, e=e):
                for nt in range(NHT):
                    if nt >= 2:
                        gp.wait_ge(env.sems["T"], base_t + nt - 1)
                    src_ = w3q_dram[(e * NHT + nt) * 128:(e * NHT + nt + 1) * 128, :]
                    last = gp.dma_start(out=w3k[nt % 2][:], in_=src_)
                    env.inc(last, "LG", 16)

            pgr = [pg, pd[:, 0:512], pd[:, 1024:1536]]
            pur = [pu, pd[:, 512:1024], pd[:, 1536:2048]]

            def hid_t(t):
                for nt in range(NHT):
                    pga = pgr[nt % 3]
                    pua = pur[nt % 3]
                    if nt >= 3:
                        t.wait_ge(env.sems["V"], base_v + nt - 2)
                    t.wait_ge(env.sems["LG"], base_lg + 16 * (nt + 1))
                    for kc in range(NCT):
                        t.matmul(pga[:, 0:512],
                                 w1buf[:, kc * HID + nt * 128: kc * HID + (nt + 1) * 128],
                                 xeseg[:, kc * MSEG:(kc + 1) * MSEG],
                                 start=(kc == 0), stop=(kc == NCT - 1))
                    for kc in range(NCT):
                        last = t.matmul(pua[:, 0:512],
                                        w3k[nt % 2][:, kc * 128:(kc + 1) * 128],
                                        xeseg[:, kc * MSEG:(kc + 1) * MSEG],
                                        start=(kc == 0), stop=(kc == NCT - 1))
                    env.inc(last, "T")

            def hid_s(s):
                for nt in range(NHT):
                    if nt >= 3:
                        s.wait_ge(env.sems["V"], base_v + nt - 2)
                    s.wait_ge(env.sems["T"], base_t + nt + 1)
                    last = s.activation(sgt[:, (nt % 3) * MSEG:(nt % 3) * MSEG + MSEG],
                                        pgr[nt % 3][:, 0:MSEG], AF.Silu)
                    env.inc(last, "S")

            def hid_v(v):
                for nt in range(NHT):
                    v.wait_ge(env.sems["S"], base_s + nt + 1)
                    last = v.tensor_tensor(hbuf[:, nt * MSEG:(nt + 1) * MSEG],
                                           sgt[:, (nt % 3) * MSEG:(nt % 3) * MSEG + MSEG],
                                           pur[nt % 3][:, 0:MSEG], ALU.mult)
                    env.inc(last, "V")
            env.phase([("gpsimd", hid_ld), ("tensor", hid_t), ("scalar", hid_s), ("vector", hid_v)])

            base_t = env.cnt["T"]
            base_v = env.cnt["V"]
            base_ld = env.cnt["LD"]

            def down_ld(sy, e=e):
                for jp in range(2):
                    bt = base_t + jp * (NHT + 2)
                    for nt in range(NHT):
                        if jp == 1 and nt == 0:
                            sy.wait_ge(env.sems["T"], base_t + NHT)
                        if nt >= 2:
                            sy.wait_ge(env.sems["T"], bt + nt - 1)
                        src_ = w2q_dram[((e * 2 + jp) * NHT + nt) * 128:
                                        ((e * 2 + jp) * NHT + nt + 1) * 128, :]
                        last = sy.dma_start(out=wdk[nt % 2][:], in_=src_)
                        env.inc(last, "LD", 16)

            def down_t(t, t0=t0):
                for jp in range(2):
                    bt = base_t + jp * (NHT + 2)
                    bl = base_ld + jp * 16 * NHT
                    if jp == 1:
                        t.wait_ge(env.sems["V"], base_v + 1)
                    for nt in range(NHT):
                        t.wait_ge(env.sems["LD"], bl + 16 * (nt + 1))
                        for tsub in range(MNTC):
                            last = t.matmul(
                                pd[:, tsub * 512:(tsub + 1) * 512],
                                hbuf[:, nt * MSEG + tsub * 128: nt * MSEG + (tsub + 1) * 128],
                                wdk[nt % 2][:],
                                start=(nt == 0), stop=(nt == NHT - 1))
                        env.inc(last, "T")
                    n = 0
                    for tsub in range(MNTC):
                        for ct in range(4):
                            if n % 8 == 0:
                                t.wait_ge(env.sems["V"], base_v + jp * 3 + 1 + (1 if n >= 8 else 0))
                            last = t.transpose(pT[:, (n % 8) * 128:(n % 8 + 1) * 128],
                                               dsbb[:, tsub * 512 + ct * 128: tsub * 512 + (ct + 1) * 128],
                                               cst["ident"][:])
                            n += 1
                            if n % 8 == 0:
                                env.inc(last, "T")

            def down_v(v, t0=t0):
                for jp in range(2):
                    bt = base_t + jp * (NHT + 2)
                    v.wait_ge(env.sems["T"], bt + NHT)
                    inst = v.tensor_copy(dsbb[:], pd[:, 0:MNTC * 512])
                    env.inc(inst, "V")
                    n = 0
                    for tsub in range(MNTC):
                        for ct in range(4):
                            if n % 8 == 0:
                                v.wait_ge(env.sems["T"], bt + NHT + n // 8 + 1)
                            ctg = jp * 4 + ct
                            sl = slice(ctg * L + t0 + tsub * 128, ctg * L + t0 + (tsub + 1) * 128)
                            last = v.tensor_tensor(xfull[:, sl], pT[:, (n % 8) * 128:(n % 8 + 1) * 128],
                                                   xfull[:, sl], ALU.add)
                            n += 1
                            if n % 8 == 0:
                                env.inc(last, "V")
            env.phase([("sync", down_ld), ("tensor", down_t), ("vector", down_v)])

def build_moe():
    nc = bass.Bass("TRN2", target_bir_lowering=False, detect_race_conditions=False)
    xin = nc.dram_tensor("xT", [DM, L], F32, kind="ExternalInput").ap()
    gate_wT_d = nc.dram_tensor("gate_wT", [DM, NEXP], F32, kind="ExternalInput").ap()
    w1_d = nc.dram_tensor("w1", [NEXP, DM, HID], F32, kind="ExternalInput").ap()
    w3_d = nc.dram_tensor("w3", [NEXP, DM, HID], F32, kind="ExternalInput").ap()
    w2_d = nc.dram_tensor("w2", [NEXP, HID, DM], F32, kind="ExternalInput").ap()
    MOE_CONSTS = [c for c in CONST_SPECS if c[0] in ("ones128", "ones128f", "ident", "identf", "esel8")]
    cst_d = {nm: nc.dram_tensor(nm, shp, dt, kind="ExternalInput").ap()
             for nm, shp, dt in MOE_CONSTS}
    xln_dram = nc.dram_tensor("m_xln_scratch", [DM, L], BF16, kind="Internal").ap()
    w2q_dram = nc.dram_tensor("m_w2q_scratch", [NEXP * 2 * NHT * 128, 512], BF16, kind="Internal").ap()
    w3q_dram = nc.dram_tensor("m_w3q_scratch", [NEXP * NHT * 128, NCT * 128], BF16, kind="Internal").ap()
    xout = nc.dram_tensor("xoutT", [DM, L], F32, kind="ExternalOutput").ap()

    with Env(nc) as env:
        cst = {nm: env.sb(nm + "_sb", shp, dt) for nm, shp, dt in MOE_CONSTS}
        for nm, val in (("epsc", EPS), ("neghc", -0.5)):
            cst[nm] = env.sb(nm, [128, 1], F32)
            cst[nm + "_val"] = val

        def p0_v(v):
            for nm in ("epsc", "neghc"):
                last = v.memset(cst[nm][:], cst[nm + "_val"])
            env.inc(last, "V")

        xfull = env.sb("m_xfull", [128, NCT * L], F32)

        def p0_sy(sy):
            for nm, shp, dt in MOE_CONSTS:
                sy.dma_start(out=cst[nm][:], in_=cst_d[nm]).then_inc(env.sems["LD"], 16)
                env.cnt["LD"] += 16
            last = sy.dma_start(out=xfull[:].rearrange("p (a t) -> p a t", t=L),
                                in_=xin.rearrange("(a p) t -> p a t", p=128))
            env.inc(last, "LD", 16)
        env.phase([("sync", p0_sy), ("vector", p0_v)])
        with env.scope():
            emit_moe(env, cst, xfull, xln_dram, w2q_dram, w3q_dram, gate_wT_d, w1_d, w3_d, w2_d)

        def pf_sy(sy):
            last = sy.dma_start(out=xout.rearrange("(a p) t -> p a t", p=128),
                                in_=xfull[:].rearrange("p (a t) -> p a t", t=L))
            env.inc(last, "ST", 16)
        env.phase([("sync", pf_sy)])

        @env.block.sync
        def _(sy):
            sy.wait_ge(env.sems["ST"], env.cnt["ST"])
    return nc


def run_moe(x_full, gate_w, w1, w2, w3):
    consts = host_consts()
    nc = build_moe()
    in_maps = []
    B = x_full.shape[0]
    for b in range(B):
        m = {"xT": np.ascontiguousarray(x_full[b].T),
             "gate_wT": np.ascontiguousarray(gate_w.T),
             "w1": w1, "w3": w3, "w2": w2}
        m.update({k: consts[k] for k in ("ones128", "ones128f", "ident", "identf", "esel8")})
        in_maps.append(m)
    res = run_bass_kernel_spmd(nc, in_maps, core_ids=list(range(B)))
    return np.stack([np.ascontiguousarray(res.results[b]["xoutT"]).T for b in range(B)])




def build_fused():
    nc = bass.Bass("TRN2", target_bir_lowering=False, detect_race_conditions=False)
    xin = nc.dram_tensor("xT", [DM, L], F32, kind="ExternalInput").ap()
    gdr = {}
    for nm, shp in (("wq", [DM, INNER]), ("wk", [DM, INNER]), ("wv", [DM, INNER]),
                    ("wg", [DM, INNER]), ("wo", [INNER, DM])):
        gdr[nm] = nc.dram_tensor(nm, shp, F32, kind="ExternalInput").ap()
    bg = nc.dram_tensor("bg", [1, INNER], F32, kind="ExternalInput").ap()
    gate_wT_d = nc.dram_tensor("gate_wT", [DM, NEXP], F32, kind="ExternalInput").ap()
    w1_d = nc.dram_tensor("w1", [NEXP, DM, HID], F32, kind="ExternalInput").ap()
    w3_d = nc.dram_tensor("w3", [NEXP, DM, HID], F32, kind="ExternalInput").ap()
    w2_d = nc.dram_tensor("w2", [NEXP, HID, DM], F32, kind="ExternalInput").ap()
    w_in_d = nc.dram_tensor("w_in", [DM, DI], F32, kind="ExternalInput").ap()
    w_gate_d = nc.dram_tensor("w_gate", [DM, DI], F32, kind="ExternalInput").ap()
    w_out_d = nc.dram_tensor("w_out", [DI, DM], F32, kind="ExternalInput").ap()
    b_in_d = nc.dram_tensor("b_in", [128, NTI], F32, kind="ExternalInput").ap()
    b_gate_d = nc.dram_tensor("b_gate", [128, NTI], F32, kind="ExternalInput").ap()
    b_out_d = nc.dram_tensor("b_out", [128, NCT], F32, kind="ExternalInput").ap()
    A_d = nc.dram_tensor("A", [DI, 256], F32, kind="ExternalInput").ap()
    B_d = nc.dram_tensor("B_ssm", [DI, DS], F32, kind="ExternalInput").ap()
    C_d = nc.dram_tensor("C_ssm", [DI, DS], F32, kind="ExternalInput").ap()
    fwg_d = nc.dram_tensor("f_w_gate", [HID, DM], F32, kind="ExternalInput").ap()
    fwu_d = nc.dram_tensor("f_w_up", [HID, DM], F32, kind="ExternalInput").ap()
    fwd_d = nc.dram_tensor("f_w_down", [DM, HID], F32, kind="ExternalInput").ap()
    cst_d = {nm: nc.dram_tensor(nm, shp, dt, kind="ExternalInput").ap()
             for nm, shp, dt in CONST_SPECS}
    xln_dram = nc.dram_tensor("m_xln_scratch", [DM, L], BF16, kind="Internal").ap()
    w2q_dram = nc.dram_tensor("m_w2q_scratch", [NEXP * 2 * NHT * 128, 512], BF16, kind="Internal").ap()
    w3q_dram = nc.dram_tensor("m_w3q_scratch", [NEXP * NHT * 128, NCT * 128], BF16, kind="Internal").ap()
    wdq_dram = nc.dram_tensor("f_wdq_scratch", [NHT * 128, DM], BF16, kind="Internal").ap()
    xout = nc.dram_tensor("xoutT", [DM, L], F32, kind="ExternalOutput").ap()

    with Env(nc) as env:
        cst = {nm: env.sb(nm + "_sb", shp, dt) for nm, shp, dt in CONST_SPECS}
        for nm, val in (("epsc", EPS), ("onec", 1.0), ("neg1c", -1.0), ("neghc", -0.5)):
            cst[nm] = env.sb(nm, [128, 1], F32)
            cst[nm + "_val"] = val
        xfull = env.sb("xfull", [128, NCT * L], F32)

        def p0_v(v):
            for nm in ("epsc", "onec", "neg1c", "neghc"):
                last = v.memset(cst[nm][:], cst[nm + "_val"])
            env.inc(last, "V")

        def p0_sy(sy):
            for nm, shp, dt in CONST_SPECS:
                sy.dma_start(out=cst[nm][:], in_=cst_d[nm]).then_inc(env.sems["LD"], 16)
                env.cnt["LD"] += 16
            last = sy.dma_start(out=xfull[:].rearrange("p (a t) -> p a t", t=L),
                                in_=xin.rearrange("(a p) t -> p a t", p=128))
            env.inc(last, "LD", 16)
        env.phase([("sync", p0_sy), ("vector", p0_v)])

        with env.scope():
            emit_gla(env, cst, xfull, gdr, bg)
        with env.scope():
            emit_moe(env, cst, xfull, xln_dram, w2q_dram, w3q_dram, gate_wT_d, w1_d, w3_d, w2_d)
        with env.scope():
            emit_ssm(env, cst, xfull, w_in_d, w_gate_d, w_out_d,
                     b_in_d, b_gate_d, b_out_d, A_d, B_d, C_d)
        with env.scope():
            emit_fp4ffn(env, cst, xfull, wdq_dram, fwg_d, fwu_d, fwd_d)

        def pf_sy(sy):
            last = sy.dma_start(out=xout.rearrange("(a p) t -> p a t", p=128),
                                in_=xfull[:].rearrange("p (a t) -> p a t", t=L))
            env.inc(last, "ST", 16)
        env.phase([("sync", pf_sy)])

        @env.block.sync
        def _(sy):
            sy.wait_ge(env.sems["ST"], env.cnt["ST"])
    return nc


def kernel(**inputs):
    ins = {k: np.asarray(v) for k, v in inputs.items()}
    x = np.asarray(ins["L0_attn_x"], np.float32)
    B = x.shape[0]
    cores = list(range(B))
    consts = host_consts()
    nc = build_fused()
    im = []
    for b in range(B):
        m = {"xT": np.ascontiguousarray(x[b].T),
             "wq": ins["L0_attn_Wq"], "wk": ins["L0_attn_Wk"],
             "wv": ins["L0_attn_Wv"], "wg": ins["L0_attn_Wg"],
             "bg": np.ascontiguousarray(np.asarray(ins["L0_attn_bg"]).reshape(1, -1)),
             "wo": permute_wo_rows(np.asarray(ins["L0_attn_Wo"])),
             "gate_wT": np.ascontiguousarray(np.asarray(ins["L0_ffn_gate_w"]).T),
             "w1": ins["L0_ffn_w1"], "w3": ins["L0_ffn_w3"], "w2": ins["L0_ffn_w2"],
             "w_in": ins["L1_attn_w_in"], "w_gate": ins["L1_attn_w_gate"],
             "w_out": ins["L1_attn_w_out"],
             "b_in": np.ascontiguousarray(np.asarray(ins["L1_attn_b_in"]).reshape(NTI, 128).T),
             "b_gate": np.ascontiguousarray(np.asarray(ins["L1_attn_b_gate"]).reshape(NTI, 128).T),
             "b_out": np.ascontiguousarray(np.asarray(ins["L1_attn_b_out"]).reshape(NCT, 128).T),
             "A": np.ascontiguousarray(np.asarray(ins["L1_attn_A"]).transpose(0, 2, 1).reshape(DI, 256)),
             "B_ssm": ins["L1_attn_B_ssm"], "C_ssm": ins["L1_attn_C_ssm"],
             "f_w_gate": ins["L1_ffn_w_gate"], "f_w_up": ins["L1_ffn_w_up"],
             "f_w_down": ins["L1_ffn_w_down"]}
        m.update({k: consts[k] for k, _s, _d in CONST_SPECS})
        im.append(m)
    res = run_bass_kernel_spmd(nc, im, core_ids=cores)
    out = np.stack([np.ascontiguousarray(res.results[b]["xoutT"]).T for b in range(B)])
    return out.astype(np.float32)


# version 08 (final)
# speedup vs baseline: 1.1742x; 1.0045x over previous
import numpy as np
from contextlib import ExitStack, contextmanager
import concourse.bass as bass
import concourse.mybir as mybir
from concourse.bass_utils import run_bass_kernel_spmd
from concourse.alu_op_type import AluOpType as ALU

F32 = mybir.dt.float32
BF16 = mybir.dt.bfloat16
U32 = mybir.dt.uint32
AF = mybir.ActivationFunctionType
AX = mybir.AxisListType

L = 2048
DM = 1024
NCT = DM // 128
SEG = 256
NSEG = L // SEG
GNTT = SEG // 128
C = 32
H, D = 16, 64
INNER = 1024
EPS = 1e-5
STOP_AFTER = None
SSM_DEBUG = False
P4_MODE = "full"

ENG_SEM = {"tensor": "T", "vector": "V", "scalar": "S", "gpsimd": "G", "sync": "Y"}
ALL_SEMS = ("T", "V", "S", "G", "Y", "LD", "LG", "ST")



def host_consts():
    import ml_dtypes
    BF = ml_dtypes.bfloat16
    c = {}
    c["ident"] = np.eye(128, dtype=np.float32).astype(BF)
    c["identf"] = np.eye(128, dtype=np.float32)
    c["ones128"] = np.ones((128, 128), np.float32).astype(BF)
    c["ones128f"] = np.ones((128, 128), np.float32)
    m = (np.arange(32)[:, None] <= np.arange(32)[None, :]).astype(np.float32)
    c["maskA"] = np.tile(m, (4, 16)).astype(BF)
    bt = np.zeros((128, 128), np.float32)
    for b in range(4):
        bt[b * 32:(b + 1) * 32, b * 32:(b + 1) * 32] = -m
    c["btrilneg"] = bt
    e8 = np.zeros((8, 8 * 128), np.float32)
    for e in range(8):
        e8[e, e * 128:(e + 1) * 128] = 1.0
    c["esel8"] = e8.astype(BF)
    e4 = np.zeros((128, 256), np.float32)
    for cc in range(4):
        e4[32 * cc + 31, cc * 64:(cc + 1) * 64] = 1.0
    c["esel4"] = e4.astype(BF)
    return c


CONST_SPECS = [("ident", [128, 128], BF16), ("identf", [128, 128], F32),
               ("esel8", [8, 1024], BF16), ("ones128", [128, 128], BF16),
               ("ones128f", [128, 128], F32), ("maskA", [128, 512], BF16),
               ("btrilneg", [128, 128], F32), ("esel4", [128, 256], BF16)]



class Env:
    def __init__(self, nc):
        self.nc = nc
        self.ctx = ExitStack()
        self.block = None
        self.sems = {}
        self.cnt = {}

    def __enter__(self):
        self.ctx.__enter__()
        self._scopes = []
        for name in ALL_SEMS:
            self.sems[name] = self.ctx.enter_context(self.nc.semaphore(name))
            self.cnt[name] = 0
        self.block = self.ctx.enter_context(self.nc.Block())
        return self

    @contextmanager
    def scope(self):
        st = ExitStack()
        self._scopes.append(st)
        try:
            yield
        finally:
            st.close()
            self._scopes.pop()

    def __exit__(self, *a):
        return self.ctx.__exit__(*a)

    def _actx(self):
        return self._scopes[-1] if getattr(self, "_scopes", None) else self.ctx

    def sb(self, name, shape, dtype=F32):
        return self._actx().enter_context(self.nc.sbuf_tensor(name, shape, dtype))

    def ps(self, name, shape, dtype=F32):
        return self._actx().enter_context(self.nc.psum_tensor(name, shape, dtype))

    def inc(self, inst, sem, n=1):
        inst.then_inc(self.sems[sem], n)
        self.cnt[sem] += n
        return inst

    def phase(self, parts):
        self.nphase = getattr(self, "nphase", 0) + 1
        if STOP_AFTER is not None and self.nphase > STOP_AFTER:
            return
        snap = {k: self.cnt[k] for k in ALL_SEMS}
        for name, fn in parts:
            dec = getattr(self.block, name)
            own = ENG_SEM[name]

            def body(eng, fn=fn, own=own):
                for k in ALL_SEMS:
                    if k != own and snap[k] > 0:
                        eng.wait_ge(self.sems[k], snap[k])
                before = sum(self.cnt.values())
                fn(eng)
                assert sum(self.cnt.values()) > before, "phase fn must inc a sem"
            dec(body)



def emit_ln(env, cst, xsrc, xln, scratch, nct=NCT, seg_w=SEG,
            src_stride=None, src_off=0, dst_stride=None, dst_off=0):
    ss = src_stride if src_stride is not None else seg_w
    ds = dst_stride if dst_stride is not None else seg_w
    sq, mu, var, rstd, lntmp = (scratch[k] for k in ("sqseg", "mu", "var", "rstd", "lntmp"))
    pS, pQ = scratch["pS"], scratch["pQ"]
    ones_sq = cst["ones128f"] if sq[:].dtype == F32 else cst["ones128"]
    bs = env.cnt["S"]
    bt = env.cnt["T"]
    bv = env.cnt["V"]

    def ssl(ct):
        return slice(ct * ss + src_off, ct * ss + src_off + seg_w)

    def f_s(s):
        for ct in range(nct):
            last = s.activation(sq[:, ct * seg_w:(ct + 1) * seg_w], xsrc[:, ssl(ct)], AF.Square)
        env.inc(last, "S")
        s.wait_ge(env.sems["V"], bv + 3)
        s.activation(lntmp[:], var[:], AF.Ln, bias=cst["epsc"][:])
        last = s.activation(rstd[:], lntmp[:], AF.Exp, scale=cst["neghc"][:])
        env.inc(last, "S")

    def f_t(t):
        for ct in range(nct):
            t.matmul(pS, cst["ones128f"][:], xsrc[:, ssl(ct)],
                     start=(ct == 0), stop=(ct == nct - 1))
        t.wait_ge(env.sems["S"], bs + 1)
        for ct in range(nct):
            last = t.matmul(pQ, ones_sq[:], sq[:, ct * seg_w:(ct + 1) * seg_w],
                            start=(ct == 0), stop=(ct == nct - 1))
        env.inc(last, "T")

    def f_v(v):
        v.wait_ge(env.sems["T"], bt + 1)
        inst = v.tensor_scalar(mu[:], pS, 1.0 / DM, None, ALU.mult)
        env.inc(inst, "V"); v.wait_ge(env.sems["V"], env.cnt["V"])
        inst = v.tensor_tensor(var[:], mu[:], mu[:], ALU.mult)
        env.inc(inst, "V"); v.wait_ge(env.sems["V"], env.cnt["V"])
        inst = v.scalar_tensor_tensor(var[:], pQ, 1.0 / DM, var[:], ALU.mult, ALU.subtract)
        env.inc(inst, "V")
        v.wait_ge(env.sems["S"], bs + 2)
        for ct in range(nct):
            v.tensor_tensor(lntmp[:], xsrc[:, ssl(ct)], mu[:], ALU.subtract)
            last = v.tensor_tensor(xln[:, ct * ds + dst_off: ct * ds + dst_off + seg_w],
                                   lntmp[:], rstd[:], ALU.mult)
        env.inc(last, "V")
    env.phase([("scalar", f_s), ("tensor", f_t), ("vector", f_v)])


def emit_proj_feat(env, dst, w_sb, src, pbank0, pbank1, njt=8, nct=NCT, seg_w=SEG,
                   evac="copy", evac_arg=None, dst_slicer=None, evac_custom=None):
    base_t = env.cnt["T"]
    jw = njt * 128
    ev_eng = "vector" if evac_custom is None else evac_custom[0]
    ev_sem = ENG_SEM[ev_eng]
    base_e = env.cnt[ev_sem]

    def _hw(h):
        return bass.AP(h.tensor, h.offset, [h.ap[0], [1, seg_w]])

    def f_t(t):
        for jt in range(njt):
            half = _hw(pbank0 if jt % 2 == 0 else pbank1)
            if jt >= 2:
                t.wait_ge(env.sems[ev_sem], base_e + jt - 1)
            for ct in range(nct):
                last = t.matmul(half, w_sb[:, ct * jw + jt * 128: ct * jw + (jt + 1) * 128],
                                src[:, ct * seg_w:(ct + 1) * seg_w],
                                start=(ct == 0), stop=(ct == nct - 1))
            env.inc(last, "T")

    def f_e(eng):
        for jt in range(njt):
            half = _hw(pbank0 if jt % 2 == 0 else pbank1)
            eng.wait_ge(env.sems["T"], base_t + jt + 1)
            if evac_custom is not None:
                last = evac_custom[1](eng, jt, half)
            else:
                dsl = dst_slicer(jt) if dst_slicer else dst[:, jt * seg_w:(jt + 1) * seg_w]
                if evac == "copy":
                    last = eng.tensor_copy(dsl, half)
                else:
                    last = eng.tensor_tensor(dsl, half, evac_arg[:, jt * seg_w:(jt + 1) * seg_w], ALU.add)
            env.inc(last, ev_sem)
    env.phase([("tensor", f_t), (ev_eng, f_e)])


def emit_proj_tok(env, dst, w_sb, xln, pb0, pb1, bias_row=None, ones128=None,
                  ntc=None, nct=NCT, seg_w=SEG, jw=1024):
    if ntc is None:
        ntc = seg_w // 128
    """Token-major projection: dst (128, ntc*jw) bf16, dst[t, j] = sum_c x[c,t] W[c,j]."""
    base_t = env.cnt["T"]
    base_v = env.cnt["V"]

    def f_t(t):
        for tc in range(ntc):
            pb = pb0 if tc % 2 == 0 else pb1
            if tc >= 2:
                t.wait_ge(env.sems["V"], base_v + tc - 1)
            nhalf = jw // 512
            for ct in range(nct):
                stat = xln[:, ct * seg_w + tc * 128: ct * seg_w + (tc + 1) * 128]
                mv = w_sb[:, ct * jw:(ct + 1) * jw]
                for hf in range(nhalf):
                    last = t.matmul(pb[:, hf * 512:(hf + 1) * 512], stat, mv[:, hf * 512:(hf + 1) * 512],
                                    start=(ct == 0), stop=(ct == nct - 1 and bias_row is None))
            if bias_row is not None:
                oc = ones128[0:1, 0:128]
                for hf in range(nhalf):
                    last = t.matmul(pb[:, hf * 512:(hf + 1) * 512], oc, bias_row[0:1, hf * 512:(hf + 1) * 512],
                                    start=False, stop=True)
            env.inc(last, "T")

    def f_v(v):
        for tc in range(ntc):
            pb = pb0 if tc % 2 == 0 else pb1
            v.wait_ge(env.sems["T"], base_t + tc + 1)
            last = v.tensor_copy(dst[:, tc * jw:(tc + 1) * jw], pb[:, 0:jw])
            env.inc(last, "V")
    env.phase([("tensor", f_t), ("vector", f_v)])



def emit_gla(env, cst, xfull, dr, bg_d):
    nc = env.nc

    w_sb = {nm: env.sb("g_" + nm + "_wsb", [128, NCT * 1024], BF16)
            for nm in ("wq", "wk", "wv", "wg", "wo")}
    bgrow = env.sb("g_bgrow", [1, INNER], BF16)

    def p0_gp(gp):
        for nm in ("wq", "wk", "wv", "wg", "wo"):
            r = dr[nm].rearrange("(a p) j -> p a j", p=128)
            last = gp.dma_start(out=w_sb[nm][:].rearrange("p (a j) -> p a j", j=1024), in_=r)
            env.inc(last, "LG", 16)
        last = gp.dma_start(out=bgrow[:], in_=bg_d)
        env.inc(last, "LG", 16)
    env.phase([("gpsimd", p0_gp)])

    xln = env.sb("g_xln", [128, NCT * SEG], BF16)
    scratch = {}
    for nm in ("mu", "var", "rstd"):
        scratch[nm] = env.sb("g_" + nm, [128, SEG], F32)
    scratch["msq"] = scratch["var"]
    scratch["lntmp"] = scratch["var"]

    QT = env.sb("g_QT", [128, NCT * SEG], BF16)
    KT = env.sb("g_KT", [128, NCT * SEG], BF16)
    Ktok = env.sb("g_Ktok", [128, GNTT * 1024], BF16)
    Vtok = env.sb("g_Vtok", [128, GNTT * 1024], BF16)
    cum = env.sb("g_cum", [128, GNTT * 1024], BF16)
    lgn = env.sb("g_lgn", [128, 512], F32)
    Am = env.sb("g_Am", [128, 512], BF16)
    S = env.sb("g_S", [128, 512], F32)
    Sbf = env.sb("g_Sbf", [128, 512], BF16)
    otok = env.sb("g_otok", [128, 1024], BF16)
    oT = env.sb("g_oT", [128, NCT * SEG], BF16)
    scratch["sqseg"] = oT
    expneg = otok

    pB = env.ps("g_pB", [128, 1024], F32)
    pI = env.ps("g_pI", [128, 1024], F32)
    pU = env.ps("g_pU", [128, 512], F32)
    pC = env.ps("g_pC", [128, 512], F32)
    pT = env.ps("g_pT", [128, 1024], BF16)
    pb0, pb1 = pB, pI
    scratch["pS"] = pU[:, 0:SEG]
    scratch["pQ"] = pC[:, 0:SEG]
    tmpB = env.sb("g_tmpB", [128, 512], F32)
    etile = tmpB
    Stmp = tmpB

    def p0_v(v):
        v.memset(S[:], 0.0)
        last = v.memset(Sbf[:], 0.0)
        env.inc(last, "V")
    env.phase([("vector", p0_v)])

    for seg in range(NSEG):
        t0 = seg * SEG

        emit_ln(env, cst, xfull, xln, scratch, src_stride=L, src_off=t0)

        emit_proj_feat(env, QT, w_sb["wq"], xln, pb0[:, 0:512], pb0[:, 512:1024])
        emit_proj_feat(env, KT, w_sb["wk"], xln, pb0[:, 0:512], pb0[:, 512:1024])
        emit_proj_tok(env, Vtok, w_sb["wv"], xln, pb0, pb1)
        emit_proj_tok(env, Ktok, w_sb["wk"], xln, pb0, pb1)

        for tc in range(GNTT):
            base_t = env.cnt["T"]
            base_s = env.cnt["S"]

            def g_t(t, tc=tc, base_t=base_t, base_s=base_s):
                for ct in range(NCT):
                    stat = xln[:, ct * SEG + tc * 128: ct * SEG + (tc + 1) * 128]
                    mv = w_sb["wg"][:, ct * 1024:(ct + 1) * 1024]
                    t.matmul(pb0[:, 0:512], stat, mv[:, 0:512], start=(ct == 0), stop=False)
                    t.matmul(pb0[:, 512:1024], stat, mv[:, 512:1024], start=(ct == 0), stop=False)
                oc = cst["ones128"][0:1, 0:128]
                t.matmul(pb0[:, 0:512], oc, bgrow[0:1, 0:512], start=False, stop=True)
                last = t.matmul(pb0[:, 512:1024], oc, bgrow[0:1, 512:1024], start=False, stop=True)
                env.inc(last, "T")
                for hf in range(2):
                    t.wait_ge(env.sems["S"], base_s + 1 + hf)
                    last = t.matmul(pb1[:, hf * 512:(hf + 1) * 512], cst["btrilneg"][:],
                                    lgn[:], start=True, stop=True)
                    env.inc(last, "T")

            def g_s(s, tc=tc, base_t=base_t):
                s.wait_ge(env.sems["T"], base_t + 1)
                for hf in range(2):
                    if hf == 1:
                        s.wait_ge(env.sems["T"], base_t + 2)
                    hsl = slice(hf * 512, (hf + 1) * 512)
                    s.activation(etile[:], pb0[:, hsl], AF.Exp, scale=cst["neg1c"][:])
                    last = s.activation(lgn[:], etile[:], AF.Ln, bias=cst["onec"][:])
                    env.inc(last, "S")
                s.wait_ge(env.sems["T"], base_t + 3)
                s.activation(expneg[:], pb1[:], AF.Exp, scale=cst["neg1c"][:])
                last = s.activation(cum[:, tc * 1024:(tc + 1) * 1024], pb1[:], AF.Exp)
                env.inc(last, "S")

            def g_v(v, tc=tc, base_s=base_s):
                v.wait_ge(env.sems["S"], base_s + 3)
                sl = slice(tc * 1024, (tc + 1) * 1024)
                last = v.tensor_tensor(Vtok[:, sl], expneg[:], Vtok[:, sl], ALU.mult)
                env.inc(last, "V")
            env.phase([("tensor", g_t), ("scalar", g_s), ("vector", g_v)])

        base_t = env.cnt["T"]
        base_v = env.cnt["V"]
        base_s = env.cnt["S"]

        def p4_t(t):
            for tt in range(GNTT):
                if tt >= 1:
                    t.wait_ge(env.sems["V"], base_v + (tt - 1) * 8 + 5)
                for cc in range(4):
                    base = 32 * cc
                    for h in [hh for hh in range(H) if hh % 2 == 0] + \
                             [hh for hh in range(H) if hh % 2 == 1]:
                        jt = h // 2
                        b64 = 64 * (h % 2)
                        dstp = pU if h % 2 == 0 else pC
                        tsl = slice(jt * SEG + tt * 128 + cc * 32,
                                    jt * SEG + tt * 128 + cc * 32 + 32)
                        last = t.matmul(dstp[base:base + 32, 32 * jt:32 * jt + 32],
                                        KT[b64:b64 + 64, tsl], QT[b64:b64 + 64, tsl],
                                        start=True, stop=True, tile_position=(b64, base))
                env.inc(last, "T")
                for cc in range(4):
                    base = 32 * cc
                    t.wait_ge(env.sems["V"], base_v + tt * 8 + 1 + cc)
                    vtile = Vtok[:, tt * 1024:(tt + 1) * 1024]
                    ktile = Ktok[:, tt * 1024:(tt + 1) * 1024]
                    for h in [hh for hh in range(H) if hh % 2 == 0] + \
                             [hh for hh in range(H) if hh % 2 == 1]:
                        jt = h // 2
                        b64 = 64 * (h % 2)
                        ocol = 64 * jt + 512 * (h % 2)
                        tsl = slice(jt * SEG + tt * 128 + cc * 32,
                                    jt * SEG + tt * 128 + cc * 32 + 32)
                        t.matmul(pB[base:base + 32, ocol:ocol + 64],
                                 QT[b64:b64 + 64, tsl], Sbf[b64:b64 + 64, 64 * jt:64 * jt + 64],
                                 start=True, stop=True, tile_position=(b64, base))
                    for h in range(H):
                        jt = h // 2
                        acol = 32 * jt + 256 * (h % 2)
                        ocol = 64 * jt + 512 * (h % 2)
                        t.matmul(pI[base:base + 32, ocol:ocol + 64],
                                 Am[base:base + 32, acol:acol + 32],
                                 vtile[base:base + 32, 64 * h:64 * h + 64],
                                 start=True, stop=True, tile_position=(base, base))
                    for h in range(H):
                        jt = h // 2
                        b64 = 64 * (h % 2)
                        last = t.matmul(pU[b64:b64 + 64, 64 * jt:64 * jt + 64],
                                        ktile[base:base + 32, 64 * h:64 * h + 64],
                                        vtile[base:base + 32, 64 * h:64 * h + 64],
                                        start=True, stop=True, tile_position=(base, b64))
                    ctile = cum[:, tt * 1024:(tt + 1) * 1024]
                    ceven = ctile[:, 0:1024]
                    ap_ev = bass.AP(ceven.tensor, ceven.offset, [ceven.ap[0], [128, 8], [1, 64]])
                    codd = ctile[:, 64:1024]
                    ap_od = bass.AP(codd.tensor, codd.offset, [codd.ap[0], [128, 8], [1, 64]])
                    t.matmul(pC[0:64, 0:512], esel_slice(cst, cc), ap_ev,
                             start=True, stop=True, tile_position=(0, 0))
                    last = t.matmul(pC[64:128, 0:512], esel_slice(cst, cc), ap_od,
                                    start=True, stop=True, tile_position=(0, 64))
                    env.inc(last, "T")
                t.wait_ge(env.sems["V"], base_v + tt * 8 + 7)
                for k in range(8):
                    last = t.transpose(pT[:, k * 128:(k + 1) * 128],
                                       otok[:, k * 128:(k + 1) * 128], cst["ident"][:])
                env.inc(last, "T")

        def p4_s(s):
            for tt in range(GNTT):
                s.wait_ge(env.sems["T"], base_t + tt * 6 + 5)
                s.wait_ge(env.sems["V"], base_v + tt * 8 + 5)
                last = s.activation(tmpB[:], pB[:, 0:512], AF.Copy)
                env.inc(last, "S")
                s.wait_ge(env.sems["V"], base_v + tt * 8 + 6)
                last = s.activation(tmpB[:], pB[:, 512:1024], AF.Copy)
                env.inc(last, "S")

        def p4_v(v):
            tv = base_t
            for tt in range(GNTT):
                v.wait_ge(env.sems["T"], tv + 1); tv += 1
                v.tensor_tensor(Am[:, 0:256], pU[:, 0:256], cst["maskA"][:, 0:256], ALU.mult)
                last = v.tensor_tensor(Am[:, 256:512], pC[:, 0:256], cst["maskA"][:, 0:256], ALU.mult)
                env.inc(last, "V")
                for cc in range(4):
                    v.wait_ge(env.sems["T"], tv + 1); tv += 1
                    v.tensor_tensor(Stmp[:], S[:], pU[:], ALU.add)
                    v.tensor_tensor(S[:], Stmp[:], pC[:], ALU.mult)
                    last = v.tensor_copy(Sbf[:], S[:])
                    env.inc(last, "V")
                for hf in range(2):
                    v.wait_ge(env.sems["S"], base_s + 2 * tt + 1 + hf)
                    hsl = slice(hf * 512, (hf + 1) * 512)
                    v.tensor_tensor(tmpB[:], pI[:, hsl], tmpB[:], ALU.add)
                    ctile = cum[:, tt * 1024:(tt + 1) * 1024]
                    cperm = bass.AP(ctile.tensor, ctile.offset + 64 * hf,
                                    [ctile.ap[0], [128, 8], [1, 64]])
                    otok3 = otok[:, hsl].rearrange("p (b c) -> p b c", c=64)
                    last = v.tensor_tensor(otok3, tmpB[:].rearrange("p (b c) -> p b c", c=64),
                                           cperm, ALU.mult)
                    env.inc(last, "V")
                v.wait_ge(env.sems["T"], tv + 1); tv += 1
                ot3 = oT[:].rearrange("p (a t) -> p a t", t=SEG)[:, :, tt * 128:(tt + 1) * 128]
                last = v.tensor_copy(ot3, pT[:].rearrange("p (a t) -> p a t", t=128))
                env.inc(last, "V")
        env.phase([("tensor", p4_t), ("scalar", p4_s), ("vector", p4_v)])

        def evac_res(eng, jt, half, t0=t0):
            dsl = xfull[:, jt * L + t0: jt * L + t0 + SEG]
            return eng.tensor_tensor(dsl, half, dsl, ALU.add)
        emit_proj_feat(env, None, w_sb["wo"], oT, pb0[:, 0:512], pb0[:, 512:1024],
                       evac_custom=("vector", evac_res))


def esel_slice(cst, cc):
    return cst["esel4"][:, cc * 64:(cc + 1) * 64]


def permute_wo_rows(wo):
    w = wo.reshape(2 * 8, 64, wo.shape[1])
    perm = [2 * jt + p for p in range(2) for jt in range(8)]
    return np.ascontiguousarray(w[perm].reshape(wo.shape))



def build_gla():
    nc = bass.Bass("TRN2", target_bir_lowering=False, detect_race_conditions=False)
    xin = nc.dram_tensor("xT", [DM, L], F32, kind="ExternalInput").ap()
    dr = {}
    for nm, shp in (("wq", [DM, INNER]), ("wk", [DM, INNER]), ("wv", [DM, INNER]),
                    ("wg", [DM, INNER]), ("wo", [INNER, DM])):
        dr[nm] = nc.dram_tensor(nm, shp, F32, kind="ExternalInput").ap()
    bg = nc.dram_tensor("bg", [1, INNER], F32, kind="ExternalInput").ap()
    GLA_CONSTS = [c for c in CONST_SPECS if c[0] != "esel8" and c[0] != "identf"]
    cst_d = {nm: nc.dram_tensor(nm, shp, dt, kind="ExternalInput").ap()
             for nm, shp, dt in GLA_CONSTS}
    xout = nc.dram_tensor("xoutT", [DM, L], F32, kind="ExternalOutput").ap()

    with Env(nc) as env:
        cst = {nm: env.sb(nm + "_sb", shp, dt) for nm, shp, dt in GLA_CONSTS}
        for nm, val in (("epsc", EPS), ("onec", 1.0), ("neg1c", -1.0), ("neghc", -0.5)):
            cst[nm] = env.sb(nm, [128, 1], F32)
            cst[nm + "_val"] = val
        xfull = env.sb("xfull", [128, NCT * L], F32)

        def p0_v(v):
            for nm in ("epsc", "onec", "neg1c", "neghc"):
                last = v.memset(cst[nm][:], cst[nm + "_val"])
            env.inc(last, "V")

        def p0_sy(sy):
            for nm, shp, dt in GLA_CONSTS:
                sy.dma_start(out=cst[nm][:], in_=cst_d[nm]).then_inc(env.sems["LD"], 16)
                env.cnt["LD"] += 16
            last = sy.dma_start(out=xfull[:].rearrange("p (a t) -> p a t", t=L),
                                in_=xin.rearrange("(a p) t -> p a t", p=128))
            env.inc(last, "LD", 16)
        env.phase([("sync", p0_sy), ("vector", p0_v)])

        with env.scope():
            emit_gla(env, cst, xfull, dr, bg)

        def pf_sy(sy):
            last = sy.dma_start(out=xout.rearrange("(a p) t -> p a t", p=128),
                                in_=xfull[:].rearrange("p (a t) -> p a t", t=L))
            env.inc(last, "ST", 16)
        env.phase([("sync", pf_sy)])

        @env.block.sync
        def _(sy):
            sy.wait_ge(env.sems["ST"], env.cnt["ST"])
    return nc


def run_gla(x_full, Wq, Wk, Wv, Wg, bgv, Wo):
    consts = host_consts()
    nc = build_gla()
    in_maps = []
    B = x_full.shape[0]
    for b in range(B):
        m = {"xT": np.ascontiguousarray(x_full[b].T),
             "wq": Wq, "wk": Wk, "wv": Wv, "wg": Wg,
             "bg": np.ascontiguousarray(bgv.reshape(1, -1)), "wo": permute_wo_rows(Wo)}
        m.update(consts)
        in_maps.append(m)
    res = run_bass_kernel_spmd(nc, in_maps, core_ids=list(range(B)))
    return np.stack([np.ascontiguousarray(res.results[b]["xoutT"]).T for b in range(B)])





DI = 2048
DS = 16
NTI = DI // 128
T0 = 32
SSEG = 512
XPW = SSEG + 32


def emit_ssm(env, cst, xfull, w_in_d, w_gate_d, w_out_d,
             b_in_d, b_gate_d, b_out_d, A_d, B_d, C_d):
    nc = env.nc

    xln = env.sb("s_xln", [128, NCT * SSEG], BF16)
    scratch = {}
    for nm in ("mu", "msq", "var", "rstd", "lntmp"):
        scratch[nm] = env.sb("s_" + nm, [128, SSEG], F32)
    wbuf = env.sb("s_wbuf", [128, 16384], BF16)
    xpseg = env.sb("s_xp", [128, NTI * XPW], BF16)
    y1 = env.sb("s_y1", [128, NTI * SSEG], F32)
    y2 = env.sb("s_y2", [128, NTI * SSEG], BF16)
    gate = env.sb("s_gate", [128, NTI * SSEG], BF16)
    scratch["sqseg"] = gate
    kap = env.sb("s_kap", [128, NTI * T0], F32)
    vA = env.sb("s_vA", [128, NTI * DS], F32)
    vB = env.sb("s_vB", [128, NTI * DS], F32)
    vtmp = env.sb("s_vtmp", [128, 256], F32)
    ktmp = env.sb("s_ktmp", [128, DS], F32)
    Bbuf = env.sb("s_B", [128, NTI * DS], F32)
    Cbuf = env.sb("s_C", [128, NTI * DS], F32)
    bincol = env.sb("s_bin", [128, NTI], F32)
    bgcol = env.sb("s_bg", [128, NTI], F32)
    bocol = env.sb("s_bo", [128, NCT], F32)

    pB = env.ps("s_pB", [128, 1024], F32)
    pI = env.ps("s_pI", [128, 1024], F32)
    pU = env.ps("s_pU", [128, 512], F32)
    pC = env.ps("s_pC", [128, 512], F32)
    scratch["pS"] = pU[:, 0:SSEG]
    scratch["pQ"] = pC[:, 0:SSEG]

    Abuf = y1

    def p_ld(sy):
        sy.dma_start(out=bincol[:], in_=b_in_d).then_inc(env.sems["LD"], 16)
        env.cnt["LD"] += 16
        sy.dma_start(out=bgcol[:], in_=b_gate_d).then_inc(env.sems["LD"], 16)
        env.cnt["LD"] += 16
        sy.dma_start(out=bocol[:], in_=b_out_d).then_inc(env.sems["LD"], 16)
        env.cnt["LD"] += 16
        sy.dma_start(out=Abuf[:, 0:NTI * 256].rearrange("p (a m) -> p a m", m=256),
                     in_=A_d.rearrange("(a p) m -> p a m", p=128)).then_inc(env.sems["LD"], 16)
        env.cnt["LD"] += 16
        sy.dma_start(out=Bbuf[:].rearrange("p (a n) -> p a n", n=DS),
                     in_=B_d.rearrange("(a p) n -> p a n", p=128)).then_inc(env.sems["LD"], 16)
        env.cnt["LD"] += 16
        last = sy.dma_start(out=Cbuf[:].rearrange("p (a n) -> p a n", n=DS),
                            in_=C_d.rearrange("(a p) n -> p a n", p=128))
        env.inc(last, "LD", 16)
    env.phase([("sync", p_ld)])

    def kprec(eng, tiles, vt, kt, own):
        def fn(e):
            def step(inst):
                env.inc(inst, own)
                e.wait_ge(env.sems[own], env.cnt[own])
            for ct in tiles:
                vsl = slice(ct * DS, (ct + 1) * DS)
                step(e.tensor_tensor(kt[:], Cbuf[:, vsl], Bbuf[:, vsl], ALU.mult))
                step(e.tensor_reduce(kap[:, ct * T0:ct * T0 + 1], kt[:], AX.X, ALU.add))
                vcur = Bbuf
                for tau in range(1, T0):
                    vnext = vA if vcur is not vA else vB
                    a3 = Abuf[:, ct * 256:(ct + 1) * 256].rearrange("p (m n) -> p m n", n=DS)
                    vc = vcur[:, vsl]
                    vbc = bass.AP(vc.tensor, vc.offset, [vc.ap[0], [1, DS], [0, DS]])
                    vt_mn = bass.AP(vt[:].tensor, vt[:].offset,
                                    [vt[:].ap[0], [1, DS], [DS, DS]])
                    step(e.tensor_tensor(vt_mn, a3, vbc, ALU.mult))
                    step(e.tensor_reduce(vnext[:, vsl], vt[:].rearrange("p (n m) -> p n m", m=DS),
                                         AX.X, ALU.add))
                    step(e.tensor_tensor(kt[:], Cbuf[:, vsl], vnext[:, vsl], ALU.mult))
                    step(e.tensor_reduce(kap[:, ct * T0 + tau:ct * T0 + tau + 1], kt[:],
                                         AX.X, ALU.add))
                    vcur = vnext
        return fn
    env.phase([("vector", kprec(None, range(0, NTI), vtmp, ktmp, "V"))])

    def p_ms(v):
        last = v.memset(xpseg[:], 0.0)
        env.inc(last, "V")
    env.phase([("vector", p_ms)])

    for s in range(L // SSEG):
        t0 = s * SSEG
        emit_ln(env, cst, xfull, xln, scratch, seg_w=SSEG,
                src_stride=L, src_off=t0)

        def ld_w(gp, wd=w_in_d, n=16):
            r = wd.rearrange("(a p) j -> p a j", p=128)
            last = gp.dma_start(out=wbuf[:].rearrange("p (a j) -> p a j", j=r.shape[2]), in_=r)
            env.inc(last, "LG", 16)
        env.phase([("gpsimd", ld_w)])

        def evac_xp(eng, jt, half):
            return eng.activation(xpseg[:, jt * XPW + 32: jt * XPW + 32 + SSEG], half,
                                  AF.Identity, bias=bincol[:, jt:jt + 1])
        emit_proj_feat(env, None, wbuf, xln, pB[:, 0:512], pB[:, 512:1024],
                       njt=16, nct=8, seg_w=SSEG, evac_custom=("scalar", evac_xp))

        def ld_wg(gp):
            r = w_gate_d.rearrange("(a p) j -> p a j", p=128)
            last = gp.dma_start(out=wbuf[:].rearrange("p (a j) -> p a j", j=2048), in_=r)
            env.inc(last, "LG", 16)
        env.phase([("gpsimd", ld_wg)])

        def evac_gate(eng, jt, half):
            return eng.activation(gate[:, jt * SSEG:(jt + 1) * SSEG], half,
                                  AF.Sigmoid, bias=bgcol[:, jt:jt + 1])
        emit_proj_feat(env, None, wbuf, xln, pB[:, 0:512], pB[:, 512:1024],
                       njt=16, nct=8, seg_w=SSEG, evac_custom=("scalar", evac_gate))

        def conv_v(v):
            for ct in range(NTI):
                ysl = y1[:, ct * SSEG:(ct + 1) * SSEG]
                for tau in range(0, T0):
                    xin = xpseg[:, ct * XPW + 32 - tau: ct * XPW + 32 - tau + SSEG]
                    kcol = kap[:, ct * T0 + tau: ct * T0 + tau + 1]
                    if tau == 0:
                        last = v.tensor_scalar(ysl, xin, kcol, None, ALU.mult)
                    else:
                        last = v.scalar_tensor_tensor(ysl, xin, kcol, ysl, ALU.mult, ALU.add)
            env.inc(last, "V")
        env.phase([("vector", conv_v)])

        def epi_v(v):
            for ct in range(NTI):
                ysl = slice(ct * SSEG, (ct + 1) * SSEG)
                last = v.tensor_tensor(y2[:, ysl], y1[:, ysl], gate[:, ysl], ALU.mult)
            srcap = xpseg[:].rearrange("p (a w) -> p a w", w=XPW)[:, :, SSEG:SSEG + 32]
            dstap = xpseg[:].rearrange("p (a w) -> p a w", w=XPW)[:, :, 0:32]
            last = v.tensor_copy(dstap, srcap)
            env.inc(last, "V")
        env.phase([("vector", epi_v)])

        if SSM_DEBUG and s == 0:
            dbg = {}
            for nm, buf, w in (("d_kap", kap, NTI * T0), ("d_xp", xpseg, NTI * XPW),
                               ("d_y1", y1, NTI * SSEG), ("d_gate", gate, NTI * SSEG),
                               ("d_xln", xln, NCT * SSEG), ("d_y2", y2, NTI * SSEG)):
                dt = {"d_kap": F32, "d_y1": F32}.get(nm, BF16)
                dbg[nm] = nc.dram_tensor(nm, [128, w], dt, kind="ExternalOutput").ap()

            def p_dbg(sy, dbg=dbg):
                for nm, buf in (("d_kap", kap), ("d_xp", xpseg), ("d_y1", y1),
                                ("d_gate", gate), ("d_xln", xln), ("d_y2", y2)):
                    last = sy.dma_start(out=dbg[nm], in_=buf[:])
                    env.inc(last, "ST", 16)
            env.phase([("sync", p_dbg)])

        def ld_wo(gp):
            r = w_out_d.rearrange("(a p) j -> p a j", p=128)
            last = gp.dma_start(out=wbuf[:].rearrange("p (a j) -> p a j", j=1024), in_=r)
            env.inc(last, "LG", 16)
        env.phase([("gpsimd", ld_wo)])

        def evac_out(eng, jt, half, t0=t0):
            dsl = xfull[:, jt * L + t0: jt * L + t0 + SSEG]
            return eng.scalar_tensor_tensor(dsl, half, bocol[:, jt:jt + 1], dsl,
                                            ALU.add, ALU.add)
        emit_proj_feat(env, None, wbuf, y2, pB[:, 0:512], pB[:, 512:1024],
                       njt=8, nct=16, seg_w=SSEG, evac_custom=("vector", evac_out))


def build_ssm():
    nc = bass.Bass("TRN2", target_bir_lowering=False, detect_race_conditions=False)
    xin = nc.dram_tensor("xT", [DM, L], F32, kind="ExternalInput").ap()
    w_in_d = nc.dram_tensor("w_in", [DM, DI], F32, kind="ExternalInput").ap()
    w_gate_d = nc.dram_tensor("w_gate", [DM, DI], F32, kind="ExternalInput").ap()
    w_out_d = nc.dram_tensor("w_out", [DI, DM], F32, kind="ExternalInput").ap()
    b_in_d = nc.dram_tensor("b_in", [128, NTI], F32, kind="ExternalInput").ap()
    b_gate_d = nc.dram_tensor("b_gate", [128, NTI], F32, kind="ExternalInput").ap()
    b_out_d = nc.dram_tensor("b_out", [128, NCT], F32, kind="ExternalInput").ap()
    A_d = nc.dram_tensor("A", [DI, 256], F32, kind="ExternalInput").ap()
    B_d = nc.dram_tensor("B_ssm", [DI, DS], F32, kind="ExternalInput").ap()
    C_d = nc.dram_tensor("C_ssm", [DI, DS], F32, kind="ExternalInput").ap()
    SSM_CONSTS = [c for c in CONST_SPECS if c[0] in ("ones128", "ones128f")]
    cst_d = {nm: nc.dram_tensor(nm, shp, dt, kind="ExternalInput").ap()
             for nm, shp, dt in SSM_CONSTS}
    xout = nc.dram_tensor("xoutT", [DM, L], F32, kind="ExternalOutput").ap()

    with Env(nc) as env:
        cst = {nm: env.sb(nm + "_sb", shp, dt) for nm, shp, dt in SSM_CONSTS}
        for nm, val in (("epsc", EPS), ("onec", 1.0), ("neg1c", -1.0), ("neghc", -0.5)):
            cst[nm] = env.sb(nm, [128, 1], F32)
            cst[nm + "_val"] = val
        xfull = env.sb("s_xfull", [128, NCT * L], F32)

        def p0_v(v):
            for nm in ("epsc", "onec", "neg1c", "neghc"):
                last = v.memset(cst[nm][:], cst[nm + "_val"])
            env.inc(last, "V")

        def p0_sy(sy):
            for nm, shp, dt in SSM_CONSTS:
                sy.dma_start(out=cst[nm][:], in_=cst_d[nm]).then_inc(env.sems["LD"], 16)
                env.cnt["LD"] += 16
            last = sy.dma_start(out=xfull[:].rearrange("p (a t) -> p a t", t=L),
                                in_=xin.rearrange("(a p) t -> p a t", p=128))
            env.inc(last, "LD", 16)
        env.phase([("sync", p0_sy), ("vector", p0_v)])
        emit_ssm(env, cst, xfull, w_in_d, w_gate_d, w_out_d,
                 b_in_d, b_gate_d, b_out_d, A_d, B_d, C_d)

        def pf_sy(sy):
            last = sy.dma_start(out=xout.rearrange("(a p) t -> p a t", p=128),
                                in_=xfull[:].rearrange("p (a t) -> p a t", t=L))
            env.inc(last, "ST", 16)
        env.phase([("sync", pf_sy)])

        @env.block.sync
        def _(sy):
            sy.wait_ge(env.sems["ST"], env.cnt["ST"])
    return nc


def run_ssm(x_full, w_in, b_in, w_gate, b_gate, A, B_ssm, C_ssm, w_out, b_out):
    consts = host_consts()
    nc = build_ssm()
    in_maps = []
    B = x_full.shape[0]
    for b in range(B):
        m = {"xT": np.ascontiguousarray(x_full[b].T),
             "w_in": w_in, "w_gate": w_gate, "w_out": w_out,
             "b_in": np.ascontiguousarray(b_in.reshape(NTI, 128).T),
             "b_gate": np.ascontiguousarray(b_gate.reshape(NTI, 128).T),
             "b_out": np.ascontiguousarray(b_out.reshape(NCT, 128).T),
             "A": np.ascontiguousarray(A.transpose(0, 2, 1).reshape(DI, 256)),
             "B_ssm": B_ssm, "C_ssm": C_ssm}
        m.update(consts)
        in_maps.append(m)
    res = run_bass_kernel_spmd(nc, in_maps, core_ids=list(range(B)))
    return np.stack([np.ascontiguousarray(res.results[b]["xoutT"]).T for b in range(B)])



HID = 2816
NHT = HID // 128
FSEG = 128
FNTC = FSEG // 128


def _q_slot(n, i):
    return (i % 2) if (i // 2) * 2 + 1 < n else 0


def quant_xn_counts(base, n):
    out = []
    for i in range(n):
        p, r = divmod(i, 2)
        if 2 * p + 1 < n:
            out.append(base + 10 * p + 7 + r)
        else:
            out.append(base + 10 * p + 4)
    return out


def quant_s_body(env, s, jobs, q, xn_counts):
    V = env.sems["V"]
    ab, l2 = q["ab"], q["l2"]
    n = len(jobs)
    for i, (srca, dsta, w) in enumerate(jobs):
        a0 = _q_slot(n, i) * w
        s.wait_ge(V, xn_counts[i])
        s.activation(l2[:, a0:a0 + w], ab[:, a0:a0 + w], AF.Sign)
        inst = s.activation(ab[:, a0:a0 + w], ab[:, a0:a0 + w], AF.Abs)
        env.inc(inst, "S")


def quant_v_body(env, v, jobs, q, s_counts):
    V, S_ = env.sems["V"], env.sems["S"]
    scl, rscl, ab, lv, l1, l2 = (q[k] for k in ("scl", "rscl", "ab", "lv", "l1", "l2"))
    n = len(jobs)
    rec = {}
    for g0 in range(0, n, 2):
        pair = list(range(g0, min(g0 + 2, n)))

        def sl(i):
            srca, dsta, w = jobs[i]
            return srca, dsta, w, w // 16, _q_slot(n, i)

        for i in pair:
            srca, dsta, w, nb, s_ = sl(i)
            inst = v.tensor_reduce(scl[:, s_ * nb:(s_ + 1) * nb],
                                   srca.rearrange("p (a b) -> p a b", b=16),
                                   AX.X, ALU.max, apply_absolute_value=True)
            env.inc(inst, "V"); rec[("r1", i)] = env.cnt["V"]
        for i in pair:
            srca, dsta, w, nb, s_ = sl(i)
            v.wait_ge(V, rec[("r1", i)])
            inst = v.tensor_scalar(scl[:, s_ * nb:(s_ + 1) * nb], scl[:, s_ * nb:(s_ + 1) * nb],
                                   1.0 / 6.0, 1e-12, ALU.mult, ALU.max)
            env.inc(inst, "V"); rec[("r2", i)] = env.cnt["V"]
        for i in pair:
            srca, dsta, w, nb, s_ = sl(i)
            v.wait_ge(V, rec[("r2", i)])
            inst = v.reciprocal(rscl[:, s_ * nb:(s_ + 1) * nb], scl[:, s_ * nb:(s_ + 1) * nb])
            env.inc(inst, "V"); rec[("r3", i)] = env.cnt["V"]
        for i in pair:
            srca, dsta, w, nb, s_ = sl(i)
            v.wait_ge(V, rec[("r3", i)])
            rb = rscl[:, s_ * nb:(s_ + 1) * nb]
            rbc = bass.AP(rb.tensor, rb.offset, [rb.ap[0], [1, nb], [0, 16]])
            a0 = s_ * w
            inst = v.tensor_tensor(ab[:, a0:a0 + w].rearrange("p (a b) -> p a b", b=16),
                                   srca.rearrange("p (a b) -> p a b", b=16), rbc, ALU.mult)
            env.inc(inst, "V")
        for i in pair:
            srca, dsta, w, nb, s_ = sl(i)
            a0 = s_ * w
            v.wait_ge(S_, s_counts[i])
            v.tensor_scalar(lv[:, a0:a0 + w], ab[:, a0:a0 + w], 0.25, None, ALU.is_gt)
            for t in (0.75, 1.25, 1.75):
                v.scalar_tensor_tensor(lv[:, a0:a0 + w], ab[:, a0:a0 + w], t, lv[:, a0:a0 + w],
                                       ALU.is_gt, ALU.add)
            v.tensor_scalar(l1[:, a0:a0 + w], ab[:, a0:a0 + w], 2.5, None, ALU.is_gt)
            v.scalar_tensor_tensor(l1[:, a0:a0 + w], ab[:, a0:a0 + w], 3.5, l1[:, a0:a0 + w],
                                   ALU.is_gt, ALU.add)
            v.scalar_tensor_tensor(lv[:, a0:a0 + w], lv[:, a0:a0 + w], 0.5, l1[:, a0:a0 + w],
                                   ALU.mult, ALU.add)
            v.tensor_scalar(l1[:, a0:a0 + w], ab[:, a0:a0 + w], 5.0, 2.0, ALU.is_gt, ALU.mult)
            v.tensor_tensor(lv[:, a0:a0 + w], lv[:, a0:a0 + w], l1[:, a0:a0 + w], ALU.add)
            v.tensor_tensor(lv[:, a0:a0 + w], lv[:, a0:a0 + w], l2[:, a0:a0 + w], ALU.mult)
            sb_ = scl[:, s_ * nb:(s_ + 1) * nb]
            sbc = bass.AP(sb_.tensor, sb_.offset, [sb_.ap[0], [1, nb], [0, 16]])
            inst = v.tensor_tensor(dsta.rearrange("p (a b) -> p a b", b=16),
                                   lv[:, a0:a0 + w].rearrange("p (a b) -> p a b", b=16), sbc, ALU.mult)
            env.inc(inst, "V")


def emit_quant_multi(env, jobs, q):
    bv = env.cnt["V"]
    bs = env.cnt["S"]
    xn_counts = quant_xn_counts(bv, len(jobs))
    s_counts = [bs + i + 1 for i in range(len(jobs))]

    def fs(s):
        quant_s_body(env, s, jobs, q, xn_counts)

    def fv(v):
        quant_v_body(env, v, jobs, q, s_counts)
    env.phase([("scalar", fs), ("vector", fv)])


def quant_chunks(env, src_ap_fn, dst_ap_fn, q, total, chunk=256):
    jobs = []
    off = 0
    while off < total:
        w = min(chunk, total - off)
        jobs.append((src_ap_fn(off, w), dst_ap_fn(off, w), w))
        off += w
    emit_quant_multi(env, jobs, q)


def quant_fin_counts(base, n):
    out = []
    for i in range(n):
        p, r = divmod(i, 2)
        if 2 * p + 1 < n:
            out.append(base + 10 * p + 9 + r)
        else:
            out.append(base + 10 * p + 5)
    return out


def emit_quant_tr(env, cst, q, jobs, dst_chunk, psum):
    n = len(jobs)
    G = (n + 3) // 4
    bv, bs, bt = env.cnt["V"], env.cnt["S"], env.cnt["T"]
    V, S_, T_ = env.sems["V"], env.sems["S"], env.sems["T"]
    fins = quant_fin_counts(bv, n)
    xns = quant_xn_counts(bv, n)
    evac_after = {}
    for g in range(G):
        i = 4 * g + 7
        evac_after.setdefault(i if i <= n - 2 else n, []).append(g)
    nev_before = [0] * (n + 1)
    cum = 0
    for i in range(n + 1):
        nev_before[i] = cum
        cum += len(evac_after.get(i, []))
    s_counts = [bs + i + 1 + nev_before[i] for i in range(n)]
    sa_counts = dict((g, None) for g in range(G))
    ev_count = {}
    pos = 0
    for i in range(n + 1):
        if i < n:
            pos += 1
        for g in evac_after.get(i, []):
            pos += 1
            ev_count[g] = bs + pos
    t_count = dict((g, bt + g + 1) for g in range(G))

    def f_v(v):
        quant_v_body(env, v, jobs, q, s_counts)

    def f_s(s):
        ab, l2 = q["ab"], q["l2"]

        def evac(g):
            s.wait_ge(T_, t_count[g])
            lo, hi = 8 * g, min(8 * g + 8, 2 * n)
            for c in range(lo, hi):
                last = s.activation(dst_chunk(c), psum[:, (c - lo) * 128:(c - lo + 1) * 128],
                                    AF.Identity)
            env.inc(last, "S")
        for i in range(n + 1):
            if i < n:
                srca, dsta, w = jobs[i]
                a0 = _q_slot(n, i) * w
                s.wait_ge(V, xns[i])
                s.activation(l2[:, a0:a0 + w], ab[:, a0:a0 + w], AF.Sign)
                inst = s.activation(ab[:, a0:a0 + w], ab[:, a0:a0 + w], AF.Abs)
                env.inc(inst, "S")
            for g in evac_after.get(i, []):
                evac(g)

    def f_t(t):
        for g in range(G):
            if g >= 1:
                t.wait_ge(S_, ev_count[g - 1])
            t.wait_ge(V, fins[min(4 * g + 3, n - 1)])
            lo, hi = 8 * g, min(8 * g + 8, 2 * n)
            for c in range(lo, hi):
                i, sub = divmod(c, 2)
                srcq = jobs[i][1]
                last = t.transpose(psum[:, (c - lo) * 128:(c - lo + 1) * 128],
                                   srcq[:, sub * 128:(sub + 1) * 128], cst["ident"][:])
            env.inc(last, "T")
    env.phase([("vector", f_v), ("scalar", f_s), ("tensor", f_t)])


def emit_transpose_tiles(env, pairs, psum, ident, evac_dtype_note=None):
    ngrp = psum.shape[1] // 128 if hasattr(psum, "shape") else 4
    base_t = env.cnt["T"]
    base_v = env.cnt["V"]
    n = len(pairs)
    ngroups = (n + ngrp - 1) // ngrp

    def f_t(t):
        for g in range(ngroups):
            if g >= 1:
                t.wait_ge(env.sems["V"], base_v + g)
            lo, hi = g * ngrp, min((g + 1) * ngrp, n)
            for i in range(lo, hi):
                src, dstp = pairs[i]
                last = t.transpose(psum[:, (i - lo) * 128:(i - lo + 1) * 128], src, ident)
            env.inc(last, "T")

    def f_v(v):
        for g in range(ngroups):
            v.wait_ge(env.sems["T"], base_t + g + 1)
            lo, hi = g * ngrp, min((g + 1) * ngrp, n)
            for i in range(lo, hi):
                src, dstp = pairs[i]
                last = v.tensor_copy(dstp, psum[:, (i - lo) * 128:(i - lo + 1) * 128])
            env.inc(last, "V")
    env.phase([("tensor", f_t), ("vector", f_v)])


def emit_wprep_gu(env, cst, q, wdram, wdstT, wstage, wq_bf, pTb):
    bv, bs, bt, bld = env.cnt["V"], env.cnt["S"], env.cnt["T"], env.cnt["LD"]
    V, S_, T_, LD = env.sems["V"], env.sems["S"], env.sems["T"], env.sems["LD"]

    def jobs_for(wt):
        sl = (wt % 2) * 1024
        return [(wstage[:, sl + o: sl + o + 256], wq_bf[:, sl + o: sl + o + 256], 256)
                for o in range(0, 1024, 256)]

    def f_ld(sy):
        for wt in range(NHT):
            if wt >= 2:
                sy.wait_ge(V, bv + 21 * (wt - 2) + 20)
            last = sy.dma_start(out=wstage[:, (wt % 2) * 1024:(wt % 2) * 1024 + 1024],
                                in_=wdram[wt * 128:(wt + 1) * 128, :])
            env.inc(last, "LD", 16)

    def f_v(v):
        for wt in range(NHT):
            if wt >= 1:
                v.wait_ge(T_, bt + wt)
                k = wt - 1
                for kc in range(8):
                    last = v.tensor_copy(wdstT[:, kc * HID + k * 128:kc * HID + (k + 1) * 128],
                                         pTb[:, kc * 128:(kc + 1) * 128])
                env.inc(last, "V")
            v.wait_ge(LD, bld + 16 * (wt + 1))
            if wt >= 2:
                v.wait_ge(T_, bt + wt - 1)
            quant_v_body(env, v, jobs_for(wt), q,
                         [bs + 4 * wt + i + 1 for i in range(4)])
        v.wait_ge(T_, bt + NHT)
        k = NHT - 1
        for kc in range(8):
            last = v.tensor_copy(wdstT[:, kc * HID + k * 128:kc * HID + (k + 1) * 128],
                                 pTb[:, kc * 128:(kc + 1) * 128])
        env.inc(last, "V")

    def f_s(s):
        for wt in range(NHT):
            quant_s_body(env, s, jobs_for(wt), q,
                         quant_xn_counts(bv + 21 * wt, 4))

    def f_t(t):
        for wt in range(NHT):
            t.wait_ge(V, bv + 21 * wt + 20)
            sl = (wt % 2) * 1024
            for kc in range(8):
                last = t.transpose(pTb[:, kc * 128:(kc + 1) * 128],
                                   wq_bf[:, sl + kc * 128: sl + (kc + 1) * 128],
                                   cst["ident"][:])
            env.inc(last, "T")
    env.phase([("sync", f_ld), ("vector", f_v), ("scalar", f_s), ("tensor", f_t)])


def emit_wprep_wd(env, cst, q, wd_d, wdq_dram, wstage, wq_bf, wdk, pTb):
    bv, bs, bt, bld, bst = (env.cnt["V"], env.cnt["S"], env.cnt["T"],
                            env.cnt["LD"], env.cnt["ST"])
    V, S_, T_, LD, ST = (env.sems["V"], env.sems["S"], env.sems["T"],
                         env.sems["LD"], env.sems["ST"])
    NJ = HID // 256
    NG = (NHT + 7) // 8
    PTV = 5 * NJ + NG

    def jobs_for(wt):
        return [(wstage[:, o:o + 256], wq_bf[:, o:o + 256], 256)
                for o in range(0, HID, 256)]

    def stores_for(sy, wt):
        for g in range(NG):
            m = NG * wt + g
            sy.wait_ge(V, bv + PTV * wt + 5 * NJ + g + 1)
            lo, hi = g * 8, min(g * 8 + 8, NHT)
            dst = wdq_dram.rearrange("(a p) j -> p a j", p=128)[:, lo:hi,
                                                                wt * 128:(wt + 1) * 128]
            last = sy.dma_start(out=dst,
                                in_=wdk[m % 2][:, 0:(hi - lo) * 128].rearrange(
                                    "p (a j) -> p a j", j=128))
            env.inc(last, "ST", 16)

    def f_ld(sy):
        for wt in range(8):
            if wt >= 1:
                sy.wait_ge(V, bv + PTV * (wt - 1) + 5 * NJ)
            last = sy.dma_start(out=wstage[:], in_=wd_d[wt * 128:(wt + 1) * 128, :])
            env.inc(last, "LD", 16)
            if wt >= 1:
                stores_for(sy, wt - 1)
        stores_for(sy, 7)

    def f_v(v):
        for wt in range(8):
            v.wait_ge(LD, bld + 16 * (wt + 1))
            if wt >= 1:
                v.wait_ge(T_, bt + 3 * wt)
            quant_v_body(env, v, jobs_for(wt), q,
                         [bs + NJ * wt + i + 1 for i in range(NJ)])
            for g in range(NG):
                m = NG * wt + g
                v.wait_ge(T_, bt + m + 1)
                if m >= 2:
                    v.wait_ge(ST, bst + 16 * (m - 1))
                lo, hi = g * 8, min(g * 8 + 8, NHT)
                for kc in range(lo, hi):
                    last = v.tensor_copy(wdk[m % 2][:, (kc - lo) * 128:(kc - lo + 1) * 128],
                                         pTb[:, (kc - lo) * 128:(kc - lo + 1) * 128])
                env.inc(last, "V")

    def f_s(s):
        for wt in range(8):
            quant_s_body(env, s, jobs_for(wt), q,
                         quant_xn_counts(bv + PTV * wt, NJ))

    def f_t(t):
        for wt in range(8):
            for g in range(NG):
                m = NG * wt + g
                if g == 0:
                    t.wait_ge(V, bv + PTV * wt + 5 * NJ)
                else:
                    t.wait_ge(V, bv + PTV * wt + 5 * NJ + g)
                lo, hi = g * 8, min(g * 8 + 8, NHT)
                for kc in range(lo, hi):
                    last = t.transpose(pTb[:, (kc - lo) * 128:(kc - lo + 1) * 128],
                                       wq_bf[:, kc * 128:(kc + 1) * 128], cst["ident"][:])
                env.inc(last, "T")

    env.phase([("sync", f_ld), ("vector", f_v), ("scalar", f_s), ("tensor", f_t)])


def emit_fp4ffn(env, cst, xfull, wdq_dram, wg_d, wu_d, wd_d):
    nc = env.nc

    wgqT = env.sb("f_wgqT", [128, 8 * HID], BF16)
    wuqT = env.sb("f_wuqT", [128, 8 * HID], BF16)
    wdk = [env.sb("f_wdk0", [128, DM], BF16), env.sb("f_wdk1", [128, DM], BF16)]

    q = {"scl": env.sb("f_scl", [128, 64], F32),
         "rscl": env.sb("f_rscl", [128, 64], F32),
         "ab": env.sb("f_qab", [128, 512], F32),
         "lv": env.sb("f_qlv", [128, 512], F32),
         "l1": env.sb("f_ql1", [128, 512], F32),
         "l2": env.sb("f_ql2", [128, 512], F32)}

    xlnf = env.sb("f_xlnf", [128, NCT * FSEG], F32)
    scratch = {}
    for nm in ("mu", "msq", "var", "rstd", "lntmp"):
        scratch[nm] = env.sb("f_" + nm, [128, FSEG], F32)
    xtok = env.sb("f_xtok", [128, FNTC * DM], F32)
    dsb = xtok

    xqT = env.sb("f_xqT", [128, NCT * FSEG], BF16)
    hmt = env.sb("f_hmt", [128, 2 * FSEG], F32)
    hmtok = env.sb("f_hmtok", [128, HID], F32)
    wstage = hmtok
    hqtok = env.sb("f_hqtok", [128, HID], BF16)
    xqtok = hqtok
    wq_bf = hqtok
    hqT = env.sb("f_hqT", [128, NHT * FSEG], BF16)
    scratch["sqseg"] = hqT
    sgt = env.sb("f_sgt", [128, 2 * FSEG], F32)

    pg = env.ps("f_pg", [128, 512], F32)
    pu = env.ps("f_pu", [128, 512], F32)
    pTf = env.ps("f_pTf", [128, 512], F32)
    pTb = env.ps("f_pTb", [128, 1024], BF16)
    pd = env.ps("f_pd", [128, NCT * FSEG], F32)
    scratch["pS"] = pg[:, 0:FSEG]
    scratch["pQ"] = pu[:, 0:FSEG]

    for wdram, wdstT in ((wg_d, wgqT), (wu_d, wuqT)):
        emit_wprep_gu(env, cst, q, wdram, wdstT, wstage, wq_bf, pTb)
    emit_wprep_wd(env, cst, q, wd_d, wdq_dram, wstage, wq_bf, wdk, pTb)

    for s_ in range(L // FSEG):
        t0 = s_ * FSEG

        emit_ln(env, cst, xfull, xlnf, scratch, seg_w=FSEG, src_stride=L, src_off=t0)

        pairs = [(xlnf[:, ct * FSEG + tc * 128: ct * FSEG + (tc + 1) * 128],
                  xtok[:, tc * DM + ct * 128: tc * DM + (ct + 1) * 128])
                 for ct in range(NCT) for tc in range(FNTC)]
        emit_transpose_tiles(env, pairs, pTf[:, 0:512], cst["identf"][:])

        for tc in range(FNTC):
            jobs = [(xtok[:, tc * DM + o: tc * DM + o + 256],
                     xqtok[:, tc * DM + o: tc * DM + o + 256], 256)
                    for o in range(0, DM, 256)]
            emit_quant_tr(env, cst, q, jobs,
                          lambda c, tc=tc: xqT[:, c * FSEG + tc * 128: c * FSEG + tc * 128 + 128],
                          pTb[:, 0:1024])

        for tsub in range(FNTC):
            base_s = env.cnt["S"]
            base_v = env.cnt["V"]
            incs = {}

            def hid_t(t, tsub=tsub):
                def tr(k):
                    t.wait_ge(env.sems["V"], base_v + 2 * k + 1)
                    last = t.transpose(pTf[:, (k % 2) * 128:(k % 2) * 128 + 128],
                                       hmt[:, (k % 2) * 128:(k % 2) * 128 + 128],
                                       cst["identf"][:])
                    env.inc(last, "T")
                    incs[("tr", k)] = env.cnt["T"]
                for nt in range(NHT):
                    if nt >= 2:
                        t.wait_ge(env.sems["V"], base_v + 2 * nt - 3)
                        t.wait_ge(env.sems["S"], base_s + nt - 1)
                    pga = pg if nt % 2 == 0 else pd[:, 0:512]
                    pua = pu if nt % 2 == 0 else pd[:, 512:1024]
                    for kc in range(8):
                        t.matmul(pga[:, 0:FSEG],
                                 wgqT[:, kc * HID + nt * 128: kc * HID + (nt + 1) * 128],
                                 xqT[:, kc * FSEG:(kc + 1) * FSEG],
                                 start=(kc == 0), stop=(kc == 7))
                    for kc in range(8):
                        last = t.matmul(pua[:, 0:FSEG],
                                        wuqT[:, kc * HID + nt * 128: kc * HID + (nt + 1) * 128],
                                        xqT[:, kc * FSEG:(kc + 1) * FSEG],
                                        start=(kc == 0), stop=(kc == 7))
                    env.inc(last, "T")
                    incs[("mm", nt)] = env.cnt["T"]
                    if nt >= 1:
                        tr(nt - 1)
                tr(NHT - 1)

            def hid_s(s, tsub=tsub):
                for nt in range(NHT):
                    pga = pg if nt % 2 == 0 else pd[:, 0:512]
                    if nt >= 2:
                        s.wait_ge(env.sems["V"], base_v + 2 * nt - 3)
                    s.wait_ge(env.sems["T"], incs[("mm", nt)])
                    last = s.activation(sgt[:, (nt % 2) * FSEG:(nt % 2) * FSEG + FSEG],
                                        pga[:, 0:FSEG], AF.Silu)
                    env.inc(last, "S")

            def hid_v(v, tsub=tsub):
                for nt in range(NHT):
                    pua = pu if nt % 2 == 0 else pd[:, 512:1024]
                    v.wait_ge(env.sems["S"], base_s + nt + 1)
                    if nt >= 2:
                        v.wait_ge(env.sems["T"], incs[("tr", nt - 2)])
                    last = v.tensor_tensor(hmt[:, (nt % 2) * 128:(nt % 2) * 128 + 128],
                                           sgt[:, (nt % 2) * FSEG:(nt % 2) * FSEG + FSEG],
                                           pua[:, 0:FSEG], ALU.mult)
                    env.inc(last, "V")
                    v.wait_ge(env.sems["T"], incs[("tr", nt)])
                    last = v.tensor_copy(hmtok[:, nt * 128:(nt + 1) * 128],
                                         pTf[:, (nt % 2) * 128:(nt % 2) * 128 + 128])
                    env.inc(last, "V")
            env.phase([("tensor", hid_t), ("scalar", hid_s), ("vector", hid_v)])

            jobs = [(hmtok[:, o:o + 256], hqtok[:, o:o + 256], 256)
                    for o in range(0, HID, 256)]
            emit_quant_tr(env, cst, q, jobs,
                          lambda c, tsub=tsub: hqT[:, c * FSEG + tsub * 128:
                                                   c * FSEG + tsub * 128 + 128],
                          pTb[:, 0:1024])

        base_t = env.cnt["T"]
        base_ld = env.cnt["LD"]

        def down_ld(sy):
            for kc in range(NHT):
                if kc >= 2:
                    sy.wait_ge(env.sems["T"], base_t + kc - 1)
                src_ = wdq_dram[kc * 128:(kc + 1) * 128, :]
                last = sy.dma_start(out=wdk[kc % 2][:], in_=src_)
                env.inc(last, "LD", 16)

        def down_t(t):
            for kc in range(NHT):
                t.wait_ge(env.sems["LD"], base_ld + 16 * (kc + 1))
                for tsub in range(FNTC):
                    for hf in range(2):
                        last = t.matmul(pd[:, tsub * DM + hf * 512: tsub * DM + (hf + 1) * 512],
                                        hqT[:, kc * FSEG + tsub * 128: kc * FSEG + (tsub + 1) * 128],
                                        wdk[kc % 2][:, hf * 512:(hf + 1) * 512],
                                        start=(kc == 0), stop=(kc == NHT - 1))
                env.inc(last, "T")

        def down_ev(v):
            v.wait_ge(env.sems["T"], base_t + NHT)
            for tsub in range(FNTC):
                last = v.tensor_copy(dsb[:, tsub * DM:(tsub + 1) * DM],
                                     pd[:, tsub * DM:(tsub + 1) * DM])
            env.inc(last, "V")
        env.phase([("sync", down_ld), ("tensor", down_t), ("vector", down_ev)])

        base_t = env.cnt["T"]
        base_v = env.cnt["V"]

        def dtr_t(t):
            n = 0
            for tsub in range(FNTC):
                for ct in range(NCT):
                    if n % 4 == 0 and n >= 4:
                        t.wait_ge(env.sems["V"], base_v + n // 4)
                    last = t.transpose(pTf[:, (n % 4) * 128:(n % 4 + 1) * 128],
                                       dsb[:, tsub * DM + ct * 128: tsub * DM + (ct + 1) * 128],
                                       cst["identf"][:])
                    n += 1
                    if n % 4 == 0:
                        env.inc(last, "T")

        def dtr_v(v, t0=t0):
            n = 0
            for tsub in range(FNTC):
                for ct in range(NCT):
                    if n % 4 == 0:
                        v.wait_ge(env.sems["T"], base_t + n // 4 + 1)
                    sl = slice(ct * L + t0 + tsub * 128, ct * L + t0 + (tsub + 1) * 128)
                    last = v.tensor_tensor(xfull[:, sl], pTf[:, (n % 4) * 128:(n % 4 + 1) * 128],
                                           xfull[:, sl], ALU.add)
                    n += 1
                    if n % 4 == 0:
                        env.inc(last, "V")
        env.phase([("tensor", dtr_t), ("vector", dtr_v)])


def build_fp4():
    nc = bass.Bass("TRN2", target_bir_lowering=False, detect_race_conditions=False)
    xin = nc.dram_tensor("xT", [DM, L], F32, kind="ExternalInput").ap()
    wg_d = nc.dram_tensor("w_gate", [HID, DM], F32, kind="ExternalInput").ap()
    wu_d = nc.dram_tensor("w_up", [HID, DM], F32, kind="ExternalInput").ap()
    wd_d = nc.dram_tensor("w_down", [DM, HID], F32, kind="ExternalInput").ap()
    FP4_CONSTS = [c for c in CONST_SPECS if c[0] in ("ones128", "ones128f", "ident", "identf")]
    cst_d = {nm: nc.dram_tensor(nm, shp, dt, kind="ExternalInput").ap()
             for nm, shp, dt in FP4_CONSTS}
    wdq_dram = nc.dram_tensor("f_wdq_scratch", [NHT * 128, DM], BF16, kind="Internal").ap()
    xout = nc.dram_tensor("xoutT", [DM, L], F32, kind="ExternalOutput").ap()

    with Env(nc) as env:
        cst = {nm: env.sb(nm + "_sb", shp, dt) for nm, shp, dt in FP4_CONSTS}
        for nm, val in (("epsc", EPS), ("neghc", -0.5)):
            cst[nm] = env.sb(nm, [128, 1], F32)
            cst[nm + "_val"] = val
        xfull = env.sb("f_xfull", [128, NCT * L], F32)

        def p0_v(v):
            for nm in ("epsc", "neghc"):
                last = v.memset(cst[nm][:], cst[nm + "_val"])
            env.inc(last, "V")

        def p0_sy(sy):
            for nm, shp, dt in FP4_CONSTS:
                sy.dma_start(out=cst[nm][:], in_=cst_d[nm]).then_inc(env.sems["LD"], 16)
                env.cnt["LD"] += 16
            last = sy.dma_start(out=xfull[:].rearrange("p (a t) -> p a t", t=L),
                                in_=xin.rearrange("(a p) t -> p a t", p=128))
            env.inc(last, "LD", 16)
        env.phase([("sync", p0_sy), ("vector", p0_v)])
        with env.scope():
            emit_fp4ffn(env, cst, xfull, wdq_dram, wg_d, wu_d, wd_d)

        def pf_sy(sy):
            last = sy.dma_start(out=xout.rearrange("(a p) t -> p a t", p=128),
                                in_=xfull[:].rearrange("p (a t) -> p a t", t=L))
            env.inc(last, "ST", 16)
        env.phase([("sync", pf_sy)])

        @env.block.sync
        def _(sy):
            sy.wait_ge(env.sems["ST"], env.cnt["ST"])
    return nc


def run_fp4(x_full, w_gate, w_up, w_down):
    consts = host_consts()
    nc = build_fp4()
    in_maps = []
    B = x_full.shape[0]
    for b in range(B):
        m = {"xT": np.ascontiguousarray(x_full[b].T),
             "w_gate": w_gate, "w_up": w_up, "w_down": w_down}
        m.update(consts)
        in_maps.append(m)
    res = run_bass_kernel_spmd(nc, in_maps, core_ids=list(range(B)))
    return np.stack([np.ascontiguousarray(res.results[b]["xoutT"]).T for b in range(B)])





NEXP = 8
MSEG = 512
MNTC = MSEG // 128
PSEG = 128


def emit_moe(env, cst, xfull, xln_dram, w2q_dram, w3q_dram, gate_wT_d, w1_d, w3_d, w2_d):
    nc = env.nc

    with env.scope():
        w2stage = env.sb("m_w2stage", [128, NHT * DM], BF16)
        for e in range(NEXP):
            def cst_gp(gp, e=e):
                r = w2_d[e].rearrange("(a p) j -> p a j", p=128)
                last = gp.dma_start(out=w2stage[:].rearrange("p (a j) -> p a j", j=DM), in_=r)
                env.inc(last, "LG", 16)
            env.phase([("gpsimd", cst_gp)])

            def cst_st(sy, e=e):
                for jp in range(2):
                    dst = w2q_dram[(e * 2 + jp) * NHT * 128:(e * 2 + jp + 1) * NHT * 128,
                                   :].rearrange("(nt p) j -> p nt j", p=128)
                    srcs = w2stage[:].rearrange("p (nt j) -> p nt j", j=DM)[
                        :, :, jp * 512:(jp + 1) * 512]
                    last = sy.dma_start(out=dst, in_=srcs)
                    env.inc(last, "ST", 16)
            env.phase([("sync", cst_st)])
            def cst3_gp(gp, e=e):
                for a in range(NCT):
                    srca = w3_d[e, a * 128:(a + 1) * 128, :].rearrange(
                        "p (nt h) -> p nt h", h=128)
                    dsta = w2stage[:, 0:NHT * 1024].rearrange(
                        "p (nt c) -> p nt c", c=1024)[:, :, a * 128:(a + 1) * 128]
                    last = gp.dma_start(out=dsta, in_=srca)
                    env.inc(last, "LG", 16)
            env.phase([("gpsimd", cst3_gp)])

            def cst3_st(sy, e=e):
                dst = w3q_dram[e * NHT * 128:(e + 1) * NHT * 128, :].rearrange(
                    "(nt p) c -> p nt c", p=128)
                last = sy.dma_start(out=dst, in_=w2stage[:, 0:NHT * 1024].rearrange(
                    "p (nt c) -> p nt c", c=1024))
                env.inc(last, "ST", 16)
            env.phase([("sync", cst3_st)])

    w1buf = env.sb("m_w1", [128, NCT * HID], BF16)
    w3k = [env.sb("m_w3k0", [128, NCT * 128], BF16), env.sb("m_w3k1", [128, NCT * 128], BF16)]
    wdk = [env.sb("m_wdk0", [128, 512], BF16), env.sb("m_wdk1", [128, 512], BF16)]
    xlnf = env.sb("m_xlnf", [128, NCT * PSEG], F32)
    xlnbf = env.sb("m_xlnbf", [128, NCT * PSEG], BF16)
    scratch = {}
    for nm in ("mu", "msq", "var", "rstd", "lntmp"):
        scratch[nm] = env.sb("m_" + nm, [128, PSEG], F32)
    gwT = env.sb("m_gwT", [128, NCT * NEXP], F32)
    lgt = env.sb("m_lgt", [128, NEXP], F32)
    rmax = env.sb("m_rmax", [128, 1], F32)
    rnegm = env.sb("m_rnegm", [128, 1], F32)
    rex = env.sb("m_rex", [128, NEXP], F32)
    rsum = env.sb("m_rsum", [128, 1], F32)
    rrs = env.sb("m_rrs", [128, 1], F32)
    req1 = env.sb("m_req1", [128, NEXP], F32)
    rtmp = env.sb("m_rtmp", [128, NEXP], F32)
    rm2 = env.sb("m_rm2", [128, 1], F32)
    re2x = env.sb("m_re2x", [128, 1], F32)
    rs2 = env.sb("m_rs2", [128, 1], F32)
    req2 = env.sb("m_req2", [128, NEXP], F32)
    rt1 = env.sb("m_rt1", [128, NEXP], F32)
    mrowtok = env.sb("m_mrowtok", [128, (L // 128) * NEXP], F32)
    mtr = env.sb("m_mtr", [8, L], BF16)
    mbc = env.sb("m_mbc", [128, MSEG], BF16)
    xeseg = env.sb("m_xeseg", [128, NCT * MSEG], BF16)
    scratch["sqseg"] = env.sb("m_sqf", [128, NCT * PSEG], F32)
    hbuf = env.sb("m_hbuf", [128, NHT * MSEG], BF16)
    sgt = env.sb("m_sgt", [128, 2 * MSEG], F32)
    dsbb = env.sb("m_dsbb", [128, MNTC * 512], BF16)

    pg = env.ps("m_pg", [128, 512], F32)
    pu = env.ps("m_pu", [128, 512], F32)
    pd = env.ps("m_pd", [128, MNTC * 512], F32)
    pT = env.ps("m_pT", [128, 1024], BF16)
    scratch["pS"] = pg[:, 0:PSEG]
    scratch["pQ"] = pu[:, 0:PSEG]

    def p_gw(sy):
        last = sy.dma_start(out=gwT[:].rearrange("p (a j) -> p a j", j=NEXP),
                            in_=gate_wT_d.rearrange("(a p) j -> p a j", p=128))
        env.inc(last, "LD", 16)
    env.phase([("sync", p_gw)])

    for ps in range(L // PSEG):
        t0 = ps * PSEG
        emit_ln(env, cst, xfull, xlnf, scratch, seg_w=PSEG, src_stride=L, src_off=t0)

        def p_cp(v, ps=ps):
            for ct in range(NCT):
                last = v.tensor_copy(xlnbf[:, ct * PSEG:(ct + 1) * PSEG],
                                     xlnf[:, ct * PSEG:(ct + 1) * PSEG])
            env.inc(last, "V")
        env.phase([("vector", p_cp)])

        def p_st(sy, t0=t0):
            dst = xln_dram.rearrange("(a p) t -> p a t", p=128)[:, :, t0:t0 + PSEG]
            last = sy.dma_start(out=dst, in_=xlnbf[:].rearrange("p (a t) -> p a t", t=PSEG))
            env.inc(last, "ST", 16)
        env.phase([("sync", p_st)])

        def p_rt(t, ps=ps):
            for ct in range(NCT):
                last = t.matmul(pd[:, 0:NEXP], xlnf[:, ct * PSEG:(ct + 1) * PSEG],
                                gwT[:, ct * NEXP:(ct + 1) * NEXP],
                                start=(ct == 0), stop=(ct == NCT - 1))
            env.inc(last, "T")
        env.phase([("tensor", p_rt)])

        s_base = env.cnt["S"]
        v_base = env.cnt["V"]

        def p_rm(v, ps=ps, s_base=s_base):
            V = env.sems["V"]

            def stp(i):
                env.inc(i, "V"); v.wait_ge(V, env.cnt["V"])
            stp(v.tensor_copy(lgt[:], pd[:, 0:NEXP]))
            stp(v.tensor_reduce(rmax[:], lgt[:], AX.X, ALU.max))
            stp(v.tensor_scalar(rnegm[:], rmax[:], -1.0, None, ALU.mult))
            v.wait_ge(env.sems["S"], s_base + 1)
            stp(v.tensor_reduce(rsum[:], rex[:], AX.X, ALU.add))
            stp(v.reciprocal(rrs[:], rsum[:]))
            mb = bass.AP(rmax[:].tensor, rmax[:].offset, [rmax[:].ap[0], [0, NEXP]])
            stp(v.tensor_tensor(req1[:], lgt[:], mb, ALU.is_equal))
            stp(v.scalar_tensor_tensor(rtmp[:], req1[:], -1e30, lgt[:], ALU.mult, ALU.add))
            stp(v.tensor_reduce(rm2[:], rtmp[:], AX.X, ALU.max))
            v.wait_ge(env.sems["S"], s_base + 2)
            stp(v.tensor_tensor(rs2[:], re2x[:], rrs[:], ALU.mult))
            m2b = bass.AP(rm2[:].tensor, rm2[:].offset, [rm2[:].ap[0], [0, NEXP]])
            stp(v.tensor_tensor(req2[:], rtmp[:], m2b, ALU.is_equal))
            stp(v.tensor_scalar(rt1[:], req1[:], rrs[:], None, ALU.mult))
            stp(v.scalar_tensor_tensor(mrowtok[:, ps * NEXP:(ps + 1) * NEXP],
                                       req2[:], rs2[:], rt1[:], ALU.mult, ALU.add))

        def p_rs(s, ps=ps, v_base=v_base):
            s.wait_ge(env.sems["V"], v_base + 3)
            last = s.activation(rex[:], lgt[:], AF.Exp, bias=rnegm[:])
            env.inc(last, "S")
            s.wait_ge(env.sems["V"], v_base + 8)
            last = s.activation(re2x[:], rm2[:], AF.Exp, bias=rnegm[:])
            env.inc(last, "S")
        env.phase([("vector", p_rm), ("scalar", p_rs)])

    base_t = env.cnt["T"]
    base_v = env.cnt["V"]

    def p_mt(t):
        for g in range(4):
            if g >= 1:
                t.wait_ge(env.sems["V"], base_v + g)
            for k in range(4):
                ps = g * 4 + k
                last = t.transpose(pd[0:8, k * 128:(k + 1) * 128]
                                   if False else pd[:, k * 128:(k + 1) * 128][0:8, :],
                                   mrowtok[:, ps * NEXP:(ps + 1) * NEXP], cst["identf"][:])
            env.inc(last, "T")

    def p_mv(v):
        for g in range(4):
            v.wait_ge(env.sems["T"], base_t + g + 1)
            for k in range(4):
                ps = g * 4 + k
                last = v.tensor_copy(mtr[:, ps * 128:(ps + 1) * 128],
                                     pd[:, k * 128:(k + 1) * 128][0:8, :])
            env.inc(last, "V")
    env.phase([("tensor", p_mt), ("vector", p_mv)])

    for e in range(NEXP):
        def p_wl(gp, e=e):
            r = w1_d[e].rearrange("(a p) j -> p a j", p=128)
            last = gp.dma_start(out=w1buf[:].rearrange("p (a j) -> p a j", j=HID), in_=r)
            env.inc(last, "LG", 16)
        env.phase([("gpsimd", p_wl)])

        for sg in range(L // MSEG):
            t0 = sg * MSEG

            base_t = env.cnt["T"]
            base_ld0 = env.cnt["LD"]

            def p_lx(sy, t0=t0):
                srcx = xln_dram.rearrange("(a p) t -> p a t", p=128)[:, :, t0:t0 + MSEG]
                last = sy.dma_start(out=xeseg[:].rearrange("p (a t) -> p a t", t=MSEG), in_=srcx)
                env.inc(last, "LD", 16)

            def p_bc(t, e=e, t0=t0):
                last = t.matmul(pg[:, 0:MSEG], cst["esel8"][:, e * 128:(e + 1) * 128],
                                mtr[0:8, t0:t0 + MSEG], start=True, stop=True)
                env.inc(last, "T")

            def p_bv(v, e=e, base_ld0=base_ld0):
                v.wait_ge(env.sems["T"], base_t + 1)
                inst = v.tensor_copy(mbc[:], pg[:, 0:MSEG])
                env.inc(inst, "V")
                v.wait_ge(env.sems["V"], env.cnt["V"])
                v.wait_ge(env.sems["LD"], base_ld0 + 16)
                for ct in range(NCT):
                    sl = slice(ct * MSEG, (ct + 1) * MSEG)
                    last = v.tensor_tensor(xeseg[:, sl], mbc[:], xeseg[:, sl], ALU.mult)
                env.inc(last, "V")
            env.phase([("sync", p_lx), ("tensor", p_bc), ("vector", p_bv)])

            base_t = env.cnt["T"]
            base_s = env.cnt["S"]
            base_v = env.cnt["V"]
            base_lg = env.cnt["LG"]

            def hid_ld(gp, e=e):
                for nt in range(NHT):
                    if nt >= 2:
                        gp.wait_ge(env.sems["T"], base_t + nt - 1)
                    src_ = w3q_dram[(e * NHT + nt) * 128:(e * NHT + nt + 1) * 128, :]
                    last = gp.dma_start(out=w3k[nt % 2][:], in_=src_)
                    env.inc(last, "LG", 16)

            def hid_t(t):
                for nt in range(NHT):
                    pga = pg if nt % 2 == 0 else pd[:, 0:512]
                    pua = pu if nt % 2 == 0 else pd[:, 512:1024]
                    if nt >= 2:
                        t.wait_ge(env.sems["V"], base_v + nt - 1)
                    t.wait_ge(env.sems["LG"], base_lg + 16 * (nt + 1))
                    for kc in range(NCT):
                        t.matmul(pga[:, 0:512],
                                 w1buf[:, kc * HID + nt * 128: kc * HID + (nt + 1) * 128],
                                 xeseg[:, kc * MSEG:(kc + 1) * MSEG],
                                 start=(kc == 0), stop=(kc == NCT - 1))
                    for kc in range(NCT):
                        last = t.matmul(pua[:, 0:512],
                                        w3k[nt % 2][:, kc * 128:(kc + 1) * 128],
                                        xeseg[:, kc * MSEG:(kc + 1) * MSEG],
                                        start=(kc == 0), stop=(kc == NCT - 1))
                    env.inc(last, "T")

            def hid_s(s):
                for nt in range(NHT):
                    pga = pg if nt % 2 == 0 else pd[:, 0:512]
                    if nt >= 2:
                        s.wait_ge(env.sems["V"], base_v + nt - 1)
                    s.wait_ge(env.sems["T"], base_t + nt + 1)
                    last = s.activation(sgt[:, (nt % 2) * MSEG:(nt % 2) * MSEG + MSEG],
                                        pga[:, 0:MSEG], AF.Silu)
                    env.inc(last, "S")

            def hid_v(v):
                for nt in range(NHT):
                    pua = pu if nt % 2 == 0 else pd[:, 512:1024]
                    v.wait_ge(env.sems["S"], base_s + nt + 1)
                    last = v.tensor_tensor(hbuf[:, nt * MSEG:(nt + 1) * MSEG],
                                           sgt[:, (nt % 2) * MSEG:(nt % 2) * MSEG + MSEG],
                                           pua[:, 0:MSEG], ALU.mult)
                    env.inc(last, "V")
            env.phase([("gpsimd", hid_ld), ("tensor", hid_t), ("scalar", hid_s), ("vector", hid_v)])

            base_t = env.cnt["T"]
            base_v = env.cnt["V"]
            base_ld = env.cnt["LD"]

            def down_ld(sy, e=e):
                for jp in range(2):
                    bt = base_t + jp * (NHT + 2)
                    for nt in range(NHT):
                        if jp == 1 and nt == 0:
                            sy.wait_ge(env.sems["T"], base_t + NHT)
                        if nt >= 2:
                            sy.wait_ge(env.sems["T"], bt + nt - 1)
                        src_ = w2q_dram[((e * 2 + jp) * NHT + nt) * 128:
                                        ((e * 2 + jp) * NHT + nt + 1) * 128, :]
                        last = sy.dma_start(out=wdk[nt % 2][:], in_=src_)
                        env.inc(last, "LD", 16)

            def down_t(t, t0=t0):
                for jp in range(2):
                    bt = base_t + jp * (NHT + 2)
                    bl = base_ld + jp * 16 * NHT
                    if jp == 1:
                        t.wait_ge(env.sems["V"], base_v + 1)
                    for nt in range(NHT):
                        t.wait_ge(env.sems["LD"], bl + 16 * (nt + 1))
                        for tsub in range(MNTC):
                            last = t.matmul(
                                pd[:, tsub * 512:(tsub + 1) * 512],
                                hbuf[:, nt * MSEG + tsub * 128: nt * MSEG + (tsub + 1) * 128],
                                wdk[nt % 2][:],
                                start=(nt == 0), stop=(nt == NHT - 1))
                        env.inc(last, "T")
                    n = 0
                    for tsub in range(MNTC):
                        for ct in range(4):
                            if n % 8 == 0:
                                t.wait_ge(env.sems["V"], base_v + jp * 3 + 1 + (1 if n >= 8 else 0))
                            last = t.transpose(pT[:, (n % 8) * 128:(n % 8 + 1) * 128],
                                               dsbb[:, tsub * 512 + ct * 128: tsub * 512 + (ct + 1) * 128],
                                               cst["ident"][:])
                            n += 1
                            if n % 8 == 0:
                                env.inc(last, "T")

            def down_v(v, t0=t0):
                for jp in range(2):
                    bt = base_t + jp * (NHT + 2)
                    v.wait_ge(env.sems["T"], bt + NHT)
                    inst = v.tensor_copy(dsbb[:], pd[:, 0:MNTC * 512])
                    env.inc(inst, "V")
                    n = 0
                    for tsub in range(MNTC):
                        for ct in range(4):
                            if n % 8 == 0:
                                v.wait_ge(env.sems["T"], bt + NHT + n // 8 + 1)
                            ctg = jp * 4 + ct
                            sl = slice(ctg * L + t0 + tsub * 128, ctg * L + t0 + (tsub + 1) * 128)
                            last = v.tensor_tensor(xfull[:, sl], pT[:, (n % 8) * 128:(n % 8 + 1) * 128],
                                                   xfull[:, sl], ALU.add)
                            n += 1
                            if n % 8 == 0:
                                env.inc(last, "V")
            env.phase([("sync", down_ld), ("tensor", down_t), ("vector", down_v)])

def build_moe():
    nc = bass.Bass("TRN2", target_bir_lowering=False, detect_race_conditions=False)
    xin = nc.dram_tensor("xT", [DM, L], F32, kind="ExternalInput").ap()
    gate_wT_d = nc.dram_tensor("gate_wT", [DM, NEXP], F32, kind="ExternalInput").ap()
    w1_d = nc.dram_tensor("w1", [NEXP, DM, HID], F32, kind="ExternalInput").ap()
    w3_d = nc.dram_tensor("w3", [NEXP, DM, HID], F32, kind="ExternalInput").ap()
    w2_d = nc.dram_tensor("w2", [NEXP, HID, DM], F32, kind="ExternalInput").ap()
    MOE_CONSTS = [c for c in CONST_SPECS if c[0] in ("ones128", "ones128f", "ident", "identf", "esel8")]
    cst_d = {nm: nc.dram_tensor(nm, shp, dt, kind="ExternalInput").ap()
             for nm, shp, dt in MOE_CONSTS}
    xln_dram = nc.dram_tensor("m_xln_scratch", [DM, L], BF16, kind="Internal").ap()
    w2q_dram = nc.dram_tensor("m_w2q_scratch", [NEXP * 2 * NHT * 128, 512], BF16, kind="Internal").ap()
    w3q_dram = nc.dram_tensor("m_w3q_scratch", [NEXP * NHT * 128, NCT * 128], BF16, kind="Internal").ap()
    xout = nc.dram_tensor("xoutT", [DM, L], F32, kind="ExternalOutput").ap()

    with Env(nc) as env:
        cst = {nm: env.sb(nm + "_sb", shp, dt) for nm, shp, dt in MOE_CONSTS}
        for nm, val in (("epsc", EPS), ("neghc", -0.5)):
            cst[nm] = env.sb(nm, [128, 1], F32)
            cst[nm + "_val"] = val

        def p0_v(v):
            for nm in ("epsc", "neghc"):
                last = v.memset(cst[nm][:], cst[nm + "_val"])
            env.inc(last, "V")

        xfull = env.sb("m_xfull", [128, NCT * L], F32)

        def p0_sy(sy):
            for nm, shp, dt in MOE_CONSTS:
                sy.dma_start(out=cst[nm][:], in_=cst_d[nm]).then_inc(env.sems["LD"], 16)
                env.cnt["LD"] += 16
            last = sy.dma_start(out=xfull[:].rearrange("p (a t) -> p a t", t=L),
                                in_=xin.rearrange("(a p) t -> p a t", p=128))
            env.inc(last, "LD", 16)
        env.phase([("sync", p0_sy), ("vector", p0_v)])
        with env.scope():
            emit_moe(env, cst, xfull, xln_dram, w2q_dram, w3q_dram, gate_wT_d, w1_d, w3_d, w2_d)

        def pf_sy(sy):
            last = sy.dma_start(out=xout.rearrange("(a p) t -> p a t", p=128),
                                in_=xfull[:].rearrange("p (a t) -> p a t", t=L))
            env.inc(last, "ST", 16)
        env.phase([("sync", pf_sy)])

        @env.block.sync
        def _(sy):
            sy.wait_ge(env.sems["ST"], env.cnt["ST"])
    return nc


def run_moe(x_full, gate_w, w1, w2, w3):
    consts = host_consts()
    nc = build_moe()
    in_maps = []
    B = x_full.shape[0]
    for b in range(B):
        m = {"xT": np.ascontiguousarray(x_full[b].T),
             "gate_wT": np.ascontiguousarray(gate_w.T),
             "w1": w1, "w3": w3, "w2": w2}
        m.update({k: consts[k] for k in ("ones128", "ones128f", "ident", "identf", "esel8")})
        in_maps.append(m)
    res = run_bass_kernel_spmd(nc, in_maps, core_ids=list(range(B)))
    return np.stack([np.ascontiguousarray(res.results[b]["xoutT"]).T for b in range(B)])




def build_fused():
    nc = bass.Bass("TRN2", target_bir_lowering=False, detect_race_conditions=False)
    xin = nc.dram_tensor("xT", [DM, L], F32, kind="ExternalInput").ap()
    gdr = {}
    for nm, shp in (("wq", [DM, INNER]), ("wk", [DM, INNER]), ("wv", [DM, INNER]),
                    ("wg", [DM, INNER]), ("wo", [INNER, DM])):
        gdr[nm] = nc.dram_tensor(nm, shp, F32, kind="ExternalInput").ap()
    bg = nc.dram_tensor("bg", [1, INNER], F32, kind="ExternalInput").ap()
    gate_wT_d = nc.dram_tensor("gate_wT", [DM, NEXP], F32, kind="ExternalInput").ap()
    w1_d = nc.dram_tensor("w1", [NEXP, DM, HID], F32, kind="ExternalInput").ap()
    w3_d = nc.dram_tensor("w3", [NEXP, DM, HID], F32, kind="ExternalInput").ap()
    w2_d = nc.dram_tensor("w2", [NEXP, HID, DM], F32, kind="ExternalInput").ap()
    w_in_d = nc.dram_tensor("w_in", [DM, DI], F32, kind="ExternalInput").ap()
    w_gate_d = nc.dram_tensor("w_gate", [DM, DI], F32, kind="ExternalInput").ap()
    w_out_d = nc.dram_tensor("w_out", [DI, DM], F32, kind="ExternalInput").ap()
    b_in_d = nc.dram_tensor("b_in", [128, NTI], F32, kind="ExternalInput").ap()
    b_gate_d = nc.dram_tensor("b_gate", [128, NTI], F32, kind="ExternalInput").ap()
    b_out_d = nc.dram_tensor("b_out", [128, NCT], F32, kind="ExternalInput").ap()
    A_d = nc.dram_tensor("A", [DI, 256], F32, kind="ExternalInput").ap()
    B_d = nc.dram_tensor("B_ssm", [DI, DS], F32, kind="ExternalInput").ap()
    C_d = nc.dram_tensor("C_ssm", [DI, DS], F32, kind="ExternalInput").ap()
    fwg_d = nc.dram_tensor("f_w_gate", [HID, DM], F32, kind="ExternalInput").ap()
    fwu_d = nc.dram_tensor("f_w_up", [HID, DM], F32, kind="ExternalInput").ap()
    fwd_d = nc.dram_tensor("f_w_down", [DM, HID], F32, kind="ExternalInput").ap()
    cst_d = {nm: nc.dram_tensor(nm, shp, dt, kind="ExternalInput").ap()
             for nm, shp, dt in CONST_SPECS}
    xln_dram = nc.dram_tensor("m_xln_scratch", [DM, L], BF16, kind="Internal").ap()
    w2q_dram = nc.dram_tensor("m_w2q_scratch", [NEXP * 2 * NHT * 128, 512], BF16, kind="Internal").ap()
    w3q_dram = nc.dram_tensor("m_w3q_scratch", [NEXP * NHT * 128, NCT * 128], BF16, kind="Internal").ap()
    wdq_dram = nc.dram_tensor("f_wdq_scratch", [NHT * 128, DM], BF16, kind="Internal").ap()
    xout = nc.dram_tensor("xoutT", [DM, L], F32, kind="ExternalOutput").ap()

    with Env(nc) as env:
        cst = {nm: env.sb(nm + "_sb", shp, dt) for nm, shp, dt in CONST_SPECS}
        for nm, val in (("epsc", EPS), ("onec", 1.0), ("neg1c", -1.0), ("neghc", -0.5)):
            cst[nm] = env.sb(nm, [128, 1], F32)
            cst[nm + "_val"] = val
        xfull = env.sb("xfull", [128, NCT * L], F32)

        def p0_v(v):
            for nm in ("epsc", "onec", "neg1c", "neghc"):
                last = v.memset(cst[nm][:], cst[nm + "_val"])
            env.inc(last, "V")

        def p0_sy(sy):
            for nm, shp, dt in CONST_SPECS:
                sy.dma_start(out=cst[nm][:], in_=cst_d[nm]).then_inc(env.sems["LD"], 16)
                env.cnt["LD"] += 16
            last = sy.dma_start(out=xfull[:].rearrange("p (a t) -> p a t", t=L),
                                in_=xin.rearrange("(a p) t -> p a t", p=128))
            env.inc(last, "LD", 16)
        env.phase([("sync", p0_sy), ("vector", p0_v)])

        with env.scope():
            emit_gla(env, cst, xfull, gdr, bg)
        with env.scope():
            emit_moe(env, cst, xfull, xln_dram, w2q_dram, w3q_dram, gate_wT_d, w1_d, w3_d, w2_d)
        with env.scope():
            emit_ssm(env, cst, xfull, w_in_d, w_gate_d, w_out_d,
                     b_in_d, b_gate_d, b_out_d, A_d, B_d, C_d)
        with env.scope():
            emit_fp4ffn(env, cst, xfull, wdq_dram, fwg_d, fwu_d, fwd_d)

        def pf_sy(sy):
            last = sy.dma_start(out=xout.rearrange("(a p) t -> p a t", p=128),
                                in_=xfull[:].rearrange("p (a t) -> p a t", t=L))
            env.inc(last, "ST", 16)
        env.phase([("sync", pf_sy)])

        @env.block.sync
        def _(sy):
            sy.wait_ge(env.sems["ST"], env.cnt["ST"])
    return nc


def kernel(**inputs):
    ins = {k: np.asarray(v) for k, v in inputs.items()}
    x = np.asarray(ins["L0_attn_x"], np.float32)
    B = x.shape[0]
    cores = list(range(B))
    consts = host_consts()
    nc = build_fused()
    im = []
    for b in range(B):
        m = {"xT": np.ascontiguousarray(x[b].T),
             "wq": ins["L0_attn_Wq"], "wk": ins["L0_attn_Wk"],
             "wv": ins["L0_attn_Wv"], "wg": ins["L0_attn_Wg"],
             "bg": np.ascontiguousarray(np.asarray(ins["L0_attn_bg"]).reshape(1, -1)),
             "wo": permute_wo_rows(np.asarray(ins["L0_attn_Wo"])),
             "gate_wT": np.ascontiguousarray(np.asarray(ins["L0_ffn_gate_w"]).T),
             "w1": ins["L0_ffn_w1"], "w3": ins["L0_ffn_w3"], "w2": ins["L0_ffn_w2"],
             "w_in": ins["L1_attn_w_in"], "w_gate": ins["L1_attn_w_gate"],
             "w_out": ins["L1_attn_w_out"],
             "b_in": np.ascontiguousarray(np.asarray(ins["L1_attn_b_in"]).reshape(NTI, 128).T),
             "b_gate": np.ascontiguousarray(np.asarray(ins["L1_attn_b_gate"]).reshape(NTI, 128).T),
             "b_out": np.ascontiguousarray(np.asarray(ins["L1_attn_b_out"]).reshape(NCT, 128).T),
             "A": np.ascontiguousarray(np.asarray(ins["L1_attn_A"]).transpose(0, 2, 1).reshape(DI, 256)),
             "B_ssm": ins["L1_attn_B_ssm"], "C_ssm": ins["L1_attn_C_ssm"],
             "f_w_gate": ins["L1_ffn_w_gate"], "f_w_up": ins["L1_ffn_w_up"],
             "f_w_down": ins["L1_ffn_w_down"]}
        m.update({k: consts[k] for k, _s, _d in CONST_SPECS})
        im.append(m)
    res = run_bass_kernel_spmd(nc, im, core_ids=cores)
    out = np.stack([np.ascontiguousarray(res.results[b]["xoutT"]).T for b in range(B)])
    return out.astype(np.float32)
